# version 06 (final)
# baseline (speedup 1.0000x reference)
import numpy as np
import ml_dtypes
from contextlib import ExitStack
import concourse.bass as bass
import concourse.mybir as mybir
from concourse.bass_utils import run_bass_kernel_spmd

F32 = mybir.dt.float32
BF16 = mybir.dt.bfloat16
I32 = mybir.dt.int32
AF = mybir.ActivationFunctionType
ALU = mybir.AluOpType
AX = mybir.AxisListType
ENG = mybir.EngineType
NPBF16 = ml_dtypes.bfloat16

USE_POLL = True

D = 1024
TOK = 2048
TFULL = 4096
DT = 8
NHP = 4
DH = 64


class Chan:
    __slots__ = ("h", "n", "name")
    def __init__(self, prog, name):
        self.h = prog.ctx.enter_context(prog.nc.semaphore(name))
        self.n = 0
        self.name = name
    def bump(self, k):
        self.n += k
        return self.n

class Prog:
    ENGINES = ("tensor", "scalar", "vector", "gpsimd", "sync")
    def __init__(self, nc, ctx):
        self.nc, self.ctx = nc, ctx
        self.q = {e: [] for e in self.ENGINES}
        self.holder = {}
        self.chain = {}
    def chan(self, name):
        return Chan(self, name)
    def op(self, eng, emit, waits=(), inc=None):
        waits = list(waits)
        if eng in self.chain:
            ch = self.chain[eng]
            assert inc is None or inc[0] is ch, f"{eng} op must inc its chain chan"
            if ch.n > 0:
                waits.append((ch, ch.n))
            tgt = ch.bump(1)
            self.q[eng].append((tuple(waits), emit, (ch, 1)))
            return tgt
        tgt = None
        if inc is not None:
            tgt = inc[0].bump(inc[1])
        self.q[eng].append((tuple(waits), emit, inc))
        return tgt
    def emit_all(self):
        self.ctx.enter_context(self.nc.allow_low_precision(
            reason="bf16 matmul operands + fp32 accumulation per problem tolerance"))
        block = self.ctx.enter_context(self.nc.Block())
        def mk(name):
            ops = self.q[name]
            def body(eng):
                for waits, emit, inc in ops:
                    for ch, v in waits:
                        eng.wait_ge(ch.h, v)
                    inst = emit(eng)
                    if inc is not None:
                        assert inst is not None, f"op needs inst for inc on {inc[0].name}"
                        inst.then_inc(inc[0].h, inc[1])
            return body
        block.tensor(mk("tensor"))
        block.scalar(mk("scalar"))
        block.vector(mk("vector"))
        block.gpsimd(mk("gpsimd"))
        block.sync(mk("sync"))


def ceil_div(a, b):
    return (a + b - 1) // b

class Ctx:
    pass

def linear_fm(C, *, nK, mlist, tlist, TQ, lhsT, rhs, banks, evac,
              rhs_waits=lambda tc: (), w_waits=lambda m: (), pend0=None):
    P = C.P
    pend = list(pend0) if pend0 else [None] * len(banks)
    i = 0
    for m in mlist:
        for tc in tlist:
            bi = i % len(banks)
            ps = banks[bi]
            w0 = list(rhs_waits(tc)) + list(w_waits(m))
            if pend[bi] is not None:
                w0.append(pend[bi])
            for k in range(nK):
                def mm(t, ps=ps, k=k, m=m, tc=tc, first=(k == 0), last=(k == nK - 1)):
                    return t.matmul(ps[:, 0:TQ], lhsT(k, m), rhs(k, tc),
                                    start=first, stop=last)
                if k == nK - 1:
                    mmv = P.op("tensor", mm, waits=(w0 if k == 0 else ()),
                               inc=(C.ch_t, 1))
                else:
                    P.op("tensor", mm, waits=(w0 if k == 0 else ()))
            pend[bi] = evac(m, tc, ps, [(C.ch_t, mmv)])
            i += 1
    return pend


def host_prep(inp, core, epoch=None):
    b, p = core // 2, core % 2
    g = lambda k: np.asarray(inp[k], np.float32)
    o = {}
    x = g("L0_attn_x")[b]
    o["x_fm"] = np.ascontiguousarray(x[p * TOK:(p + 1) * TOK].T)
    hs = slice(p * 512, (p + 1) * 512)

    for nm in ("wq", "wk", "wv"):
        o[f"L0_{nm}T"] = np.ascontiguousarray(g(f"L0_attn_{nm}_w").T[:, hs])
        o[f"L0_{nm}b"] = np.ascontiguousarray(g(f"L0_attn_{nm}_b")[hs])
    woT = g("L0_attn_wo_w").T
    o["L0_woT"] = np.ascontiguousarray(
        np.concatenate([woT[p * 512:(p + 1) * 512], woT[(1 - p) * 512:(2 - p) * 512]], 0))
    o["L0_wob"] = g("L0_attn_wo_b")
    o["L0_normw"] = g("L0_ffn_norm_weight")
    o["L0_routerT"] = np.ascontiguousarray(g("L0_ffn_router_w").T)
    o["L0_sw1"] = g("L0_ffn_sw1"); o["L0_sb1"] = g("L0_ffn_sb1")
    o["L0_sw2"] = g("L0_ffn_sw2"); o["L0_sb2"] = g("L0_ffn_sb2")
    o["L0_ew1"] = g("L0_ffn_ew1"); o["L0_eb1"] = g("L0_ffn_eb1")
    o["L0_ew2"] = g("L0_ffn_ew2"); o["L0_eb2"] = g("L0_ffn_eb2")
    for nm in ("W_Q", "W_K", "W_V", "W_K_prime"):
        o[f"L1_{nm}T"] = np.ascontiguousarray(g(f"L1_attn_{nm}").T[:, hs])
    w1o = g("L1_attn_W_O").T
    o["L1_woT"] = np.ascontiguousarray(
        np.concatenate([w1o[p * 512:(p + 1) * 512], w1o[(1 - p) * 512:(2 - p) * 512]], 0))
    o["L1_alpha"] = np.asarray(inp["L1_attn_alpha"], np.float32).reshape(1, 1)
    o["L1_routerT"] = np.ascontiguousarray(g("L1_ffn_w_router").T)
    o["L1_cent"] = g("L1_ffn_head_centroids")
    o["L1_temp"] = g("L1_ffn_temperature").reshape(1, 1)
    o["L1_ie"] = g("L1_ffn_input_experts")
    o["L1_oe"] = g("L1_ffn_output_experts")
    qkvT = g("L2_attn_w_qkv").T
    o["L2_qT"] = np.ascontiguousarray(qkvT[:, 0 * D + p * 512:0 * D + (p + 1) * 512])
    o["L2_kT"] = np.ascontiguousarray(qkvT[:, 1 * D + p * 512:1 * D + (p + 1) * 512])
    o["L2_vT"] = np.ascontiguousarray(qkvT[:, 2 * D + p * 512:2 * D + (p + 1) * 512])
    w2o = g("L2_attn_w_proj").T
    o["L2_projT"] = np.ascontiguousarray(
        np.concatenate([w2o[p * 512:(p + 1) * 512], w2o[(1 - p) * 512:(2 - p) * 512]], 0))
    o["L2_gateT"] = np.ascontiguousarray(g("L2_ffn_gate_w").T)
    o["L2_wg"] = g("L2_ffn_w_gate"); o["L2_wu"] = g("L2_ffn_w_up"); o["L2_wd"] = g("L2_ffn_w_down")
    o["L2_swg"] = g("L2_ffn_sw_gate"); o["L2_swu"] = g("L2_ffn_sw_up"); o["L2_swd"] = g("L2_ffn_sw_down")
    o["L2_sgT"] = np.ascontiguousarray(g("L2_ffn_shared_gate_w").T)
    Wq = g("L3_attn_Wq")
    base = p * 8
    qperm = []
    for i in range(4):
        qperm += [base + i, base + i + 4]
    qcols = np.concatenate([np.arange(h * 64, (h + 1) * 64) for h in qperm])
    o["L3_wq"] = np.ascontiguousarray(Wq[:, qcols])
    Wkv = g("L3_attn_Wkv")
    o["L3_wk"] = np.ascontiguousarray(Wkv[:, p * 128:(p + 1) * 128])
    o["L3_wv"] = np.ascontiguousarray(Wkv[:, 256 + p * 128:256 + (p + 1) * 128])
    operm = []
    for pp in (p, 1 - p):
        bb = pp * 8
        for i in range(4):
            operm += [bb + i, bb + i + 4]
    orows = np.concatenate([np.arange(h * 64, (h + 1) * 64) for h in operm])
    o["L3_wo"] = np.ascontiguousarray(g("L3_attn_Wo")[orows, :])
    o["L3_bo"] = g("L3_attn_bo")
    o["L3_wg"] = g("L3_ffn_w_gate")
    o["L3_wu"] = g("L3_ffn_w_up")
    o["L3_wd"] = g("L3_ffn_w_down")
    ep = int(np.random.randint(1, 1 << 30)) if epoch is None else int(epoch)
    o["epoch"] = np.asarray([[ep]], np.int32)
    pay = np.zeros((1, 34), np.int32)
    pay[0, 0::2] = ep
    pay[0, 1::2][:16] = np.arange(1, 17)
    o["flagpay"] = pay
    return o


def _tril_maskT():
    m = np.zeros((128, 128), np.float32)
    for k in range(128):
        m[k, k:] = 1.0
    return m.astype(NPBF16)

def _diag_masks_l2():
    out = np.zeros((128, 4 * 512), np.float32)
    for i in range(4):
        for k in range(128):
            out[k, i * 512 + i * 128 + k: (i + 1) * 512] = 1.0
    return out.astype(NPBF16)

def _rope_tables():
    n, d = TFULL, DH
    inv = 1.0 / 10000.0 ** (np.arange(0, d, 2, dtype=np.float32) / d)
    fr = np.einsum("i,j->ij", np.arange(n, dtype=np.float32), inv)
    fr = np.concatenate([fr, fr], -1)
    cos = np.cos(fr).T
    sin = np.sin(fr).T
    sgn = np.ones((64, 1), np.float32)
    sgn[0:32] = -1.0
    cos2 = np.tile(cos, (2, 1))
    sin2 = np.tile(sin * sgn, (2, 1))
    return cos2.astype(NPBF16), sin2.astype(NPBF16)


def build_program(layers, dbg=None):
    nc = bass.Bass("TRN2", target_bir_lowering=False, detect_race_conditions=False,
                   dynamic_dma_scratch_size=16384)
    NL = len(layers)
    ctx = ExitStack()
    P = Prog(nc, ctx)
    C = Ctx()
    C.nc, C.P, C.ctx = nc, P, ctx
    C.dbg = dbg

    def din(name, shape, dt=F32):
        return nc.dram_tensor(name, list(shape), dt, kind="ExternalInput").ap()
    inp = {}
    names = {
        "x_fm": (D, TOK), "epoch": (1, 1), "flagpay": (1, 34),
        "L0_wqT": (D, 512), "L0_wkT": (D, 512), "L0_wvT": (D, 512),
        "L0_wqb": (512,), "L0_wkb": (512,), "L0_wvb": (512,),
        "L0_woT": (D, D), "L0_wob": (D,),
        "L0_normw": (D,), "L0_routerT": (D, 8),
        "L0_sw1": (D, 4096), "L0_sb1": (4096,), "L0_sw2": (4096, D), "L0_sb2": (D,),
        "L0_ew1": (8, D, 4096), "L0_eb1": (8, 4096),
        "L0_ew2": (8, 4096, D), "L0_eb2": (8, D),
        "L1_W_QT": (D, 512), "L1_W_KT": (D, 512), "L1_W_VT": (D, 512),
        "L1_W_K_primeT": (D, 512), "L1_woT": (D, D), "L1_alpha": (1, 1),
        "L1_routerT": (D, 128), "L1_cent": (16, 128), "L1_temp": (1, 1),
        "L1_ie": (16, D, 512), "L1_oe": (16, 512, 64),
        "L2_qT": (D, 512), "L2_kT": (D, 512), "L2_vT": (D, 512),
        "L2_projT": (D, D),
        "L2_gateT": (D, 8), "L2_wg": (8, D, 1408), "L2_wu": (8, D, 1408),
        "L2_wd": (8, 1408, D), "L2_swg": (D, 1408), "L2_swu": (D, 1408),
        "L2_swd": (1408, D), "L2_sgT": (D, 1),
        "L3_wq": (D, 512), "L3_wk": (D, 128), "L3_wv": (D, 128),
        "L3_wo": (D, D), "L3_bo": (D,),
        "L3_wg": (2816, D), "L3_wu": (2816, D), "L3_wd": (D, 2816),
    }
    need = {"x_fm", "epoch", "flagpay"}
    for l in layers:
        for k in names:
            if k.startswith(f"L{l}_"):
                need.add(k)
    for k in sorted(need):
        dt = I32 if k in ("epoch", "flagpay") else F32
        inp[k] = din(k, names[k], dt)
    x_out = nc.dram_tensor("x_out", [D, TOK], F32, kind="ExternalOutput").ap()
    dbg_out = None
    if dbg is not None:
        dbg_out = nc.dram_tensor("dbg_out", list(dbg[1]), F32, kind="ExternalOutput").ap()
    C.inp, C.x_out, C.dbg_out = inp, x_out, dbg_out

    xn_sh = nc.dram_tensor("xn_sh", [2, NL, 128, DT * TOK], BF16, addr_space="Shared").ap()
    oh_sh = nc.dram_tensor("oh_sh", [2, NL, NHP, 128, TOK], BF16, addr_space="Shared").ap()
    flg = nc.dram_tensor("flg", [2, 64, 2], I32, addr_space="Shared").ap()
    C.xn_sh, C.oh_sh, C.flg = xn_sh, oh_sh, flg
    C.scr = nc.dram_tensor("scr", [64, 128, 4096], F32).ap()

    C.identb = nc.inline_tensor(np.eye(128, dtype=np.float32).astype(NPBF16), "identb").ap()
    C.identf = nc.inline_tensor(np.eye(128, dtype=np.float32), "identf").ap()
    C.trilT = nc.inline_tensor(_tril_maskT(), "trilT").ap()
    dm = _diag_masks_l2()
    C.diagm = nc.inline_tensor(dm, "diagm").ap()
    cos, sin = _rope_tables()
    C.ropec = nc.inline_tensor(cos, "ropec").ap()
    C.ropes = nc.inline_tensor(sin, "ropes").ap()
    sel8 = np.zeros((8, 8 * 128), np.float32)
    for e in range(8):
        sel8[e, e * 128:(e + 1) * 128] = 1.0
    C.sel8 = nc.inline_tensor(sel8.astype(NPBF16), "sel8").ap()
    sel16 = np.zeros((16, 16 * 128), np.float32)
    for e in range(16):
        sel16[e, e * 128:(e + 1) * 128] = 1.0
    C.sel16 = nc.inline_tensor(sel16.astype(NPBF16), "sel16").ap()
    onesf = np.ones((128, 1), np.float32)
    C.onescol_f = nc.inline_tensor(onesf, "onescol_f").ap()
    C.onescol_b = nc.inline_tensor(onesf.astype(NPBF16), "onescol_b").ap()
    C.onesrow_f = nc.inline_tensor(np.ones((1, 128), np.float32), "onesrow_f").ap()
    C.onesrow_b = nc.inline_tensor(np.ones((1, 128), np.float32).astype(NPBF16), "onesrow_b").ap()
    sel2h = np.zeros((128, 2), np.float32)
    sel2h[0:64, 0] = 1.0
    sel2h[64:128, 1] = 1.0
    C.sel2h = nc.inline_tensor(sel2h.astype(NPBF16), "sel2h").ap()

    C.ch_t = P.chan("ch_t")
    C.ch_s = P.chan("ch_s")
    C.ch_v = P.chan("ch_v")
    C.ch_g = P.chan("ch_g")
    C.ch_wd = P.chan("ch_wd")
    C.ch_sd = P.chan("ch_sd")
    C.ch_ad = P.chan("ch_ad")
    C.ch_vd = P.chan("ch_vd")
    P.chain = {"vector": C.ch_v, "scalar": C.ch_s}
    C.psem = ctx.enter_context(nc.semaphore("psem"))

    sb = lambda name, shape, dt: ctx.enter_context(nc.sbuf_tensor(name, list(shape), dt))
    C.sb = sb

    C.PB = [ctx.enter_context(nc.psum_tensor(f"pb{i}", [128, 512], F32)) for i in range(7)]
    C.pend45 = {}
    C.PT = ctx.enter_context(nc.psum_tensor("pt", [128, 1024], BF16))

    C.X = sb("X", [128, DT * TOK], F32)
    C.CONST = {}
    C.flagv = sb("flagv", [1, 34], I32)
    C.flagr = sb("flagr", [1, 2], I32)
    C.identb_s = sb("identb_s", [128, 128], BF16)
    C.trilT_s = sb("trilT_s", [128, 128], BF16)
    C.onescol_f_s = sb("onescol_f_s", [128, 1], F32)
    C.onescol_b_s = sb("onescol_b_s", [128, 1], BF16)
    C.onesrow_f_s = sb("onesrow_f_s", [1, 128], F32)
    C.onesrow_b_s = sb("onesrow_b_s", [1, 128], BF16)
    C.sel8_s = sb("sel8_s", [8, 8 * 128], BF16)
    C.sel16_s = sb("sel16_s", [16, 16 * 128], BF16)
    C.sel2h_s = sb("sel2h_s", [128, 2], BF16)
    C.ln_ms = sb("ln_ms", [1, 512], F32)
    C.ln_rv = sb("ln_rv", [1, 512], F32)
    C.ln_rs = C.ln_rv
    C.ln_b0 = sb("ln_b0", [1, 512], F32)
    C.ln_t2 = sb("ln_t2", [128, 512], BF16)
    C.ffn_srow = sb("ffn_srow", [128, 8], F32)
    C.ffn_trow = sb("ffn_trow", [128, 16], F32)
    C.att_nr = sb("att_nr", [1, 16], F32)
    C.scr_oh = nc.dram_tensor("scr_oh", [NHP, 128, TFULL], BF16).ap()
    C.scr_h = nc.dram_tensor("scr_h", [128, 22 * 2048], BF16).ap()

    def init_sync(sy):
        h = P.holder
        pid = sy.compute_val(nc.partition_id(engines=[ENG.SP]))
        h["par"] = nc.alloc_register(ENG.SP, "par")
        sy.reg_alu(h["par"], pid, 2, ALU.mod)
        h["pcnt"] = nc.alloc_register(ENG.SP, "pcnt")
        sy.reg_mov(h["pcnt"], 0)
        h["repoch"] = nc.alloc_register(ENG.SP, "repoch")
        h["rflag_e"] = nc.alloc_register(ENG.SP, "rflag_e")
        h["rflag_k"] = nc.alloc_register(ENG.SP, "rflag_k")
        h["t1"] = nc.alloc_register(ENG.SP, "t1")
        h["t2"] = nc.alloc_register(ENG.SP, "t2")
        h["t3"] = nc.alloc_register(ENG.SP, "t3")
        return sy.nop()
    P.op("sync", init_sync)
    P.op("sync", lambda sy: sy.dma_start(out=C.flagv[0:1, 0:34], in_=inp["flagpay"][0:1, 0:34]),
         inc=(C.ch_sd, 16))
    ep_ready = C.ch_sd.n
    if USE_POLL:
        P.op("sync", lambda sy: sy.reg_load(P.holder["repoch"], C.flagv[0:1, 0:1]),
             waits=[(C.ch_sd, ep_ready)])
    consts = [(C.identb_s, C.identb), (C.trilT_s, C.trilT),
              (C.onescol_f_s, C.onescol_f), (C.onescol_b_s, C.onescol_b),
              (C.onesrow_f_s, C.onesrow_f), (C.onesrow_b_s, C.onesrow_b),
              (C.sel8_s, C.sel8), (C.sel16_s, C.sel16),
              (C.sel2h_s, C.sel2h)]
    for dst, src in consts:
        P.op("sync", lambda sy, dst=dst, src=src: sy.dma_start(out=dst[:, :], in_=src[:, :]),
             inc=(C.ch_sd, 16))
    P.op("sync", lambda sy: sy.dma_start(
        out=C.X.rearrange("p (j t) -> p j t", j=DT)[:, :, :],
        in_=inp["x_fm"].rearrange("(j p) t -> p j t", p=128)),
        inc=(C.ch_sd, 16))
    C.x_ready = [(C.ch_sd, C.ch_sd.n)]
    C.const_ready = [(C.ch_sd, C.ch_sd.n)]
    return C


def sh_write(C, dst2, src_ap, fidx, data_waits):
    P = C.P
    v1 = C.ch_sd.bump(16)
    v2 = C.ch_sd.bump(16)
    def emit(sy):
        h = P.holder
        with sy.If_eq(h["par"], 0):
            sy.dma_start(out=dst2(0), in_=src_ap).then_inc(C.ch_sd.h, 16)
        with sy.Else():
            sy.dma_start(out=dst2(1), in_=src_ap).then_inc(C.ch_sd.h, 16)
        sy.end_ifs()
        sy.wait_ge(C.ch_sd.h, v1)
        with sy.If_eq(h["par"], 0):
            sy.dma_start(out=C.flg[0, fidx, 0:2], in_=C.flagv[0:1, 2 * fidx:2 * fidx + 2]).then_inc(C.ch_sd.h, 16)
        with sy.Else():
            sy.dma_start(out=C.flg[1, fidx, 0:2], in_=C.flagv[0:1, 2 * fidx:2 * fidx + 2]).then_inc(C.ch_sd.h, 16)
        sy.end_ifs()
        return sy.nop()
    P.op("sync", emit, waits=data_waits)
    return v2

def sh_poll_read(C, fidx, reads, extra_waits=()):
    P = C.P
    ktar = fidx + 1
    vend = C.ch_sd.bump(16 * len(reads))
    def emit(sy):
        h = P.holder
        def mkcond(slot):
            def cond():
                sy.reg_add(h["pcnt"], h["pcnt"], 16)
                sy.dma_start(out=C.flagr[0:1, 0:2], in_=C.flg[slot, fidx, 0:2]).then_inc(C.psem, 16)
                sy.wait_ge(C.psem, h["pcnt"])
                sy.reg_load(h["rflag_e"], C.flagr[0:1, 0:1])
                sy.reg_load(h["rflag_k"], C.flagr[0:1, 1:2])
                sy.reg_alu(h["t1"], h["rflag_e"], h["repoch"], ALU.subtract)
                sy.reg_alu(h["t2"], h["rflag_k"], ktar, ALU.is_lt)
                sy.reg_alu(h["t3"], h["t1"], h["t2"], ALU.bitwise_or)
                return h["t3"]
            return cond
        if not USE_POLL:
            for _ in range(100):
                sy.nop(cycle_cnt=10000, nofuse=True)
        for slot_if, slot in ((0, 1), (1, 0)):
            guard = sy.If_eq(P.holder["par"], slot_if) if slot_if == 0 else sy.Else()
            with guard:
                if USE_POLL:
                    with sy.While(mkcond(slot)):
                        sy.nop()
                for dst, src2 in reads:
                    sy.dma_start(out=dst, in_=src2(slot)).then_inc(C.ch_sd.h, 16)
        sy.end_ifs()
        return sy.nop()
    P.op("sync", emit, waits=list(extra_waits))
    return vend


def emit_ln(C, xno, sq, x_waits):
    P = C.P
    ms, rv, rs, b0, t2 = (C.ln_ms, C.ln_rv, C.ln_rs, C.ln_b0, C.ln_t2)
    t1 = b0
    PB = C.PB
    X = C.X

    def xsl(j, c):
        return X[:, j * TOK + c * 512: j * TOK + (c + 1) * 512]

    prev_chunk_v = None
    mm_hist = {}
    lastv = None
    for c in range(4):
        half = (c % 2) * 4096
        sqv = None
        for j in range(DT):
            w = list(x_waits) if (c == 0 and j == 0) else []
            if j == 0 and c >= 2:
                w.append((C.ch_t, mm_hist[c - 2]))
            sqv = P.op("scalar", lambda s, j=j, c=c, half=half: s.activation(
                out=sq[:, half + j * 512: half + (j + 1) * 512], in_=xsl(j, c),
                func=AF.Square), waits=w, inc=(C.ch_s, 1))
        w0 = list(x_waits) + ([prev_chunk_v] if prev_chunk_v else [])
        for j in range(DT):
            P.op("tensor", lambda t, j=j, c=c: t.matmul(
                PB[0][0:1, 0:512], C.onescol_f_s[:, 0:1], xsl(j, c),
                start=(j == 0), stop=(j == DT - 1)), waits=(w0 if j == 0 else ()))
        mmv = None
        for j in range(DT):
            kwargs = {}
            if j == DT - 1:
                kwargs = dict(inc=(C.ch_t, 1))
            mmv = P.op("tensor", lambda t, j=j, half=half: t.matmul(
                PB[1][0:1, 0:512], C.onescol_b_s[:, 0:1],
                sq[:, half + j * 512: half + (j + 1) * 512],
                start=(j == 0), stop=(j == DT - 1)),
                waits=([(C.ch_s, sqv)] if j == 0 else ()), **kwargs)
        mm_hist[c] = mmv
        P.op("vector", lambda v: v.tensor_scalar(ms[0:1, :], PB[0][0:1, 0:512],
             1.0 / 1024.0, None, ALU.mult), waits=[(C.ch_t, mmv)])
        P.op("vector", lambda v: v.tensor_mul(t1[0:1, :], ms[0:1, :], ms[0:1, :]))
        P.op("vector", lambda v: v.scalar_tensor_tensor(rv[0:1, :], PB[1][0:1, 0:512],
             1.0 / 1024.0, t1[0:1, :], ALU.mult, ALU.subtract))
        P.op("vector", lambda v: v.tensor_scalar(rv[0:1, :], rv[0:1, :], 1e-5, None, ALU.add))
        vv = P.op("vector", lambda v: v.reciprocal(rv[0:1, :], rv[0:1, :]), inc=(C.ch_v, 1))
        sv = P.op("scalar", lambda s: s.activation(out=rs[0:1, :], in_=rv[0:1, :],
             func=AF.Sqrt), waits=[(C.ch_v, vv)], inc=(C.ch_s, 1))
        bv = P.op("vector", lambda v: v.scalar_tensor_tensor(b0[0:1, :], ms[0:1, :],
             -1.0, rs[0:1, :], ALU.mult, ALU.mult), waits=[(C.ch_s, sv)], inc=(C.ch_v, 1))
        P.op("tensor", lambda t: t.matmul(PB[2][:, 0:512], C.onesrow_f_s[0:1, :],
             rs[0:1, :], start=True, stop=True), waits=[(C.ch_v, bv)])
        mmb = P.op("tensor", lambda t: t.matmul(PB[3][:, 0:512], C.onesrow_f_s[0:1, :],
             b0[0:1, :], start=True, stop=True), inc=(C.ch_t, 1))
        for j in range(DT):
            P.op("vector", lambda v, j=j, c=c: v.tensor_mul(t2[:, :], xsl(j, c),
                 PB[2][:, 0:512]), waits=([(C.ch_t, mmb)] if j == 0 else ()))
            lastv = P.op("vector", lambda v, j=j, c=c: v.tensor_add(
                 xno[:, j * TOK + c * 512: j * TOK + (c + 1) * 512],
                 t2[:, :], PB[3][:, 0:512]), inc=(C.ch_v, 1))
        prev_chunk_v = (C.ch_v, lastv)
    return [(C.ch_v, lastv)]

def phase_barrier(C):
    vals = [(ch, ch.n) for ch in (C.ch_t, C.ch_s, C.ch_v, C.ch_g, C.ch_wd,
                                  C.ch_sd, C.ch_ad, C.ch_vd) if ch.n > 0]
    for e in Prog.ENGINES:
        C.P.op(e, lambda eng: eng.nop(), waits=vals)

def emit_xn_assemble(C, li, XN, xno_written_v, poll=True, poll_own=False):
    P = C.P
    vown = C.ch_sd.bump(16)
    def emit_own(sy):
        h = P.holder
        for pv in (0, 1):
            guard = sy.If_eq(h["par"], 0) if pv == 0 else sy.Else()
            with guard:
                sy.dma_start(
                    out=XN.rearrange("p (j t) -> p j t", j=DT)[:, :, pv * TOK:(pv + 1) * TOK],
                    in_=C.xn_sh[pv, li].rearrange("p (j t) -> p j t", j=DT)[:, :, :],
                ).then_inc(C.ch_sd.h, 16)
        sy.end_ifs()
        return sy.nop()
    P.op("sync", emit_own, waits=[(C.ch_sd, xno_written_v)])
    ktar = 2 * li + 1
    fidx = 2 * li
    vend = C.ch_sd.bump(16)
    def emit_poll(sy):
        h = P.holder
        def mkcond(slot):
            def cond():
                sy.reg_add(h["pcnt"], h["pcnt"], 16)
                sy.dma_start(out=C.flagr[0:1, 0:2], in_=C.flg[slot, fidx, 0:2]).then_inc(C.psem, 16)
                sy.wait_ge(C.psem, h["pcnt"])
                sy.reg_load(h["rflag_e"], C.flagr[0:1, 0:1])
                sy.reg_load(h["rflag_k"], C.flagr[0:1, 1:2])
                sy.reg_alu(h["t1"], h["rflag_e"], h["repoch"], ALU.subtract)
                sy.reg_alu(h["t2"], h["rflag_k"], ktar, ALU.is_lt)
                sy.reg_alu(h["t3"], h["t1"], h["t2"], ALU.bitwise_or)
                return h["t3"]
            return cond
        if (not poll) or not USE_POLL:
            for _ in range(400):
                sy.nop(cycle_cnt=10000, nofuse=True)
        for myp in (0, 1):
            slot = 1 - myp
            guard = sy.If_eq(h["par"], 0) if myp == 0 else sy.Else()
            with guard:
                if poll and USE_POLL:
                    with sy.While(mkcond(slot)):
                        sy.nop()
                elif poll_own:
                    with sy.While(mkcond(myp)):
                        sy.nop()
                sy.dma_start(
                    out=XN.rearrange("p (j t) -> p j t", j=DT)[:, :, slot * TOK:(slot + 1) * TOK],
                    in_=C.xn_sh[slot, li].rearrange("p (j t) -> p j t", j=DT)[:, :, :],
                ).then_inc(C.ch_sd.h, 16)
        sy.end_ifs()
        return sy.nop()
    P.op("sync", emit_poll)
    return [(C.ch_sd, vown), (C.ch_sd, vend)]

def emit_dbg_dump(C, src_ap, waits):
    v = C.P.op("gpsimd", lambda g: g.dma_start(out=C.dbg_out[:, :], in_=src_ap),
               waits=waits, inc=(C.ch_wd, 16))
    return [(C.ch_wd, v)]

def finish_program(C, x_done_waits):
    P = C.P
    v = P.op("sync", lambda sy: sy.dma_start(
        out=C.x_out.rearrange("(j p) t -> p j t", p=128),
        in_=C.X.rearrange("p (j t) -> p j t", j=DT)[:, :, :]),
        waits=x_done_waits, inc=(C.ch_sd, 16))
    P.op("sync", lambda sy: sy.nop(), waits=[(C.ch_sd, v)])
    phase_barrier(C)


def build_full(layers, stage="full"):
    dbg = None
    if stage in ("ln", "lnw", "lnd", "lnp1"):
        dbg = ("xn", (128, DT * TFULL))
    elif stage in ("ln_local", "l0norm"):
        dbg = ("xno", (128, DT * TOK))
    elif stage == "l0gates":
        dbg = ("g", (8, TOK))
    elif stage == "l0shared":
        dbg = ("sh", (1, 1))
    C = build_program(layers, dbg=dbg)
    nc = C.nc
    x_waits = list(C.x_ready) + list(C.const_ready)
    if stage == "pass":
        finish_program(C, x_waits)
        C.P.emit_all()
        C.ctx.close()
        return C
    stop = False
    for li, l in enumerate(layers):
        phase_barrier(C)
        dbgstage = stage in ("ln", "lnw", "lnd", "lnp1", "ln_local")
        if dbgstage:
            with nc.sbuf_tensor(f"xno{li}", [128, DT * TOK], BF16) as xno:
                with nc.sbuf_tensor(f"sq{li}", [128, 2 * 4096], BF16) as sq:
                    ln_done = emit_ln(C, xno, sq, x_waits)
                if stage == "ln_local":
                    dv = emit_dbg_dump(C, xno[:, :], ln_done)
                    finish_program(C, dv)
                    break
                vw = sh_write(C, lambda slot, li=li: C.xn_sh[slot, li][:, :],
                              xno[:, :], 2 * li, ln_done)
            with nc.sbuf_tensor(f"XN_{li}", [128, DT * TFULL], BF16) as XN:
                xn_ready = emit_xn_assemble(C, li, XN, vw, poll=(stage == "ln"),
                                            poll_own=(stage == "lnp1"))
                dv = emit_dbg_dump(C, XN[:, :], xn_ready)
                finish_program(C, dv)
            stop = True
            break
        x_waits = emit_layer(C, li, l, x_waits, stage)
    if not stop:
        if stage in ("full", "l0ffn", "l0attn", "l0shared", "ffnonly", "attnonly"):
            finish_program(C, x_waits)
    C.P.emit_all()
    C.ctx.close()
    return C

def emit_layer(C, li, l, x_waits, stage="full"):
    nc = C.nc
    if l == 0:
        with nc.sbuf_tensor(f"f0_norm_{li}", [128, DT * TOK], BF16) as NORM:
            C.f0_norm = NORM
            with nc.sbuf_tensor(f"xno{li}", [128, DT * TOK], BF16) as xno:
                with nc.sbuf_tensor(f"sq{li}", [128, 2 * 4096], BF16) as sq:
                    ln_done = emit_ln(C, xno, sq, x_waits)
                vw = sh_write(C, lambda slot, li=li: C.xn_sh[slot, li][:, :],
                              xno[:, :], 2 * li, ln_done)
                with nc.sbuf_tensor(f"f0_rsq_{li}", [128, DT * 512], BF16) as RSQ, \
                     nc.sbuf_tensor(f"f0_nwt_{li}", [128, 8], F32) as NWT:
                    C.f0_rsq = RSQ
                    phase_barrier(C)
                    norm_done = emit_ffn_l0_norm(C, xno, ln_done, NWT)
                if stage == "l0norm":
                    dv = emit_dbg_dump(C, NORM[:, :], norm_done)
                    finish_program(C, dv)
                    return norm_done
            with nc.sbuf_tensor(f"f0_wb1_{li}", [128, 8 * 2048], BF16) as WB1, \
                 nc.sbuf_tensor(f"f0_wb2_{li}", [128, 16 * 1024], BF16) as WB2, \
                 nc.sbuf_tensor(f"f0_ht_{li}", [128, 16 * 512], BF16) as HT, \
                 nc.sbuf_tensor(f"f0_prb_{li}", [128, 16 * 8], F32) as PRB, \
                 nc.sbuf_tensor(f"f0_gt_{li}", [128, 16 * 8], BF16) as GT, \
                 nc.sbuf_tensor(f"f0_gf_{li}", [8, TOK], BF16) as GF, \
                 nc.sbuf_tensor(f"f0_eb2_{li}", [8, 1024], BF16) as EB2, \
                 nc.sbuf_tensor(f"f0_sb1_{li}", [128, 32], F32) as SB1, \
                 nc.sbuf_tensor(f"f0_sb2_{li}", [128, 8], F32) as SB2, \
                 nc.sbuf_tensor(f"f0_eb1_{li}", [128, 32], F32) as EB1, \
                 nc.sbuf_tensor(f"f0_rw_{li}", [128, 8 * 8], BF16) as RW:
                for nm, tt in (("f0_wb1", WB1), ("f0_wb2", WB2), ("f0_ht", HT),
                               ("f0_prb", PRB), ("f0_gt", GT), ("f0_gf", GF),
                               ("f0_eb2", EB2), ("f0_sb1", SB1),
                               ("f0_sb2", SB2), ("f0_eb1", EB1), ("f0_rw", RW)):
                    setattr(C, nm, tt)
                phase_barrier(C)
                if stage == "l0attn":
                    ffn_done = norm_done
                else:
                    ffn_done = emit_ffn_l0(C, norm_done, ln_done)
        if stage in ("l0ffn", "l0shared"):
            return ffn_done
        with nc.sbuf_tensor(f"XN_{li}", [128, DT * TFULL], BF16) as XN, \
             nc.sbuf_tensor(f"a_wq_{li}", [128, 8 * 512], BF16) as WQ, \
             nc.sbuf_tensor(f"a_wk_{li}", [128, 8 * 512], BF16) as WK, \
             nc.sbuf_tensor(f"a_wv_{li}", [128, 8 * 512], BF16) as WV, \
             nc.sbuf_tensor(f"a_qf_{li}", [128, TFULL], BF16) as QF, \
             nc.sbuf_tensor(f"a_kf_{li}", [128, TFULL], BF16) as KF, \
             nc.sbuf_tensor(f"a_kt_{li}", [128, TFULL], BF16) as KT, \
             nc.sbuf_tensor(f"a_vt_{li}", [128, 32 * 130], BF16) as VT, \
             nc.sbuf_tensor(f"a_ohq_{li}", [128, 2 * 512], BF16) as OHQ, \
             nc.sbuf_tensor(f"a_ss_{li}", [128, 64], BF16) as SS, \
             nc.sbuf_tensor(f"a_ssf_{li}", [128, 64], F32) as SSF, \
             nc.sbuf_tensor(f"a_pti_{li}", [128, 2 * 128], BF16) as PTI, \
             nc.sbuf_tensor(f"a_bq_{li}", [128, 12], F32) as BQ, \
             nc.sbuf_tensor(f"a_bkb_{li}", [128, 256], BF16) as BKB, \
             nc.sbuf_tensor(f"a_br_{li}", [1, 128], BF16) as BR:
            for nm, tt in (("att_wq", WQ), ("att_wk", WK), ("att_wv", WV),
                           ("att_qf", QF), ("att_kf", KF), ("att_kt", KT),
                           ("att_vt", VT), ("att_ohq", OHQ), ("att_ss", SS),
                           ("att_ssf", SSF),
                           ("att_pti", PTI), ("att_bq", BQ), ("att_bkb", BKB),
                           ("att_br", BR)):
                setattr(C, nm, tt)
            phase_barrier(C)
            xn_ready = emit_xn_assemble(C, li, XN, C.ch_sd.n)
            oh_done, vfl = emit_attn_l0(C, li, XN, xn_ready, C.scr_oh)
        with nc.sbuf_tensor(f"a_wo_{li}", [128, 8 * 1024], BF16) as WO, \
             nc.sbuf_tensor(f"a_ohs_{li}", [128, 8 * 512], BF16) as OHS, \
             nc.sbuf_tensor(f"a_bo_{li}", [128, 8], F32) as BO:
            C.att_ohs = OHS
            phase_barrier(C)
            wov = load_w_fm(C, "L0_woT", WO, D, D)
            bov = load_bias_fm(C, "L0_wob", BO, D)
            x_done = emit_wo(C, li, l, C.scr_oh, list(oh_done) + list(ffn_done) + [wov, bov],
                             vfl, WO, wov, bias_tile=BO, x_waits=list(ffn_done))
        return x_done
    if l == 1:
        with nc.sbuf_tensor(f"xno{li}", [128, DT * TOK], BF16) as xno:
            with nc.sbuf_tensor(f"sq{li}", [128, 2 * 4096], BF16) as sq:
                ln_done = emit_ln(C, xno, sq, x_waits)
            vw = sh_write(C, lambda slot, li=li: C.xn_sh[slot, li][:, :],
                          xno[:, :], 2 * li, ln_done)
            with nc.sbuf_tensor(f"f1_rw_{li}", [128, 8 * 128], BF16) as RW1, \
                 nc.sbuf_tensor(f"f1_zf_{li}", [128, TOK], BF16) as ZF, \
                 nc.sbuf_tensor(f"f1_sqz_{li}", [128, TOK], BF16) as SQZ, \
                 nc.sbuf_tensor(f"f1_rsd_{li}", [1, TOK], F32) as RSD, \
                 nc.sbuf_tensor(f"f1_ct_{li}", [16, 128], F32) as CTt, \
                 nc.sbuf_tensor(f"f1_cs_{li}", [16, 128], BF16) as CS, \
                 nc.sbuf_tensor(f"f1_cnt_{li}", [128, 16], BF16) as CNT, \
                 nc.sbuf_tensor(f"f1_lt_{li}", [128, 16 * 16], F32) as LT, \
                 nc.sbuf_tensor(f"f1_prb_{li}", [128, 16 * 16], F32) as PRB, \
                 nc.sbuf_tensor(f"f1_gt_{li}", [128, 16 * 16], BF16) as GT, \
                 nc.sbuf_tensor(f"f1_gf_{li}", [16, TOK], BF16) as GF1, \
                 nc.sbuf_tensor(f"f1_wie_{li}", [128, 8 * 512], BF16) as WIE, \
                 nc.sbuf_tensor(f"f1_woe_{li}", [128, 4 * 64], BF16) as WOE, \
                 nc.sbuf_tensor(f"f1_ht_{li}", [128, 4 * 512], BF16) as HT1, \
                 nc.sbuf_tensor(f"f1_t32_{li}", [128, 512], F32) as T32, \
                 nc.sbuf_tensor(f"f1_etb_{li}", [128, 1], F32) as ETB:
                for nm, tt in (("f1_rw", RW1), ("f1_zf", ZF), ("f1_sqz", SQZ),
                               ("f1_rsd", RSD), ("f1_ct", CTt), ("f1_cs", CS),
                               ("f1_cnt", CNT), ("f1_lt", LT), ("f1_prb", PRB),
                               ("f1_gt", GT), ("f1_gf", GF1), ("f1_wie", WIE),
                               ("f1_woe", WOE), ("f1_ht", HT1), ("f1_t32", T32),
                               ("f1_etb", ETB)):
                    setattr(C, nm, tt)
                phase_barrier(C)
                if stage == "attnonly":
                    ffn_done = ln_done
                else:
                    ffn_done = emit_ffn_l1(C, xno, ln_done, ln_done)
        if stage == "ffnonly":
            return ffn_done
        with nc.sbuf_tensor(f"XN_{li}", [128, DT * TFULL], BF16) as XN, \
             nc.sbuf_tensor(f"a_wq_{li}", [128, 8 * 128], BF16) as WQ, \
             nc.sbuf_tensor(f"a_wk_{li}", [128, 8 * 128], BF16) as WK, \
             nc.sbuf_tensor(f"a_wv_{li}", [128, 8 * 128], BF16) as WV, \
             nc.sbuf_tensor(f"a_qf_{li}", [128, TFULL], BF16) as QF, \
             nc.sbuf_tensor(f"a_kf_{li}", [128, TFULL], BF16) as KF, \
             nc.sbuf_tensor(f"a_vt_{li}", [128, 32 * 130], BF16) as VT, \
             nc.sbuf_tensor(f"a_ohq_{li}", [128, 2 * 512], BF16) as OHQ, \
             nc.sbuf_tensor(f"a_pt_{li}", [128, 4 * 512], BF16) as PT4, \
             nc.sbuf_tensor(f"a_sqb_{li}", [128, TFULL], BF16) as SQB, \
             nc.sbuf_tensor(f"a_o32_{li}", [128, TFULL], BF16) as O32, \
             nc.sbuf_tensor(f"a_t32_{li}", [128, 512], F32) as T32a, \
             nc.sbuf_tensor(f"a_bh_{li}", [128, 2], F32) as BH, \
             nc.sbuf_tensor(f"a_r0_{li}", [1, 2 * 512], BF16) as R0, \
             nc.sbuf_tensor(f"a_al_{li}", [1, 2], F32) as AL, \
             nc.sbuf_tensor(f"a_alb_{li}", [128, 2], F32) as ALB:
            for nm, tt in (("att_wq", WQ), ("att_wk", WK), ("att_wv", WV),
                           ("att_qf", QF), ("att_kf", KF), ("att_vt", VT),
                           ("att_ohq", OHQ), ("att_pt", PT4), ("att_sqb", SQB),
                           ("att_o32", O32), ("att_t32", T32a), ("att_bh", BH),
                           ("att_r0", R0), ("att_al", AL), ("att_alb", ALB)):
                setattr(C, nm, tt)
            phase_barrier(C)
            av = C.P.op("gpsimd", lambda g: g.dma_start(out=AL[0:1, 0:1],
                 in_=C.inp["L1_alpha"][0:1, 0:1]), inc=(C.ch_wd, 16))
            a1 = C.P.op("vector", lambda v: v.tensor_scalar(AL[0:1, 1:2], AL[0:1, 0:1],
                 -1.0, 1.0, ALU.mult, ALU.add), waits=[(C.ch_wd, av)],
                 inc=(C.ch_v, 1))
            mma = C.P.op("tensor", lambda t: t.matmul(C.PB[6][:, 0:2],
                 C.onesrow_f_s[0:1, :], AL[0:1, 0:2], start=True, stop=True),
                 waits=[(C.ch_v, a1)] + list(C.const_ready), inc=(C.ch_t, 1))
            C.P.op("vector", lambda v: v.tensor_copy(ALB[:, 0:2], C.PB[6][:, 0:2]),
                 waits=[(C.ch_t, mma)], inc=(C.ch_v, 1))
            cfg = {
                "load_q": lambda hp: load_w_fm_slice(C, "L1_W_QT", WQ, D, 512, hp * 128, 128),
                "load_k": lambda hp: load_w_fm_slice(C, "L1_W_KT", WK, D, 512, hp * 128, 128),
                "load_v": lambda hp: load_w_fm_slice(C, "L1_W_VT", WV, D, 512, hp * 128, 128),
                "load_k2": lambda hp: load_w_fm_slice(
                    C, "L1_W_K_primeT", WK, D, 512, hp * 128, 128,
                    waits=[(C.ch_s, C.ch_s.n)]),
                "passes": 2,
            }
            xn_ready = emit_xn_assemble(C, li, XN, C.ch_sd.n)
            oh_done, vfl = emit_attn_softmax_layer(C, li, l, XN, xn_ready,
                                                   C.scr_oh, cfg)
        with nc.sbuf_tensor(f"a_wo_{li}", [128, 8 * 1024], BF16) as WO, \
             nc.sbuf_tensor(f"a_ohs_{li}", [128, 8 * 512], BF16) as OHS:
            C.att_ohs = OHS
            phase_barrier(C)
            wov = load_w_fm(C, "L1_woT", WO, D, D)
            x_done = emit_wo(C, li, l, C.scr_oh, list(oh_done) + list(ffn_done) + [wov],
                             vfl, WO, wov, bias_tile=None, x_waits=list(ffn_done))
        return x_done
    if l == 2:
        with nc.sbuf_tensor(f"xno{li}", [128, DT * TOK], BF16) as xno:
            with nc.sbuf_tensor(f"sq{li}", [128, 2 * 4096], BF16) as sq:
                ln_done = emit_ln(C, xno, sq, x_waits)
            vw = sh_write(C, lambda slot, li=li: C.xn_sh[slot, li][:, :],
                          xno[:, :], 2 * li, ln_done)
            with nc.sbuf_tensor(f"f2_gw_{li}", [128, 8 * 8], BF16) as GW, \
                 nc.sbuf_tensor(f"f2_lt_{li}", [128, 16 * 8], F32) as LT, \
                 nc.sbuf_tensor(f"f2_prb_{li}", [128, 16 * 8], F32) as PRB, \
                 nc.sbuf_tensor(f"f2_gt_{li}", [128, 16 * 8], BF16) as GT, \
                 nc.sbuf_tensor(f"f2_gf_{li}", [8, TOK], BF16) as GF2, \
                 nc.sbuf_tensor(f"f2_wbg_{li}", [128, 8 * 1408], BF16) as WBg, \
                 nc.sbuf_tensor(f"f2_wbu_{li}", [128, 8 * 1408], BF16) as WBu, \
                 nc.sbuf_tensor(f"f2_wbd_{li}", [128, 11 * 1024], BF16) as WBd, \
                 nc.sbuf_tensor(f"f2_hti_{li}", [128, 11 * 512], BF16) as HTI, \
                 nc.sbuf_tensor(f"f2_sgf_{li}", [1, TOK], BF16) as SGF, \
                 nc.sbuf_tensor(f"f2_sgw_{li}", [128, 8 * 1], BF16) as SGW, \
                 nc.sbuf_tensor(f"f2_t32_{li}", [128, 512], F32) as T32:
                for nm, tt in (("f2_gw", GW), ("f2_lt", LT), ("f2_prb", PRB),
                               ("f2_gt", GT), ("f2_gf", GF2), ("f2_wbg", WBg),
                               ("f2_wbu", WBu), ("f2_wbd", WBd),
                               ("f2_hti", HTI), ("f2_sgf", SGF), ("f2_sgw", SGW),
                               ("f2_t32", T32)):
                    setattr(C, nm, tt)
                phase_barrier(C)
                if stage == "attnonly":
                    ffn_done = ln_done
                else:
                    ffn_done = emit_ffn_l2(C, xno, ln_done, ln_done)
        if stage == "ffnonly":
            return ffn_done
        with nc.sbuf_tensor(f"XN_{li}", [128, DT * TFULL], BF16) as XN, \
             nc.sbuf_tensor(f"a_wq_{li}", [128, 8 * 128], BF16) as WQ, \
             nc.sbuf_tensor(f"a_wk_{li}", [128, 8 * 128], BF16) as WK, \
             nc.sbuf_tensor(f"a_wv_{li}", [128, 8 * 128], BF16) as WV, \
             nc.sbuf_tensor(f"a_qf_{li}", [128, TFULL], BF16) as QF, \
             nc.sbuf_tensor(f"a_kf_{li}", [128, TFULL], BF16) as KF, \
             nc.sbuf_tensor(f"a_vt_{li}", [128, 32 * 130], BF16) as VT, \
             nc.sbuf_tensor(f"a_ohq_{li}", [128, 2 * 512], BF16) as OHQ, \
             nc.sbuf_tensor(f"a_pt_{li}", [128, 4 * 512], BF16) as PT4, \
             nc.sbuf_tensor(f"a_sqb_{li}", [128, TFULL], BF16) as SQB, \
             nc.sbuf_tensor(f"a_dm_{li}", [128, 4 * 512], BF16) as DM, \
             nc.sbuf_tensor(f"a_bh_{li}", [128, 2], F32) as BH, \
             nc.sbuf_tensor(f"a_t32_{li}", [128, 512], F32) as T32a, \
             nc.sbuf_tensor(f"a_r0_{li}", [1, 2 * 512], BF16) as R0:
            for nm, tt in (("att_wq", WQ), ("att_wk", WK), ("att_wv", WV),
                           ("att_qf", QF), ("att_kf", KF), ("att_vt", VT),
                           ("att_ohq", OHQ), ("att_pt", PT4), ("att_sqb", SQB),
                           ("att_bh", BH), ("att_r0", R0), ("att_t32", T32a)):
                setattr(C, nm, tt)
            C.diagm_s = DM
            phase_barrier(C)
            C.P.op("sync", lambda sy: sy.dma_start(out=DM[:, :], in_=C.diagm[:, :]),
                   inc=(C.ch_sd, 16))
            cfg = {
                "load_q": lambda hp: load_w_fm_slice(C, "L2_qT", WQ, D, 512, hp * 128, 128),
                "load_k": lambda hp: load_w_fm_slice(C, "L2_kT", WK, D, 512, hp * 128, 128),
                "load_v": lambda hp: load_w_fm_slice(C, "L2_vT", WV, D, 512, hp * 128, 128),
                "causal": True,
            }
            xn_ready = emit_xn_assemble(C, li, XN, C.ch_sd.n)
            oh_done, vfl = emit_attn_softmax_layer(C, li, l, XN, xn_ready,
                                                   C.scr_oh, cfg)
        with nc.sbuf_tensor(f"a_wo_{li}", [128, 8 * 1024], BF16) as WO, \
             nc.sbuf_tensor(f"a_ohs_{li}", [128, 8 * 512], BF16) as OHS:
            C.att_ohs = OHS
            phase_barrier(C)
            wov = load_w_fm(C, "L2_projT", WO, D, D)
            x_done = emit_wo(C, li, l, C.scr_oh, list(oh_done) + list(ffn_done) + [wov],
                             vfl, WO, wov, bias_tile=None, x_waits=list(ffn_done))
        return x_done
    if l == 3:
        with nc.sbuf_tensor(f"xno{li}", [128, DT * TOK], BF16) as xno:
            with nc.sbuf_tensor(f"sq{li}", [128, 2 * 4096], BF16) as sq:
                ln_done = emit_ln(C, xno, sq, x_waits)
            vw = sh_write(C, lambda slot, li=li: C.xn_sh[slot, li][:, :],
                          xno[:, :], 2 * li, ln_done)
        with nc.sbuf_tensor(f"XN_{li}", [128, DT * TFULL], BF16) as XN, \
             nc.sbuf_tensor(f"a_wq_{li}", [128, 8 * 128], BF16) as WQ, \
             nc.sbuf_tensor(f"a_wk_{li}", [128, 8 * 128], BF16) as WK, \
             nc.sbuf_tensor(f"a_wv_{li}", [128, 8 * 128], BF16) as WV, \
             nc.sbuf_tensor(f"a_qf_{li}", [128, TFULL], BF16) as QF, \
             nc.sbuf_tensor(f"a_kf_{li}", [128, TFULL], BF16) as KF, \
             nc.sbuf_tensor(f"a_vt_{li}", [128, 32 * 130], BF16) as VT, \
             nc.sbuf_tensor(f"a_ohq_{li}", [128, 2 * 512], BF16) as OHQ, \
             nc.sbuf_tensor(f"a_pt_{li}", [128, 4 * 512], BF16) as PT4, \
             nc.sbuf_tensor(f"a_rc_{li}", [128, TFULL], BF16) as RC, \
             nc.sbuf_tensor(f"a_rs2_{li}", [128, TFULL], BF16) as RS2, \
             nc.sbuf_tensor(f"a_qs_{li}", [128, TFULL], BF16) as QS, \
             nc.sbuf_tensor(f"a_bh_{li}", [128, 2], F32) as BH, \
             nc.sbuf_tensor(f"a_t32_{li}", [128, 512], F32) as T32a, \
             nc.sbuf_tensor(f"a_r0_{li}", [1, 2 * 512], BF16) as R0:
            for nm, tt in (("att_wq", WQ), ("att_wk", WK), ("att_wv", WV),
                           ("att_qf", QF), ("att_kf", KF), ("att_vt", VT),
                           ("att_ohq", OHQ), ("att_pt", PT4),
                           ("att_rc", RC), ("att_rs2", RS2), ("att_qs", QS),
                           ("att_bh", BH), ("att_r0", R0), ("att_t32", T32a)):
                setattr(C, nm, tt)
            C.att_sqb = QS
            phase_barrier(C)
            C.P.op("sync", lambda sy: sy.dma_start(out=RC[:, :], in_=C.ropec[:, :]),
                   inc=(C.ch_sd, 16))
            C.P.op("sync", lambda sy: sy.dma_start(out=RS2[:, :], in_=C.ropes[:, :]),
                   inc=(C.ch_sd, 16))
            rope_ready = [(C.ch_sd, C.ch_sd.n)]
            def ropefn(C2, QF2, KF2, ready, do_k):
                r = emit_rope(C2, QF2, list(ready) + rope_ready)
                if do_k:
                    r2 = emit_rope(C2, KF2, list(ready) + rope_ready + r)
                    C2._l3_krope = r2
                return list(r) + list(C2._l3_krope)
            cfg = {
                "load_q": lambda hp: load_w_fm_slice(C, "L3_wq", WQ, D, 512, hp * 128, 128),
                "load_k": lambda hp: load_w_fm(C, "L3_wk", WK, D, 128,
                                               waits=[(C.ch_t, C.ch_t.n)]),
                "load_v": lambda hp: load_w_fm(C, "L3_wv", WV, D, 128,
                                               waits=[(C.ch_t, C.ch_t.n)]),
                "kv_shared": True,
                "rope": ropefn,
            }
            xn_ready = emit_xn_assemble(C, li, XN, C.ch_sd.n)
            oh_done, vfl = emit_attn_softmax_layer(C, li, l, XN, xn_ready,
                                                   C.scr_oh, cfg)
        with nc.sbuf_tensor(f"a_wo_{li}", [128, 8 * 1024], BF16) as WO, \
             nc.sbuf_tensor(f"a_ohs_{li}", [128, 8 * 512], BF16) as OHS, \
             nc.sbuf_tensor(f"a_bo_{li}", [128, 8], F32) as BO:
            C.att_ohs = OHS
            phase_barrier(C)
            wov = load_w_fm(C, "L3_wo", WO, D, D)
            bov = load_bias_fm(C, "L3_bo", BO, D)
            attn_x_done = emit_wo(C, li, l, C.scr_oh, list(oh_done) + [wov, bov],
                                  vfl, WO, wov, bias_tile=BO, x_waits=list(ln_done))
        with nc.sbuf_tensor(f"q_bm_{li}", [128, 176], F32) as QBM, \
             nc.sbuf_tensor(f"q_rs_{li}", [128, 176], F32) as QRS, \
             nc.sbuf_tensor(f"q_stg_{li}", [128, 2816], BF16) as QSTG, \
             nc.sbuf_tensor(f"q_qd_{li}", [128, 2816], BF16) as QQD, \
             nc.sbuf_tensor(f"q_a_{li}", [128, 2816], BF16) as QA, \
             nc.sbuf_tensor(f"q_q_{li}", [128, 2816], BF16) as QQ, \
             nc.sbuf_tensor(f"q_sg_{li}", [128, 2816], BF16) as QSG, \
             nc.sbuf_tensor(f"f3_xn_{li}", [128, 8, 128], BF16) as F3XN, \
             nc.sbuf_tensor(f"f3_xq_{li}", [128, 8 * 512], BF16) as F3XQ, \
             nc.sbuf_tensor(f"f3_ht_{li}", [128, 512], BF16) as F3HT, \
             nc.sbuf_tensor(f"f3_hq_{li}", [128, 512], BF16) as F3HQ, \
             nc.sbuf_tensor(f"f3_hf_{li}", [128, 128], BF16) as F3HF:
            for nm, tt in (("q_bm", QBM), ("q_rs", QRS), ("q_stg", QSTG),
                           ("q_qd", QQD), ("q_a", QA), ("q_q", QQ), ("q_sg", QSG),
                           ("f3_xn", F3XN), ("f3_xq", F3XQ), ("f3_ht", F3HT),
                           ("f3_hq", F3HQ), ("f3_hf", F3HF)):
                setattr(C, nm, tt)
            C.q_t = QQD
            phase_barrier(C)
            with nc.sbuf_tensor(f"f3_wgq_{li}", [128, 8 * 2816], BF16) as WGQ, \
                 nc.sbuf_tensor(f"f3_wuq_{li}", [128, 8 * 2816], BF16) as WUQ:
                hq_done = emit_ffn_l3_phase1(C, li, WGQ, WUQ, attn_x_done)
            phase_barrier(C)
            with nc.sbuf_tensor(f"f3_wdq_{li}", [128, 22 * 1024], BF16) as WDQ, \
                 nc.sbuf_tensor(f"f3_hqf_{li}", [128, 22 * 512], BF16) as HQF:
                C.f3_hqf = HQF
                x_done = emit_ffn_l3_phase2(C, li, WDQ, hq_done, attn_x_done)
        return x_done
    raise NotImplementedError(f"layer {l}")

def emit_ffn_l0_norm(C, XNo, xn_ready, NW):
    P = C.P
    NORM, RSQ = C.f0_norm, C.f0_rsq
    nwv = load_bias_fm(C, "L0_normw", NW, D)
    nv = None
    prevst = None
    for c in range(4):
        sv = None
        for j in range(DT):
            sv = P.op("scalar", lambda s, j=j, c=c: s.activation(
                out=RSQ[:, j * 512:(j + 1) * 512],
                in_=XNo[:, j * TOK + c * 512: j * TOK + (c + 1) * 512],
                func=AF.Square), waits=(list(xn_ready) + [nwv] if (c == 0 and j == 0) else ()),
                inc=(C.ch_s, 1))
        mmv = None
        for j in range(DT):
            kw = dict(inc=(C.ch_t, 1)) if j == DT - 1 else {}
            w0 = [(C.ch_s, sv)] + ([prevst] if prevst else [])
            mmv = P.op("tensor", lambda t, j=j: t.matmul(
                C.PB[0][0:1, 0:512], C.onescol_b_s[:, 0:1],
                RSQ[:, j * 512:(j + 1) * 512], start=(j == 0), stop=(j == DT - 1)),
                waits=(w0 if j == 0 else ()), **kw)
        P.op("vector", lambda v: v.tensor_scalar(C.ln_ms[0:1, :], C.PB[0][0:1, 0:512],
             1.0 / 1024.0, 1e-8, ALU.mult, ALU.add), waits=[(C.ch_t, mmv)])
        vv = P.op("vector", lambda v: v.reciprocal(C.ln_rv[0:1, :], C.ln_ms[0:1, :]),
                  inc=(C.ch_v, 1))
        sv2 = P.op("scalar", lambda s: s.activation(out=C.ln_rs[0:1, :],
              in_=C.ln_rv[0:1, :], func=AF.Sqrt), waits=[(C.ch_v, vv)], inc=(C.ch_s, 1))
        mmb = P.op("tensor", lambda t: t.matmul(C.PB[1][:, 0:512], C.onesrow_f_s[0:1, :],
              C.ln_rs[0:1, :], start=True, stop=True), waits=[(C.ch_s, sv2)],
              inc=(C.ch_t, 1))
        for j in range(DT):
            P.op("vector", lambda v, j=j, c=c: v.tensor_mul(C.ln_t2[:, :],
                 XNo[:, j * TOK + c * 512: j * TOK + (c + 1) * 512], C.PB[1][:, 0:512]),
                 waits=([(C.ch_t, mmb)] if j == 0 else ()))
            nv = P.op("vector", lambda v, j=j, c=c: v.tensor_scalar(
                 NORM[:, j * TOK + c * 512: j * TOK + (c + 1) * 512],
                 C.ln_t2[:, :], NW[:, j:j + 1], None, ALU.mult), inc=(C.ch_v, 1))
        prevst = (C.ch_v, nv)
    return [(C.ch_v, nv)]

def run_cores(C, in_maps, **kw):
    return run_bass_kernel_spmd(C.nc, in_maps, core_ids=list(range(8)), **kw)


def load_w_fm(C, name, dst, rows, cols, waits=(), slow=False):
    src = C.inp[name].rearrange("(kc p) e -> p kc e", p=128)
    v = C.P.op("gpsimd", lambda g: g.dma_start(
        out=dst.rearrange("p (kc e) -> p kc e", e=cols)[:, :, :], in_=src,
        allow_slow_non_contiguous=slow),
        waits=waits, inc=(C.ch_wd, 16))
    return (C.ch_wd, v)

def load_w_fm3(C, name, idx, dst, rows, cols, waits=()):
    src = C.inp[name][idx].rearrange("(kc p) e -> p kc e", p=128)
    v = C.P.op("gpsimd", lambda g: g.dma_start(
        out=dst.rearrange("p (kc e) -> p kc e", e=cols)[:, :, :], in_=src),
        waits=waits, inc=(C.ch_wd, 16))
    return (C.ch_wd, v)

def load_w_fm_slice(C, name, dst, rows, cols, c0, cw, waits=()):
    src = C.inp[name].rearrange("(kc p) e -> p kc e", p=128)[:, :, c0:c0 + cw]
    v = C.P.op("gpsimd", lambda g: g.dma_start(
        out=dst.rearrange("p (kc e) -> p kc e", e=cw)[:, :, :], in_=src),
        waits=list(waits) + [(C.ch_t, C.ch_t.n)], inc=(C.ch_wd, 16))
    return (C.ch_wd, v)

def load_bias_fm(C, name, dst, n, waits=(), col0=0):
    v = C.P.op("gpsimd", lambda g: g.dma_start(
        out=dst[:, col0:col0 + n // 128],
        in_=C.inp[name].rearrange("(j p) -> p j", p=128),
        allow_slow_non_contiguous=True), waits=waits, inc=(C.ch_wd, 16))
    return (C.ch_wd, v)


def XNv(C, XN, j):
    return XN.rearrange("p (j t) -> p j t", j=DT)[:, j, :]

def emit_proj_fm(C, XN, xn_ready, wbuf, wcols, wc0, wready, dst, bias=None,
                 dst_col0=0, nrows=128):
    P = C.P
    pend = C.pend45
    for tc in range(TFULL // 512):
        ps = C.PB[4 + (tc % 2)]
        w0 = list(xn_ready) + [wready]
        if pend.get(tc % 2):
            w0.append(pend[tc % 2])
        mmv = None
        for k in range(DT):
            kw = dict(inc=(C.ch_t, 1)) if k == DT - 1 else {}
            mmv = P.op("tensor", lambda t, k=k, tc=tc, ps=ps: t.matmul(
                ps[0:nrows, 0:512],
                wbuf[:, k * wcols + wc0: k * wcols + wc0 + nrows],
                XNv(C, XN, k)[:, tc * 512:(tc + 1) * 512],
                start=(k == 0), stop=(k == DT - 1)),
                waits=(w0 if k == 0 else ()), **kw)
        if bias is not None:
            ev = P.op("scalar", lambda s, tc=tc, ps=ps: s.activation(
                out=dst[0:nrows, dst_col0 + tc * 512: dst_col0 + (tc + 1) * 512],
                in_=ps[0:nrows, 0:512], func=AF.Identity, bias=bias, scale=1.0),
                inc=(C.ch_s, 1), waits=[(C.ch_t, mmv)])
        else:
            ev = P.op("scalar", lambda s, tc=tc, ps=ps: s.copy(
                out=dst[0:nrows, dst_col0 + tc * 512: dst_col0 + (tc + 1) * 512],
                in_=ps[0:nrows, 0:512]), inc=(C.ch_s, 1), waits=[(C.ch_t, mmv)])
        pend[tc % 2] = (C.ch_s, ev)
    return [(C.ch_s, C.ch_s.n)]

def emit_proj_tm(C, XN, xn_ready, wbuf, wcols, wc0, wready, dstv, ncols,
                 bias_tile=None, src_view=None):
    P = C.P
    pend = C.pend45
    for tt in range(TFULL // 128):
        ps = C.PB[4 + (tt % 2)]
        w0 = list(xn_ready) + [wready]
        if pend.get(tt % 2):
            w0.append(pend[tt % 2])
        mmv = None
        for k in range(DT):
            kw = dict(inc=(C.ch_t, 1)) if k == DT - 1 else {}
            mmv = P.op("tensor", lambda t, k=k, tt=tt, ps=ps: t.matmul(
                ps[:, 0:ncols],
                XNv(C, XN, k)[:, tt * 128:(tt + 1) * 128],
                wbuf[:, k * wcols + wc0: k * wcols + wc0 + ncols],
                start=(k == 0), stop=(k == DT - 1)),
                waits=(w0 if k == 0 else ()), **kw)
        sview = (lambda p_: p_[:, 0:ncols]) if src_view is None else src_view
        if bias_tile is not None:
            ev = P.op("vector", lambda v, tt=tt, ps=ps: v.tensor_add(
                dstv(tt), sview(ps), bias_tile),
                inc=(C.ch_v, 1), waits=[(C.ch_t, mmv)])
            pend[tt % 2] = (C.ch_v, ev)
        else:
            ev = P.op("scalar", lambda s, tt=tt, ps=ps: s.copy(
                out=dstv(tt), in_=sview(ps)), inc=(C.ch_s, 1),
                waits=[(C.ch_t, mmv)])
            pend[tt % 2] = (C.ch_s, ev)
    return [(C.ch_s, C.ch_s.n), (C.ch_v, C.ch_v.n)]

def emit_head_bounds(C, QF, KF, ready, bh, scale=0.125):
    P = C.P
    SQB = C.att_sqb
    NR = C.att_nr
    mx = {}
    for si, src in enumerate((QF, KF)):
        sqw = list(ready)
        if si == 1:
            sqw.append((C.ch_t, C.ch_t.n))
        sv = P.op("scalar", lambda s, src=src: s.activation(
            out=SQB[:, :], in_=src[:, :], func=AF.Square), inc=(C.ch_s, 1),
            waits=sqw)
        for h in range(2):
            first = True
            for tc in range(TFULL // 512):
                prev = {(0, 1): (0, 0), (1, 0): (0, 1), (1, 1): (1, 0)}.get((si, h))
                fw = [(C.ch_s, sv)] + ([mx[prev]] if prev else [])
                mmv = P.op("tensor", lambda t, tc=tc, h=h: t.matmul(
                    C.PB[6][0:1, 0:512], C.sel2h_s[:, h:h + 1],
                    SQB[:, tc * 512:(tc + 1) * 512],
                    start=True, stop=True),
                    waits=(fw if first else [mx[(si, h)]]),
                    inc=(C.ch_t, 1))
                P.op("vector", lambda v, tc=tc: v.tensor_reduce(
                    NR[0:1, 8:9], C.PB[6][0:1, 0:512], axis=AX.X, op=ALU.max),
                    waits=[(C.ch_t, mmv)])
                if tc == 0:
                    vv = P.op("vector", lambda v, si=si, h=h: v.tensor_copy(
                        NR[0:1, si * 2 + h: si * 2 + h + 1], NR[0:1, 8:9]),
                        inc=(C.ch_v, 1))
                else:
                    vv = P.op("vector", lambda v, si=si, h=h: v.tensor_max(
                        NR[0:1, si * 2 + h: si * 2 + h + 1],
                        NR[0:1, si * 2 + h: si * 2 + h + 1], NR[0:1, 8:9]),
                        inc=(C.ch_v, 1))
                mx[(si, h)] = (C.ch_v, vv)
                first = False
    out_waits = []
    for h in range(2):
        vmu = P.op("vector", lambda v, h=h: v.tensor_mul(
            NR[0:1, 9:10], NR[0:1, 0 + h:1 + h], NR[0:1, 2 + h:3 + h]),
            waits=[mx[(0, h)], mx[(1, h)]], inc=(C.ch_v, 1))
        sv = P.op("scalar", lambda s: s.activation(
            out=NR[0:1, 10:11], in_=NR[0:1, 9:10], func=AF.Sqrt),
            waits=[(C.ch_v, vmu)], inc=(C.ch_s, 1))
        vv = P.op("vector", lambda v: v.tensor_scalar(
            NR[0:1, 11:12], NR[0:1, 10:11], -scale, None, ALU.mult),
            waits=[(C.ch_s, sv)], inc=(C.ch_v, 1))
        mmv = P.op("tensor", lambda t: t.matmul(
            C.PB[6][:, 0:1], C.onesrow_f_s[0:1, :], NR[0:1, 11:12],
            start=True, stop=True), waits=[(C.ch_v, vv)], inc=(C.ch_t, 1))
        ev = P.op("vector", lambda v, h=h: v.tensor_copy(bh[:, h:h + 1],
            C.PB[6][:, 0:1]), waits=[(C.ch_t, mmv)], inc=(C.ch_v, 1))
        out_waits.append((C.ch_v, ev))
    return out_waits

def emit_softmax_pass(C, li, l, QF, KF, VT, bh, ready, qt_sink, causal=False):
    P = C.P
    PT = C.att_pt
    pend_pt = {}
    pend_out = None
    for qt in range(TFULL // 512):
        kts = list(range((qt + 1) * 4)) if causal else list(range(32))
        mm_pv_last = None
        for ki, kt in enumerate(kts):
            b = ki % 2
            scps = (C.PB[2 * b], C.PB[2 * b + 1])
            w0 = list(ready)
            if b in pend_pt:
                w0.append(pend_pt[b])
            if ki == 0 and pend_out is not None:
                w0.append(pend_out)
            for h in range(2):
                base = h * 64
                kw = dict(inc=(C.ch_t, 1)) if h == 1 else {}
                mmsc = P.op("tensor", lambda t, h=h, base=base, kt=kt, qt=qt, ps=scps[h]: t.matmul(
                    ps[:, 0:512], KF[base:base + 64, kt * 128:(kt + 1) * 128],
                    QF[base:base + 64, qt * 512:(qt + 1) * 512],
                    start=True, stop=True,
                    tile_position=(base, 0)),
                    waits=(w0 if h == 0 else ()), **kw)
            ev = None
            for h in range(2):
                ev = P.op("scalar", lambda s, h=h, b=b, ps=scps[h]: s.activation(
                    out=PT[:, (2 * b + h) * 512:(2 * b + h + 1) * 512],
                    in_=ps[:, 0:512], func=AF.Exp, scale=0.125,
                    bias=bh[:, h:h + 1]), waits=[(C.ch_t, mmsc)], inc=(C.ch_s, 1))
            ptv = (C.ch_s, ev)
            if causal and kt >= qt * 4:
                dmi = kt - qt * 4
                for h in range(2):
                    ev2 = P.op("vector", lambda v, h=h, b=b, dmi=dmi: v.tensor_mul(
                        PT[:, (2 * b + h) * 512:(2 * b + h + 1) * 512],
                        PT[:, (2 * b + h) * 512:(2 * b + h + 1) * 512],
                        C.diagm_s[:, dmi * 512:(dmi + 1) * 512]),
                        waits=[ptv], inc=(C.ch_v, 1))
                ptv = (C.ch_v, ev2)
            for h in range(2):
                ops = C.PB[4 + h]
                kw = dict(inc=(C.ch_t, 1)) if h == 1 else {}
                mm_pv_last = P.op("tensor", lambda t, h=h, b=b, kt=kt, ops=ops,
                                  first=(ki == 0), last=(ki == len(kts) - 1): t.matmul(
                    ops[0:65, 0:512], VT[:, kt * 130 + h * 65: kt * 130 + (h + 1) * 65],
                    PT[:, (2 * b + h) * 512:(2 * b + h + 1) * 512],
                    start=first, stop=last),
                    waits=([ptv] if h == 0 else ()), **kw)
            pend_pt[b] = (C.ch_t, mm_pv_last)
        pend_out = qt_sink(qt, C.PB[4], C.PB[5], [(C.ch_t, mm_pv_last)])
    return pend_out

def emit_attn_softmax_layer(C, li, l, XN, xn_ready, scr_oh, cfg):
    P = C.P
    WQ = C.att_wq; WK = C.att_wk; WV = C.att_wv
    QF = C.att_qf; KF = C.att_kf; VT = C.att_vt
    OHQ = C.att_ohq
    bh = C.att_bh
    R0 = C.att_r0
    msv = P.op("gpsimd", lambda g: g.memset(VT[:, :], 1.0), inc=(C.ch_g, 1))
    oh_done = []
    for hp in range(NHP):
        xn_ready2 = list(xn_ready) + [(C.ch_t, C.ch_t.n), (C.ch_s, C.ch_s.n),
                                      (C.ch_v, C.ch_v.n), (C.ch_g, msv)]
        wvq = cfg["load_q"](hp)
        pq = emit_proj_fm(C, XN, xn_ready2, WQ, 128, 0, wvq, QF)
        if cfg.get("kv_shared") and hp > 0:
            pk, pv = C._l3_pk, C._l3_pv
        else:
            wvk = cfg["load_k"](hp)
            wvv = cfg["load_v"](hp)
            pk = emit_proj_fm(C, XN, xn_ready2, WK, 128, 0, wvk, KF)
            def vdst(tt):
                return VT.rearrange("p (kt hb e) -> p kt hb e", hb=2, e=65)[:, tt, :, 0:64]
            pv = emit_proj_tm(C, XN, xn_ready2, WV, 128, 0, wvv,
                              vdst, 128,
                              src_view=lambda p_: p_[:, 0:128].rearrange(
                                  "p (hb e) -> p hb e", e=64))
            if cfg.get("kv_shared"):
                C._l3_pk, C._l3_pv = pk, pv
        ready = pq + pk + pv
        if cfg.get("rope"):
            ready = cfg["rope"](C, QF, KF, ready, do_k=(hp == 0))
        passes = cfg.get("passes", 1)
        oh_slot_pend = [None, None]
        for pi in range(passes):
            if pi == 1:
                wk2 = cfg["load_k2"](hp)
                pk2 = emit_proj_fm(C, XN, list(xn_ready2) + [(C.ch_t, C.ch_t.n)],
                                   WK, 128, 0, wk2, KF)
                ready = pq + pk2
            bw = emit_head_bounds(C, QF, KF, ready, bh)
            ready2 = list(ready) + bw

            def qt_sink(qt, ps0, ps1, mmw, pi=pi, hp=hp):
                vv = None
                for h, ps in ((0, ps0), (1, ps1)):
                    vv = P.op("vector", lambda v, h=h, ps=ps: v.reciprocal(
                        R0[0:1, h * 512:(h + 1) * 512], ps[64:65, 0:512]),
                        waits=mmw, inc=(C.ch_v, 1))
                slot = qt % 2
                out_free = []
                prev_ev = None
                for h, ps in ((0, ps0), (1, ps1)):
                    bw2 = [(C.ch_v, vv)] + ([prev_ev] if prev_ev else [])
                    if pi == passes - 1 and h == 0 and oh_slot_pend[slot] is not None:
                        bw2.append(oh_slot_pend[slot])
                    mmb = P.op("tensor", lambda t, h=h: t.matmul(
                        C.PB[6][0:64, 0:512], C.onesrow_b_s[0:1, 0:64],
                        R0[0:1, h * 512:(h + 1) * 512], start=True, stop=True),
                        waits=bw2, inc=(C.ch_t, 1))
                    if passes == 1:
                        b2 = h * 64
                        sc = P.op("scalar", lambda s, ps=ps, b2=b2: s.copy(
                            out=C.att_t32[b2:b2 + 64, 0:512], in_=ps[0:64, 0:512]),
                            waits=[(C.ch_t, mmb), (C.ch_v, C.ch_v.n)], inc=(C.ch_s, 1))
                        ev = P.op("vector", lambda v, h=h, slot=slot, b2=b2: v.tensor_mul(
                            OHQ[h * 64:(h + 1) * 64, slot * 512:(slot + 1) * 512],
                            C.att_t32[b2:b2 + 64, 0:512], C.PB[6][0:64, 0:512]),
                            waits=[(C.ch_s, sc)], inc=(C.ch_v, 1))
                    elif pi == 0:
                        b2 = h * 64
                        sc0 = P.op("scalar", lambda s, ps=ps, b2=b2: s.copy(
                            out=C.att_t32[b2:b2 + 64, 0:512], in_=ps[0:64, 0:512]),
                            waits=[(C.ch_t, mmb), (C.ch_v, C.ch_v.n)], inc=(C.ch_s, 1))
                        P.op("vector", lambda v, b2=b2: v.tensor_mul(
                            C.att_t32[b2:b2 + 64, 0:512], C.att_t32[b2:b2 + 64, 0:512],
                            C.PB[6][0:64, 0:512]), waits=[(C.ch_s, sc0)])
                        ev = P.op("vector", lambda v, h=h, qt=qt, b2=b2: v.tensor_scalar(
                            C.att_o32[h * 64:(h + 1) * 64, qt * 512:(qt + 1) * 512],
                            C.att_t32[b2:b2 + 64, 0:512], C.att_alb[b2:b2 + 64, 0:1], None,
                            ALU.mult), inc=(C.ch_v, 1))
                    else:
                        b2 = h * 64
                        sc0 = P.op("scalar", lambda s, ps=ps, b2=b2: s.copy(
                            out=C.att_t32[b2:b2 + 64, 0:512], in_=ps[0:64, 0:512]),
                            waits=[(C.ch_t, mmb), (C.ch_v, C.ch_v.n)], inc=(C.ch_s, 1))
                        P.op("vector", lambda v, b2=b2: v.tensor_mul(
                            C.att_t32[b2:b2 + 64, 0:512], C.att_t32[b2:b2 + 64, 0:512],
                            C.PB[6][0:64, 0:512]), waits=[(C.ch_s, sc0)])
                        P.op("vector", lambda v, b2=b2: v.tensor_scalar(
                            C.att_t32[b2:b2 + 64, 0:512], C.att_t32[b2:b2 + 64, 0:512],
                            C.att_alb[b2:b2 + 64, 1:2], None, ALU.mult))
                        ev = P.op("vector", lambda v, h=h, qt=qt, slot=slot, b2=b2: v.tensor_add(
                            OHQ[h * 64:(h + 1) * 64, slot * 512:(slot + 1) * 512],
                            C.att_t32[b2:b2 + 64, 0:512],
                            C.att_o32[h * 64:(h + 1) * 64, qt * 512:(qt + 1) * 512]),
                            inc=(C.ch_v, 1))
                    out_free.append((C.ch_v, ev))
                    prev_ev = (C.ch_v, ev)
                if pi == passes - 1:
                    dv = P.op("sync", lambda sy, hp=hp, qt=qt, slot=slot: sy.dma_start(
                        out=scr_oh[hp][:, qt * 512:(qt + 1) * 512],
                        in_=OHQ[:, slot * 512:(slot + 1) * 512]),
                        waits=[prev_ev], inc=(C.ch_sd, 16))
                    oh_slot_pend[slot] = (C.ch_sd, C.ch_sd.n)
                return out_free[-1]

            emit_softmax_pass(C, li, l, QF, KF, VT, bh, ready2, qt_sink,
                              causal=cfg.get("causal", False))
        oh_done.append((C.ch_sd, C.ch_sd.n))
    vfl = emit_oh_ship(C, li, scr_oh, oh_done)
    return oh_done, vfl

def emit_wo(C, li, l, scr_oh, oh_waits, vfl, WO, wo_ready, bias_tile=None,
            x_waits=()):
    P = C.P
    OHS = C.att_ohs
    done = []
    pend = {}
    for tc in range(TOK // 512):
        v0 = C.ch_sd.bump(32)
        def emit_ld(sy, tc=tc, v0=v0):
            h = P.holder
            for myp in (0, 1):
                guard = sy.If_eq(h["par"], 0) if myp == 0 else sy.Else()
                with guard:
                    sy.dma_start(
                        out=OHS.rearrange("p (b e) -> p b e", e=512)[:, 0:4, :],
                        in_=scr_oh[:, :, myp * TOK + tc * 512: myp * TOK + (tc + 1) * 512].rearrange(
                            "h p e -> p h e"),
                    ).then_inc(C.ch_sd.h, 16)
                    sy.dma_start(
                        out=OHS.rearrange("p (b e) -> p b e", e=512)[:, 4:8, :],
                        in_=C.oh_sh[1 - myp, li][:, :, tc * 512:(tc + 1) * 512].rearrange(
                            "h p e -> p h e"),
                    ).then_inc(C.ch_sd.h, 16)
            sy.end_ifs()
            return sy.nop()
        if tc == 0:
            vpoll = sh_poll_read(C, 2 * li + 1, [], extra_waits=[])
        P.op("sync", emit_ld, waits=list(oh_waits) + list(pend.get(("ld", tc - 1), ())))
        ldv = (C.ch_sd, v0)
        for m in range(DT):
            ps = C.PB[m % 4]
            w0 = [ldv, wo_ready] + list(x_waits)
            if ("ps", m % 4) in pend:
                w0.append(pend[("ps", m % 4)])
            mmv = None
            for ct in range(8):
                kw = dict(inc=(C.ch_t, 1)) if ct == 7 else {}
                mmv = P.op("tensor", lambda t, m=m, ct=ct, ps=ps: t.matmul(
                    ps[:, 0:512], WO[:, ct * 1024 + m * 128:(ct * 1024 + m * 128) + 128],
                    OHS[:, ct * 512:(ct + 1) * 512],
                    start=(ct == 0), stop=(ct == 7)),
                    waits=(w0 if ct == 0 else ()), **kw)
            if bias_tile is not None:
                ev = P.op("vector", lambda v, m=m, tc=tc, ps=ps: v.scalar_tensor_tensor(
                    C.X[:, m * TOK + tc * 512: m * TOK + (tc + 1) * 512],
                    ps[:, 0:512], bias_tile[:, m:m + 1],
                    C.X[:, m * TOK + tc * 512: m * TOK + (tc + 1) * 512],
                    ALU.add, ALU.add), waits=[(C.ch_t, mmv)], inc=(C.ch_v, 1))
            else:
                ev = P.op("vector", lambda v, m=m, tc=tc, ps=ps: v.tensor_add(
                    C.X[:, m * TOK + tc * 512: m * TOK + (tc + 1) * 512],
                    C.X[:, m * TOK + tc * 512: m * TOK + (tc + 1) * 512],
                    ps[:, 0:512]), waits=[(C.ch_t, mmv)], inc=(C.ch_v, 1))
            pend[("ps", m % 4)] = (C.ch_v, ev)
            done.append((C.ch_v, ev))
        pend[("ld", tc)] = [(C.ch_t, C.ch_t.n)]
    return [done[-1]]


def emit_attn_l0(C, li, XN, xn_ready, scr_oh):
    P = C.P
    WQ, WK, WV = C.att_wq, C.att_wk, C.att_wv
    QF, KF, VT, KT = C.att_qf, C.att_kf, C.att_vt, C.att_kt
    OHQ = C.att_ohq
    SS = C.att_ss
    SSF = C.att_ssf
    PTI = C.att_pti
    BQ = C.att_bq
    BKB = C.att_bkb
    wvq = load_w_fm(C, "L0_wqT", WQ, D, 512)
    wvk = load_w_fm(C, "L0_wkT", WK, D, 512)
    wvv = load_w_fm(C, "L0_wvT", WV, D, 512)
    bq = load_bias_fm(C, "L0_wqb", BQ, 512, col0=0)
    bk = load_bias_fm(C, "L0_wkb", BQ, 512, col0=4)
    bv = load_bias_fm(C, "L0_wvb", BQ, 512, col0=8)
    oh_done = []
    for hp in range(NHP):
        xn_ready = list(xn_ready) + [(C.ch_t, C.ch_t.n), (C.ch_s, C.ch_s.n),
                                     (C.ch_v, C.ch_v.n)]
        BR = C.att_br
        v1 = P.op("gpsimd", lambda g, hp=hp: g.dma_start(
            out=BR[0:1, 0:128], in_=C.inp["L0_wkb"][hp * 128:(hp + 1) * 128].rearrange("(a e) -> a e", a=1)),
            inc=(C.ch_wd, 16))
        mmk = P.op("tensor", lambda t: t.matmul(C.PB[6][:, 0:128], C.onesrow_b_s[0:1, :],
                   BR[0:1, 0:128], start=True, stop=True),
                   waits=[(C.ch_wd, v1)] + list(C.const_ready) + list(xn_ready),
                   inc=(C.ch_t, 1))
        vk1 = P.op("vector", lambda v: v.tensor_copy(BKB[:, 0:128], C.PB[6][:, 0:128]),
                   waits=[(C.ch_t, mmk)], inc=(C.ch_v, 1))
        v2 = P.op("gpsimd", lambda g, hp=hp: g.dma_start(
            out=BR[0:1, 0:128], in_=C.inp["L0_wvb"][hp * 128:(hp + 1) * 128].rearrange("(a e) -> a e", a=1)),
            waits=[(C.ch_t, mmk)], inc=(C.ch_wd, 16))
        mmv2 = P.op("tensor", lambda t: t.matmul(C.PB[6][:, 0:128], C.onesrow_b_s[0:1, :],
                    BR[0:1, 0:128], start=True, stop=True),
                    waits=[(C.ch_v, vk1), (C.ch_wd, v2)], inc=(C.ch_t, 1))
        vk2 = P.op("vector", lambda v: v.tensor_copy(BKB[:, 128:256], C.PB[6][:, 0:128]),
                   waits=[(C.ch_t, mmv2)], inc=(C.ch_v, 1))
        bias_ready = [(C.ch_v, vk2)]
        pq = emit_proj_fm(C, XN, list(xn_ready) + [bq], WQ, 512, hp * 128, wvq, QF,
                          bias=BQ[:, hp:hp + 1])
        pk = emit_proj_fm(C, XN, list(xn_ready) + [bk, bv], WK, 512, hp * 128, wvk, KF,
                          bias=BQ[:, 4 + hp:5 + hp])
        def ktdst(tt):
            return KT[:, tt * 128:(tt + 1) * 128]
        pkt = emit_proj_tm(C, XN, list(xn_ready) + bias_ready, WK, 512, hp * 128,
                           wvk, ktdst, 128, bias_tile=BKB[:, 0:128])
        def vdst(tt):
            return VT.rearrange("p (kt hb e) -> p kt hb e", hb=2, e=65)[:, tt, :, 0:64]
        pv = emit_proj_tm(C, XN, list(xn_ready) + bias_ready, WV, 512, hp * 128,
                          wvv, vdst, 128,
                          bias_tile=BKB[:, 128:256].rearrange("p (hb e) -> p hb e", e=64),
                          src_view=lambda p_: p_[:, 0:128].rearrange("p (hb e) -> p hb e", e=64))
        ready = pq + pk + pkt + pv
        P.op("gpsimd", lambda g: g.memset(SS[:, :], 0.0), waits=ready,
             inc=(C.ch_g, 1))
        gz = P.op("gpsimd", lambda g: g.memset(SSF[:, :], 0.0), waits=ready,
                  inc=(C.ch_g, 1))
        snapv = (C.ch_g, gz)
        pend_out = None
        pend_pti = None
        supd_v = None
        oh_slot_pend = [None, None]
        for ct in range(TFULL // 128):
            w0 = list(ready) + [snapv]
            if pend_out is not None:
                w0.append(pend_out)
            for h in range(2):
                base = h * 64
                P.op("tensor", lambda t, h=h, base=base, ct=ct: t.matmul(
                    C.PB[4 + h][0:64, 0:128], SS[base:base + 64, 0:64],
                    QF[base:base + 64, ct * 128:(ct + 1) * 128],
                    start=True, stop=False, tile_position=(base, 0)),
                    waits=(w0 if h == 0 else ()))
            mmsc = None
            for h in range(2):
                base = h * 64
                kw = dict(inc=(C.ch_t, 1)) if h == 1 else {}
                scw = [pend_pti] if (pend_pti is not None and h == 0) else ()
                mmsc = P.op("tensor", lambda t, h=h, base=base, ct=ct: t.matmul(
                    C.PB[h][:, 0:128], KF[base:base + 64, ct * 128:(ct + 1) * 128],
                    QF[base:base + 64, ct * 128:(ct + 1) * 128],
                    start=True, stop=True, tile_position=(base, 0)),
                    waits=scw, **kw)
            ev = None
            for h in range(2):
                ev = P.op("vector", lambda v, h=h: v.tensor_mul(
                    PTI[:, h * 128:(h + 1) * 128], C.PB[h][:, 0:128],
                    C.trilT_s[:, :]), waits=[(C.ch_t, mmsc)], inc=(C.ch_v, 1))
            mmpv = None
            for h in range(2):
                kw = dict(inc=(C.ch_t, 1)) if h == 1 else {}
                mmpv = P.op("tensor", lambda t, h=h, ct=ct: t.matmul(
                    C.PB[4 + h][0:64, 0:128],
                    VT[:, ct * 130 + h * 65: ct * 130 + h * 65 + 64],
                    PTI[:, h * 128:(h + 1) * 128],
                    start=False, stop=True),
                    waits=([(C.ch_v, ev)] if h == 0 else ()), **kw)
            pend_pti = (C.ch_t, mmpv)
            w1 = [snapv]
            supd = None
            for h in range(2):
                kw = dict(inc=(C.ch_t, 1)) if h == 1 else {}
                supd = P.op("tensor", lambda t, h=h, ct=ct: t.matmul(
                    C.PB[2 + h][0:64, 0:64],
                    KT[:, ct * 128 + h * 64: ct * 128 + (h + 1) * 64],
                    VT[:, ct * 130 + h * 65: ct * 130 + h * 65 + 64],
                    start=True, stop=True),
                    waits=(w1 if h == 0 else ()), **kw)
            supd_v = (C.ch_t, supd)
            P.op("vector", lambda v: v.tensor_add(SSF[0:64, :], SSF[0:64, :],
                 C.PB[2][0:64, 0:64]), waits=[supd_v])
            P.op("vector", lambda v: v.tensor_add(SSF[64:128, :], SSF[64:128, :],
                 C.PB[3][0:64, 0:64]))
            gz2 = P.op("vector", lambda v: v.tensor_copy(SS[:, :], SSF[:, :]),
                       inc=(C.ch_v, 1))
            snapv = (C.ch_v, gz2)
            slot = ct % 2
            ev2 = None
            w_ev = [(C.ch_t, mmpv)]
            if oh_slot_pend[slot] is not None:
                w_ev.append(oh_slot_pend[slot])
            for h in range(2):
                ev2 = P.op("scalar", lambda s, h=h, slot=slot: s.copy(
                    out=OHQ[h * 64:(h + 1) * 64, slot * 128:(slot + 1) * 128],
                    in_=C.PB[4 + h][0:64, 0:128]), waits=(w_ev if h == 0 else ()),
                    inc=(C.ch_s, 1))
            pend_out = (C.ch_s, ev2)
            dv = P.op("sync", lambda sy, hp=hp, ct=ct, slot=slot: sy.dma_start(
                out=scr_oh[hp][:, ct * 128:(ct + 1) * 128],
                in_=OHQ[:, slot * 128:(slot + 1) * 128]),
                waits=[pend_out], inc=(C.ch_sd, 16))
            oh_slot_pend[slot] = (C.ch_sd, C.ch_sd.n)
        oh_done.append((C.ch_sd, C.ch_sd.n))
    vfl = emit_oh_ship(C, li, scr_oh, oh_done)
    return oh_done, vfl

def emit_oh_ship(C, li, scr_oh, oh_done):
    P = C.P
    vsh = C.ch_sd.bump(16)
    vfl = C.ch_sd.bump(16)
    def emit_ship(sy):
        h = P.holder
        for myp in (0, 1):
            guard = sy.If_eq(h["par"], 0) if myp == 0 else sy.Else()
            with guard:
                sy.dma_start(
                    out=C.oh_sh[myp, li][:, :, :],
                    in_=scr_oh[:, :, (1 - myp) * TOK:(2 - myp) * TOK],
                ).then_inc(C.ch_sd.h, 16)
        sy.end_ifs()
        sy.wait_ge(C.ch_sd.h, vsh)
        for myp in (0, 1):
            guard = sy.If_eq(h["par"], 0) if myp == 0 else sy.Else()
            with guard:
                sy.dma_start(out=C.flg[myp, 2 * li + 1, 0:2],
                             in_=C.flagv[0:1, 2 * (2 * li + 1):2 * (2 * li + 1) + 2]
                             ).then_inc(C.ch_sd.h, 16)
        sy.end_ifs()
        return sy.nop()
    P.op("sync", emit_ship, waits=oh_done)
    return vfl


from concourse.ap import AP as _AP

def bc0(ap2d, inner):
    assert len(ap2d.ap) == 2
    return _AP(tensor=ap2d.tensor, offset=ap2d.offset,
               ap=[[ap2d.ap[0][0], ap2d.ap[0][1]],
                   [ap2d.ap[1][0], ap2d.ap[1][1]], [0, inner]])

def emit_top2(C, PRB, GT, tlist, renorm_eps=None, probs_ready=()):
    P = C.P
    E = (PRB.shape[1] if hasattr(PRB, "shape") else None)
    last = None
    for ti, E_ in tlist:
        s = C.ffn_srow
        sl = slice(ti * E_, (ti + 1) * E_)
        w0 = list(probs_ready)
        P.op("vector", lambda v, sl=sl: v.tensor_reduce(
            s[:, 0:1], PRB[:, sl], axis=AX.X, op=ALU.max), waits=w0)
        P.op("vector", lambda v, sl=sl: v.tensor_scalar(
            C.ffn_trow[:, 0:E_], PRB[:, sl], s[:, 0:1], -1e30, ALU.is_ge, ALU.mult))
        P.op("vector", lambda v, sl=sl: v.tensor_add(
            C.ffn_trow[:, 0:E_], C.ffn_trow[:, 0:E_], PRB[:, sl]))
        P.op("vector", lambda v: v.tensor_reduce(
            s[:, 1:2], C.ffn_trow[:, 0:E_], axis=AX.X, op=ALU.max))
        P.op("vector", lambda v, sl=sl: v.tensor_scalar(
            C.ffn_trow[:, 0:E_], PRB[:, sl], s[:, 1:2], None, ALU.is_ge))
        vv = P.op("vector", lambda v, sl=sl: v.tensor_mul(
            C.ffn_trow[:, 0:E_], C.ffn_trow[:, 0:E_], PRB[:, sl]), inc=(C.ch_v, 1))
        if renorm_eps is not None:
            P.op("vector", lambda v: v.tensor_reduce(
                s[:, 2:3], C.ffn_trow[:, 0:E_], axis=AX.X, op=ALU.add))
            if renorm_eps != 0.0:
                P.op("vector", lambda v: v.tensor_scalar(
                    s[:, 2:3], s[:, 2:3], renorm_eps, None, ALU.add))
            P.op("vector", lambda v: v.reciprocal(s[:, 2:3], s[:, 2:3]))
            P.op("vector", lambda v: v.tensor_scalar(
                C.ffn_trow[:, 0:E_], C.ffn_trow[:, 0:E_], s[:, 2:3], None, ALU.mult))
        last = P.op("vector", lambda v, sl=sl: v.tensor_copy(GT[:, sl],
            C.ffn_trow[:, 0:E_]), inc=(C.ch_v, 1))
    return [(C.ch_v, last)]

def emit_gate_transpose(C, GT, GF, nt, E, ready):
    P = C.P
    last = None
    pend = None
    for ti in range(nt):
        w0 = list(ready)
        if pend is not None:
            w0.append(pend)
        mmv = P.op("tensor", lambda t, ti=ti: t.transpose(
            C.PT[0:E, 0:128], GT[:, ti * E:(ti + 1) * E], C.identb_s[:, :]),
            waits=w0, inc=(C.ch_t, 1))
        last = P.op("vector", lambda v, ti=ti: v.tensor_copy(
            GF[0:E, ti * 128:(ti + 1) * 128], C.PT[0:E, 0:128]),
            waits=[(C.ch_t, mmv)], inc=(C.ch_v, 1))
        pend = (C.ch_v, last)
    return [(C.ch_v, last)]


def emit_ffn_l0(C, norm_ready, x_waits):
    P = C.P
    NORM = C.f0_norm
    WB1 = C.f0_wb1
    WB2 = C.f0_wb2
    HT = C.f0_ht
    PRB = C.f0_prb
    GT = C.f0_gt
    GF = C.f0_gf
    EB2 = C.f0_eb2
    SB1 = C.f0_sb1
    SB2 = C.f0_sb2
    EB1 = C.f0_eb1
    RW = C.f0_rw
    rwv = load_w_fm(C, "L0_routerT", RW, D, 8)
    sb1v = load_bias_fm(C, "L0_sb1", SB1, 4096)
    sb2v = load_bias_fm(C, "L0_sb2", SB2, D)
    eb2v = P.op("gpsimd", lambda g: g.dma_start(out=EB2[0:8, :], in_=C.inp["L0_eb2"][:, :]),
                inc=(C.ch_wd, 16))
    pv = None
    for tt in range(TOK // 128):
        mmv = None
        for k in range(DT):
            kw = dict(inc=(C.ch_t, 1)) if k == DT - 1 else {}
            w0 = list(norm_ready) + [rwv] + ([(C.ch_s, pv)] if pv else [])
            mmv = P.op("tensor", lambda t, k=k, tt=tt: t.matmul(
                C.PB[2][:, 0:8],
                NORM.rearrange("p (j t) -> p j t", j=DT)[:, k, tt * 128:(tt + 1) * 128],
                RW[:, k * 8:(k + 1) * 8], start=(k == 0), stop=(k == DT - 1)),
                waits=(w0 if k == 0 else ()), **kw)
        pv = P.op("scalar", lambda s, tt=tt: s.activation(
            out=PRB[:, tt * 8:(tt + 1) * 8], in_=C.PB[2][:, 0:8], func=AF.Sigmoid),
            waits=[(C.ch_t, mmv)], inc=(C.ch_s, 1))
    g2 = emit_top2(C, PRB, GT, [(ti, 8) for ti in range(16)], renorm_eps=1e-6,
                   probs_ready=[(C.ch_s, pv)])
    gf = emit_gate_transpose(C, GT, GF, 16, 8, g2)
    if C.dbg is not None and C.dbg[0] == "g":
        dv = emit_dbg_dump(C, GF[0:8, :], gf)
        finish_program(C, dv)
        return gf
    def half_mlp(w1name, w1idx, w2name, w2idx, e_or_none, Hh, first_sh):
        wfree = [(C.ch_t, C.ch_t.n), (C.ch_s, C.ch_s.n), (C.ch_v, C.ch_v.n)]
        def ld(name, idx, dst, rows, cols, c0, cw):
            if idx is None:
                src = C.inp[name].rearrange("(kc p) e -> p kc e", p=128)[:, :, c0:c0 + cw]
            else:
                src = C.inp[name][idx].rearrange("(kc p) e -> p kc e", p=128)[:, :, c0:c0 + cw]
            v = P.op("gpsimd", lambda g: g.dma_start(
                out=dst.rearrange("p (kc e) -> p kc e", e=cw)[:, :, :], in_=src),
                waits=wfree, inc=(C.ch_wd, 16))
            return (C.ch_wd, v)
        wv1 = ld(w1name, w1idx, WB1, D, 4096, Hh * 2048, 2048)
        if w2idx is None:
            src2 = C.inp[w2name].rearrange("(kc p) e -> p kc e", p=128)[:, Hh * 16:(Hh + 1) * 16, :]
        else:
            src2 = C.inp[w2name][w2idx].rearrange("(kc p) e -> p kc e", p=128)[:, Hh * 16:(Hh + 1) * 16, :]
        wv2v = P.op("gpsimd", lambda g: g.dma_start(
            out=WB2.rearrange("p (kc e) -> p kc e", e=1024)[:, :, :], in_=src2),
            waits=wfree, inc=(C.ch_wd, 16))
        wv2 = (C.ch_wd, wv2v)
        if e_or_none is not None:
            ev1 = P.op("gpsimd", lambda g, e=e_or_none, Hh=Hh: g.dma_start(
                out=EB1[:, 0:16],
                in_=C.inp["L0_eb1"][e, Hh * 2048:(Hh + 1) * 2048].rearrange("(j p) -> p j", p=128),
                allow_slow_non_contiguous=True), waits=wfree, inc=(C.ch_wd, 16))
            b1w = (C.ch_wd, ev1)
        pend_o = {}
        last_hv = None
        for tc in range(4):
            if e_or_none is not None:
                gbw = list(gf) + ([last_hv] if last_hv else []) + wfree
                gb = P.op("tensor", lambda t, e=e_or_none, tc=tc: t.matmul(
                    C.PB[6][:, 0:512], C.sel8_s[0:8, e * 128:(e + 1) * 128],
                    GF[0:8, tc * 512:(tc + 1) * 512],
                    start=True, stop=True), waits=gbw, inc=(C.ch_t, 1))
            hv = None
            for mH in range(16):
                ps = C.PB[2 + (mH % 2)]
                w0 = norm_ready + [wv1] + ([b1w] if e_or_none is not None else [sb1v])
                if (mH % 2) in pend_o:
                    w0.append(pend_o[mH % 2])
                mmv = None
                for k in range(DT):
                    kw = dict(inc=(C.ch_t, 1)) if k == DT - 1 else {}
                    mmv = P.op("tensor", lambda t, k=k, mH=mH, tc=tc, ps=ps: t.matmul(
                        ps[:, 0:512], WB1[:, k * 2048 + mH * 128: k * 2048 + (mH + 1) * 128],
                        NORM.rearrange("p (j t) -> p j t", j=DT)[:, k, tc * 512:(tc + 1) * 512],
                        start=(k == 0), stop=(k == DT - 1)),
                        waits=(w0 if k == 0 else ()), **kw)
                bias_ap = (EB1[:, mH:mH + 1] if e_or_none is not None
                           else SB1[:, Hh * 16 + mH: Hh * 16 + mH + 1])
                sv = P.op("scalar", lambda s, mH=mH, ps=ps, bias_ap=bias_ap: s.activation(
                    out=HT[:, mH * 512:(mH + 1) * 512], in_=ps[:, 0:512],
                    func=AF.Relu, bias=bias_ap, scale=1.0),
                    waits=[(C.ch_t, mmv)], inc=(C.ch_s, 1))
                if e_or_none is not None:
                    hv = P.op("vector", lambda v, mH=mH: v.tensor_mul(
                        HT[:, mH * 512:(mH + 1) * 512], HT[:, mH * 512:(mH + 1) * 512],
                        C.PB[6][:, 0:512]), waits=[(C.ch_s, sv), (C.ch_t, gb)],
                        inc=(C.ch_v, 1))
                    pend_o[mH % 2] = (C.ch_v, hv)
                    last_hv = (C.ch_v, hv)
                else:
                    pend_o[mH % 2] = (C.ch_s, sv)
            h_ready = pend_o[0], pend_o[1]
            for mO in range(DT):
                ps = C.PB[4 + (mO % 2)]
                w0 = [wv2, h_ready[0], h_ready[1]] + list(x_waits)
                if ("o", mO % 2) in pend_o:
                    w0.append(pend_o[("o", mO % 2)])
                mmv = None
                for k in range(16):
                    kw = dict(inc=(C.ch_t, 1)) if k == 15 else {}
                    mmv = P.op("tensor", lambda t, k=k, mO=mO, ps=ps: t.matmul(
                        ps[:, 0:512], WB2[:, k * 1024 + mO * 128: k * 1024 + (mO + 1) * 128],
                        HT[:, k * 512:(k + 1) * 512],
                        start=(k == 0), stop=(k == 15)),
                        waits=(w0 if k == 0 else ()), **kw)
                if e_or_none is None and first_sh:
                    ev = P.op("vector", lambda v, mO=mO, tc=tc, ps=ps: v.scalar_tensor_tensor(
                        C.X[:, mO * TOK + tc * 512: mO * TOK + (tc + 1) * 512],
                        ps[:, 0:512], SB2[:, mO:mO + 1],
                        C.X[:, mO * TOK + tc * 512: mO * TOK + (tc + 1) * 512],
                        ALU.add, ALU.add), waits=[(C.ch_t, mmv)],
                        inc=(C.ch_v, 1))
                else:
                    ev = P.op("vector", lambda v, mO=mO, tc=tc, ps=ps: v.tensor_add(
                        C.X[:, mO * TOK + tc * 512: mO * TOK + (tc + 1) * 512],
                        C.X[:, mO * TOK + tc * 512: mO * TOK + (tc + 1) * 512],
                        ps[:, 0:512]), waits=[(C.ch_t, mmv)], inc=(C.ch_v, 1))
                pend_o[("o", mO % 2)] = (C.ch_v, ev)
        return (C.ch_v, C.ch_v.n)
    lastv = None
    for Hh in range(2):
        lastv = half_mlp("L0_sw1", None, "L0_sw2", None, None, Hh, Hh == 0)
    if C.dbg is not None and C.dbg[0] == "sh":
        return [(C.ch_v, C.ch_v.n)]
    for e in range(8):
        for Hh in range(2):
            lastv = half_mlp("L0_ew1", e, "L0_ew2", e, e, Hh, False)
    pend = {}
    for mO in range(DT):
        for tc in range(4):
            ps = C.PB[2 + ((mO * 4 + tc) % 2)]
            w0 = gf + [(C.ch_wd, eb2v)] + list(x_waits) + [
                (C.ch_v, C.ch_v.n), (C.ch_s, C.ch_s.n)]
            if ((mO * 4 + tc) % 2) in pend:
                w0.append(pend[(mO * 4 + tc) % 2])
            mmv = P.op("tensor", lambda t, mO=mO, tc=tc, ps=ps: t.matmul(
                ps[:, 0:512], EB2[0:8, mO * 128:(mO + 1) * 128],
                GF[0:8, tc * 512:(tc + 1) * 512], start=True, stop=True),
                waits=w0, inc=(C.ch_t, 1))
            ev = P.op("vector", lambda v, mO=mO, tc=tc, ps=ps: v.tensor_add(
                C.X[:, mO * TOK + tc * 512: mO * TOK + (tc + 1) * 512],
                C.X[:, mO * TOK + tc * 512: mO * TOK + (tc + 1) * 512],
                ps[:, 0:512]), waits=[(C.ch_t, mmv)], inc=(C.ch_v, 1))
            pend[(mO * 4 + tc) % 2] = (C.ch_v, ev)
    return [(C.ch_v, C.ch_v.n)]


def emit_ffn_l1(C, XNo, xn_ready, x_waits):
    P = C.P
    nc = C.nc
    RW1 = C.f1_rw; ZF = C.f1_zf; SQZ = C.f1_sqz; RSD = C.f1_rsd
    CTt = C.f1_ct; CS = C.f1_cs; CNT = C.f1_cnt
    LT = C.f1_lt; PRB = C.f1_prb; GT = C.f1_gt; GF1 = C.f1_gf
    WIE = C.f1_wie; WOE = C.f1_woe; HT1 = C.f1_ht; T32 = C.f1_t32
    ETB = C.f1_etb
    rwv = load_w_fm(C, "L1_routerT", RW1, D, 128, waits=list(xn_ready))
    pend = {}
    zv = None
    for tc in range(4):
        ps = C.PB[tc % 2]
        w0 = list(xn_ready) + [rwv] + ([pend[tc % 2]] if tc % 2 in pend else [])
        mmv = None
        for k in range(DT):
            kw = dict(inc=(C.ch_t, 1)) if k == DT - 1 else {}
            mmv = P.op("tensor", lambda t, k=k, tc=tc, ps=ps: t.matmul(
                ps[:, 0:512], RW1[:, k * 128:(k + 1) * 128],
                XNo.rearrange("p (j t) -> p j t", j=DT)[:, k, tc * 512:(tc + 1) * 512],
                start=(k == 0), stop=(k == DT - 1)),
                waits=(w0 if k == 0 else ()), **kw)
        zv = P.op("scalar", lambda s, tc=tc, ps=ps: s.copy(
            out=ZF[:, tc * 512:(tc + 1) * 512], in_=ps[:, 0:512]),
            waits=[(C.ch_t, mmv)], inc=(C.ch_s, 1))
        pend[tc % 2] = (C.ch_s, zv)
    sqv = P.op("scalar", lambda s: s.activation(out=SQZ[:, :], in_=ZF[:, :],
         func=AF.Square), waits=[(C.ch_s, zv)], inc=(C.ch_s, 1))
    rv = None
    for tc in range(4):
        mmv = P.op("tensor", lambda t, tc=tc: t.matmul(
            C.PB[2][0:1, 0:512], C.onescol_b_s[:, 0:1],
            SQZ[:, tc * 512:(tc + 1) * 512], start=True, stop=True),
            waits=[(C.ch_s, sqv)] + ([(C.ch_v, rv)] if rv else []), inc=(C.ch_t, 1))
        sq2 = P.op("scalar", lambda s, tc=tc: s.activation(
            out=RSD[0:1, tc * 512:(tc + 1) * 512], in_=C.PB[2][0:1, 0:512],
            func=AF.Sqrt), waits=[(C.ch_t, mmv)], inc=(C.ch_s, 1))
        P.op("vector", lambda v, tc=tc: v.tensor_scalar(
            RSD[0:1, tc * 512:(tc + 1) * 512], RSD[0:1, tc * 512:(tc + 1) * 512],
            1e-12, None, ALU.max), waits=[(C.ch_s, sq2)])
        rv = P.op("vector", lambda v, tc=tc: v.reciprocal(
            RSD[0:1, tc * 512:(tc + 1) * 512], RSD[0:1, tc * 512:(tc + 1) * 512]),
            inc=(C.ch_v, 1))
    znv = None
    for tc in range(4):
        mmb = P.op("tensor", lambda t, tc=tc: t.matmul(
            C.PB[3][:, 0:512], C.onesrow_f_s[0:1, :],
            RSD[0:1, tc * 512:(tc + 1) * 512], start=True, stop=True),
            waits=[(C.ch_v, rv)] + ([(C.ch_v, znv)] if znv else []), inc=(C.ch_t, 1))
        znv = P.op("vector", lambda v, tc=tc: v.tensor_mul(
            ZF[:, tc * 512:(tc + 1) * 512], ZF[:, tc * 512:(tc + 1) * 512],
            C.PB[3][:, 0:512]), waits=[(C.ch_t, mmb)], inc=(C.ch_v, 1))
    ctv = P.op("gpsimd", lambda g: g.dma_start(out=CTt[0:16, :], in_=C.inp["L1_cent"][:, :]),
               waits=list(xn_ready), inc=(C.ch_wd, 16))
    s1 = P.op("scalar", lambda s: s.activation(out=CS[0:16, :], in_=CTt[0:16, :],
         func=AF.Square), waits=[(C.ch_wd, ctv)], inc=(C.ch_s, 1))
    r1 = P.op("vector", lambda v: v.tensor_reduce(C.ffn_srow[0:16, 0:1], CS[0:16, :],
         axis=AX.X, op=ALU.add), waits=[(C.ch_s, s1)], inc=(C.ch_v, 1))
    s2 = P.op("scalar", lambda s: s.activation(out=C.ffn_srow[0:16, 1:2],
         in_=C.ffn_srow[0:16, 0:1], func=AF.Sqrt), waits=[(C.ch_v, r1)],
         inc=(C.ch_s, 1))
    P.op("vector", lambda v: v.tensor_scalar(C.ffn_srow[0:16, 1:2],
         C.ffn_srow[0:16, 1:2], 1e-12, None, ALU.max), waits=[(C.ch_s, s2)])
    vv2 = P.op("vector", lambda v: v.reciprocal(C.ffn_srow[0:16, 2:3],
         C.ffn_srow[0:16, 1:2]), inc=(C.ch_v, 1))
    cnv = P.op("vector", lambda v: v.tensor_scalar(CS[0:16, :], CTt[0:16, :],
         C.ffn_srow[0:16, 2:3], None, ALU.mult), inc=(C.ch_v, 1))
    mmt = P.op("tensor", lambda t: t.transpose(C.PT[0:128, 0:16], CS[0:16, 0:128],
         C.identb_s[0:16, 0:16]), waits=[(C.ch_v, cnv)], inc=(C.ch_t, 1))
    cntv = P.op("vector", lambda v: v.tensor_copy(CNT[:, 0:16], C.PT[0:128, 0:16]),
         waits=[(C.ch_t, mmt)], inc=(C.ch_v, 1))
    tv = P.op("gpsimd", lambda g: g.dma_start(out=C.ffn_srow[0:1, 3:4],
         in_=C.inp["L1_temp"][0:1, 0:1]), waits=list(xn_ready), inc=(C.ch_wd, 16))
    bcv = P.op("gpsimd", lambda g: g.memset(C.ffn_srow[0:1, 5:6], -2.4258509299404572),
               waits=[(C.ch_wd, tv)], inc=(C.ch_g, 1))
    e1 = P.op("scalar", lambda s: s.activation(out=C.ffn_srow[0:1, 4:5],
         in_=C.ffn_srow[0:1, 3:4], func=AF.Exp, bias=C.ffn_srow[0:1, 5:6],
         scale=1.0), waits=[(C.ch_g, bcv), (C.ch_v, r1)], inc=(C.ch_s, 1))
    mmE = P.op("tensor", lambda t: t.matmul(C.PB[2][:, 0:1], C.onesrow_f_s[0:1, :],
         C.ffn_srow[0:1, 4:5], start=True, stop=True), waits=[(C.ch_s, e1)],
         inc=(C.ch_t, 1))
    etv = P.op("vector", lambda v: v.tensor_copy(ETB[:, 0:1], C.PB[2][:, 0:1]),
         waits=[(C.ch_t, mmE)], inc=(C.ch_v, 1))
    pv = None
    for tt in range(TOK // 128):
        w0 = [(C.ch_v, znv), (C.ch_v, cntv), (C.ch_v, etv)] + ([(C.ch_s, pv)] if pv else [])
        mmv = P.op("tensor", lambda t, tt=tt: t.matmul(
            C.PB[2][:, 0:16], ZF[:, tt * 128:(tt + 1) * 128], CNT[:, 0:16],
            start=True, stop=True), waits=w0, inc=(C.ch_t, 1))
        pv = P.op("scalar", lambda s, tt=tt: s.activation(
            out=LT[:, tt * 16:(tt + 1) * 16], in_=C.PB[2][:, 0:16],
            func=AF.Identity, scale=ETB[:, 0:1]), waits=[(C.ch_t, mmv)],
            inc=(C.ch_s, 1))
    smv = None
    for tt in range(TOK // 128):
        sl = slice(tt * 16, (tt + 1) * 16)
        P.op("vector", lambda v, sl=sl: v.tensor_reduce(
            C.ffn_srow[:, 0:1], LT[:, sl], axis=AX.X, op=ALU.max),
            waits=[(C.ch_s, pv)])
        P.op("vector", lambda v, sl=sl: v.tensor_scalar(
            LT[:, sl], LT[:, sl], C.ffn_srow[:, 0:1], None, ALU.subtract))
        sev = P.op("scalar", lambda s, sl=sl: s.activation(out=LT[:, sl],
             in_=LT[:, sl], func=AF.Exp), waits=[(C.ch_v, C.ch_v.n)], inc=(C.ch_s, 1))
        P.op("vector", lambda v, sl=sl: v.tensor_reduce(
            C.ffn_srow[:, 0:1], LT[:, sl], axis=AX.X, op=ALU.add),
            waits=[(C.ch_s, sev)])
        P.op("vector", lambda v: v.reciprocal(C.ffn_srow[:, 0:1], C.ffn_srow[:, 0:1]))
        smv = P.op("vector", lambda v, sl=sl: v.tensor_scalar(
            PRB[:, sl], LT[:, sl], C.ffn_srow[:, 0:1], None, ALU.mult),
            inc=(C.ch_v, 1))
    g2 = emit_top2(C, PRB, GT, [(ti, 16) for ti in range(16)], renorm_eps=None,
                   probs_ready=[(C.ch_v, smv)])
    gf = emit_gate_transpose(C, GT, GF1, 16, 16, g2)
    for n in range(16):
        wfree = [(C.ch_t, C.ch_t.n), (C.ch_s, C.ch_s.n), (C.ch_v, C.ch_v.n)]
        wv1v = P.op("gpsimd", lambda g, n=n: g.dma_start(
            out=WIE.rearrange("p (kc e) -> p kc e", e=512)[:, :, :],
            in_=C.inp["L1_ie"][n].rearrange("(kc p) e -> p kc e", p=128)),
            waits=wfree, inc=(C.ch_wd, 16))
        wv2v = P.op("gpsimd", lambda g, n=n: g.dma_start(
            out=WOE.rearrange("p (kc e) -> p kc e", e=64)[:, :, :],
            in_=C.inp["L1_oe"][n].rearrange("(kc p) e -> p kc e", p=128)),
            waits=wfree, inc=(C.ch_wd, 16))
        wv1 = (C.ch_wd, wv1v); wv2 = (C.ch_wd, wv2v)
        pend_h = {}
        for tc in range(4):
            gbw = list(gf) + wfree + [(C.ch_v, C.ch_v.n)]
            gb = P.op("tensor", lambda t, n=n, tc=tc: t.matmul(
                C.PB[6][0:64, 0:512], C.sel16_s[0:16, n * 128:n * 128 + 64],
                GF1[0:16, tc * 512:(tc + 1) * 512], start=True, stop=True),
                waits=gbw, inc=(C.ch_t, 1))
            for mH in range(4):
                ps = C.PB[2 + (mH % 2)]
                w0 = [(C.ch_v, znv), wv1] + ([pend_h[mH % 2]] if mH % 2 in pend_h else [])
                mmv = None
                for k in range(DT):
                    kw = dict(inc=(C.ch_t, 1)) if k == DT - 1 else {}
                    mmv = P.op("tensor", lambda t, k=k, mH=mH, tc=tc, ps=ps: t.matmul(
                        ps[:, 0:512], WIE[:, k * 512 + mH * 128: k * 512 + (mH + 1) * 128],
                        XNo.rearrange("p (j t) -> p j t", j=DT)[:, k, tc * 512:(tc + 1) * 512],
                        start=(k == 0), stop=(k == DT - 1)),
                        waits=(w0 if k == 0 else ()), **kw)
                sv = P.op("scalar", lambda s, mH=mH, ps=ps: s.activation(
                    out=HT1[:, mH * 512:(mH + 1) * 512], in_=ps[:, 0:512],
                    func=AF.Gelu), waits=[(C.ch_t, mmv)], inc=(C.ch_s, 1))
                pend_h[mH % 2] = (C.ch_s, sv)
            w0 = [wv2, pend_h[0], pend_h[1]] + ([pend_h.get("o")] if pend_h.get("o") else [])
            mmv = None
            for k in range(4):
                kw = dict(inc=(C.ch_t, 1)) if k == 3 else {}
                mmv = P.op("tensor", lambda t, k=k, ps=None: t.matmul(
                    C.PB[4][0:64, 0:512], WOE[:, k * 64:(k + 1) * 64],
                    HT1[:, k * 512:(k + 1) * 512],
                    start=(k == 0), stop=(k == 3)),
                    waits=(w0 if k == 0 else ()), **kw)
            base = (n % 2) * 64
            j = n // 2
            sv2 = P.op("scalar", lambda s, base=base: s.copy(out=T32[base:base + 64, :],
                 in_=C.PB[4][0:64, 0:512]),
                 waits=[(C.ch_t, mmv), (C.ch_v, C.ch_v.n)], inc=(C.ch_s, 1))
            vv3 = P.op("vector", lambda v, base=base: v.tensor_mul(
                 T32[base:base + 64, :], T32[base:base + 64, :],
                 C.PB[6][0:64, 0:512]), waits=[(C.ch_s, sv2), (C.ch_t, gb)],
                 inc=(C.ch_v, 1))
            ev = P.op("vector", lambda v, base=base, j=j, tc=tc: v.tensor_add(
                 C.X[base:base + 64, j * TOK + tc * 512: j * TOK + (tc + 1) * 512],
                 C.X[base:base + 64, j * TOK + tc * 512: j * TOK + (tc + 1) * 512],
                 T32[base:base + 64, :]), waits=list(x_waits), inc=(C.ch_v, 1))
            pend_h["o"] = (C.ch_v, ev)
    return [(C.ch_v, C.ch_v.n)]


def emit_ffn_l2(C, XNo, xn_ready, x_waits):
    P = C.P
    GW = C.f2_gw; LT = C.f2_lt; PRB = C.f2_prb; GT = C.f2_gt; GF2 = C.f2_gf
    WBg = C.f2_wbg; WBu = C.f2_wbu; WBd = C.f2_wbd
    HTI = C.f2_hti; SGF = C.f2_sgf; SGW = C.f2_sgw
    T32 = C.f2_t32
    gwv = load_w_fm(C, "L2_gateT", GW, D, 8, waits=list(xn_ready))
    sgv = load_w_fm(C, "L2_sgT", SGW, D, 1, waits=list(xn_ready), slow=True)
    pv = None
    for tt in range(TOK // 128):
        w0 = list(xn_ready) + [gwv] + ([(C.ch_s, pv)] if pv else [])
        mmv = None
        for k in range(DT):
            kw = dict(inc=(C.ch_t, 1)) if k == DT - 1 else {}
            mmv = P.op("tensor", lambda t, k=k, tt=tt: t.matmul(
                C.PB[2][:, 0:8],
                XNo.rearrange("p (j t) -> p j t", j=DT)[:, k, tt * 128:(tt + 1) * 128],
                GW[:, k * 8:(k + 1) * 8], start=(k == 0), stop=(k == DT - 1)),
                waits=(w0 if k == 0 else ()), **kw)
        pv = P.op("scalar", lambda s, tt=tt: s.copy(
            out=LT[:, tt * 8:(tt + 1) * 8], in_=C.PB[2][:, 0:8]),
            waits=[(C.ch_t, mmv)], inc=(C.ch_s, 1))
    smv = None
    for tt in range(TOK // 128):
        sl = slice(tt * 8, (tt + 1) * 8)
        P.op("vector", lambda v, sl=sl: v.tensor_reduce(
            C.ffn_srow[:, 0:1], LT[:, sl], axis=AX.X, op=ALU.max),
            waits=[(C.ch_s, pv)])
        P.op("vector", lambda v, sl=sl: v.tensor_scalar(
            LT[:, sl], LT[:, sl], C.ffn_srow[:, 0:1], None, ALU.subtract))
        sev = P.op("scalar", lambda s, sl=sl: s.activation(out=LT[:, sl],
             in_=LT[:, sl], func=AF.Exp), waits=[(C.ch_v, C.ch_v.n)], inc=(C.ch_s, 1))
        P.op("vector", lambda v, sl=sl: v.tensor_reduce(
            C.ffn_srow[:, 0:1], LT[:, sl], axis=AX.X, op=ALU.add),
            waits=[(C.ch_s, sev)])
        P.op("vector", lambda v: v.reciprocal(C.ffn_srow[:, 0:1], C.ffn_srow[:, 0:1]))
        smv = P.op("vector", lambda v, sl=sl: v.tensor_scalar(
            PRB[:, sl], LT[:, sl], C.ffn_srow[:, 0:1], None, ALU.mult),
            inc=(C.ch_v, 1))
    g2 = emit_top2(C, PRB, GT, [(ti, 8) for ti in range(16)], renorm_eps=0.0,
                   probs_ready=[(C.ch_v, smv)])
    gf = emit_gate_transpose(C, GT, GF2, 16, 8, g2)
    pend_sg = None
    for tc in range(4):
        w0 = list(xn_ready) + [sgv] + ([pend_sg] if pend_sg else [])
        mmv = None
        for k in range(DT):
            kw = dict(inc=(C.ch_t, 1)) if k == DT - 1 else {}
            mmv = P.op("tensor", lambda t, k=k, tc=tc: t.matmul(
                C.PB[3][0:1, 0:512], SGW[:, k:k + 1],
                XNo.rearrange("p (j t) -> p j t", j=DT)[:, k, tc * 512:(tc + 1) * 512],
                start=(k == 0), stop=(k == DT - 1)),
                waits=(w0 if k == 0 else ()), **kw)
        svv = P.op("scalar", lambda s, tc=tc: s.activation(
            out=SGF[0:1, tc * 512:(tc + 1) * 512], in_=C.PB[3][0:1, 0:512],
            func=AF.Sigmoid), waits=[(C.ch_t, mmv)], inc=(C.ch_s, 1))
        pend_sg = (C.ch_s, svv)
    sg_done = pend_sg
    def expert(widx):
        wfree = [(C.ch_t, C.ch_t.n), (C.ch_s, C.ch_s.n), (C.ch_v, C.ch_v.n)]
        def ld(name, dst):
            if widx is None:
                src = C.inp[name].rearrange("(kc p) e -> p kc e", p=128)
            else:
                src = C.inp[name][widx].rearrange("(kc p) e -> p kc e", p=128)
            v = P.op("gpsimd", lambda g: g.dma_start(
                out=dst.rearrange("p (kc e) -> p kc e", e=src.shape[2])[:, :, :],
                in_=src), waits=wfree, inc=(C.ch_wd, 16))
            return (C.ch_wd, v)
        if widx is None:
            wg = ld("L2_swg", WBg); wu = ld("L2_swu", WBu); wd = ld("L2_swd", WBd)
        else:
            wg = ld("L2_wg", WBg); wu = ld("L2_wu", WBu); wd = ld("L2_wd", WBd)
        pend = {}
        for tc in range(4):
            if widx is not None:
                gbw = list(gf) + wfree + [(C.ch_v, C.ch_v.n)]
                gb = P.op("tensor", lambda t, e=widx, tc=tc: t.matmul(
                    C.PB[6][:, 0:512], C.sel8_s[0:8, e * 128:(e + 1) * 128],
                    GF2[0:8, tc * 512:(tc + 1) * 512], start=True, stop=True),
                    waits=gbw, inc=(C.ch_t, 1))
            for mI in range(11):
                psg = C.PB[2]
                psu = C.PB[3]
                w0 = list(xn_ready) + [wg, wu] + ([pend.get("gu")] if pend.get("gu") else [])
                mmg = None
                for k in range(DT):
                    kw = dict(inc=(C.ch_t, 1)) if k == DT - 1 else {}
                    mmg = P.op("tensor", lambda t, k=k, mI=mI, tc=tc: t.matmul(
                        psg[:, 0:512], WBg[:, k * 1408 + mI * 128: k * 1408 + (mI + 1) * 128],
                        XNo.rearrange("p (j t) -> p j t", j=DT)[:, k, tc * 512:(tc + 1) * 512],
                        start=(k == 0), stop=(k == DT - 1)),
                        waits=(w0 if k == 0 else ()), **kw)
                mmu = None
                for k in range(DT):
                    kw = dict(inc=(C.ch_t, 1)) if k == DT - 1 else {}
                    mmu = P.op("tensor", lambda t, k=k, mI=mI, tc=tc: t.matmul(
                        psu[:, 0:512], WBu[:, k * 1408 + mI * 128: k * 1408 + (mI + 1) * 128],
                        XNo.rearrange("p (j t) -> p j t", j=DT)[:, k, tc * 512:(tc + 1) * 512],
                        start=(k == 0), stop=(k == DT - 1)),
                        waits=(w0 if k == 0 else ()), **kw)
                sv = P.op("scalar", lambda s, mI=mI: s.activation(
                    out=T32[:, 0:512], in_=psg[:, 0:512],
                    func=AF.Silu), waits=[(C.ch_t, mmu), (C.ch_v, C.ch_v.n)],
                    inc=(C.ch_s, 1))
                if widx is not None:
                    P.op("vector", lambda v, mI=mI: v.tensor_mul(
                        T32[:, 0:512], T32[:, 0:512],
                        C.PB[6][:, 0:512]), waits=[(C.ch_s, sv), (C.ch_t, gb)])
                hv = P.op("vector", lambda v, mI=mI: v.tensor_mul(
                    HTI[:, mI * 512:(mI + 1) * 512], T32[:, 0:512],
                    psu[:, 0:512]), waits=[(C.ch_s, sv)], inc=(C.ch_v, 1))
                pend["gu"] = (C.ch_v, hv)
            for mO in range(DT):
                ps = C.PB[4 + (mO % 2)]
                w0 = [wd, pend["gu"]] + list(x_waits) + ([pend.get(("o", mO % 2))] if pend.get(("o", mO % 2)) else [])
                mmv = None
                for k in range(11):
                    kw = dict(inc=(C.ch_t, 1)) if k == 10 else {}
                    mmv = P.op("tensor", lambda t, k=k, mO=mO, ps=ps: t.matmul(
                        ps[:, 0:512], WBd[:, k * 1024 + mO * 128: k * 1024 + (mO + 1) * 128],
                        HTI[:, k * 512:(k + 1) * 512],
                        start=(k == 0), stop=(k == 10)),
                        waits=(w0 if k == 0 else ()), **kw)
                if widx is not None:
                    ev = P.op("vector", lambda v, mO=mO, tc=tc, ps=ps: v.tensor_add(
                        C.X[:, mO * TOK + tc * 512: mO * TOK + (tc + 1) * 512],
                        C.X[:, mO * TOK + tc * 512: mO * TOK + (tc + 1) * 512],
                        ps[:, 0:512]), waits=[(C.ch_t, mmv)], inc=(C.ch_v, 1))
                else:
                    mmb = P.op("tensor", lambda t, tc=tc: t.matmul(
                        C.PB[6][:, 0:512], C.onesrow_b_s[0:1, :],
                        SGF[0:1, tc * 512:(tc + 1) * 512], start=True, stop=True),
                        waits=[sg_done, (C.ch_v, C.ch_v.n)], inc=(C.ch_t, 1))
                    s32 = P.op("scalar", lambda s, ps=ps: s.copy(
                        out=T32[:, :], in_=ps[:, 0:512]),
                        waits=[(C.ch_t, mmv), (C.ch_v, C.ch_v.n)], inc=(C.ch_s, 1))
                    P.op("vector", lambda v: v.tensor_mul(T32[:, :], T32[:, :],
                        C.PB[6][:, 0:512]), waits=[(C.ch_s, s32), (C.ch_t, mmb)])
                    ev = P.op("vector", lambda v, mO=mO, tc=tc: v.tensor_add(
                        C.X[:, mO * TOK + tc * 512: mO * TOK + (tc + 1) * 512],
                        C.X[:, mO * TOK + tc * 512: mO * TOK + (tc + 1) * 512],
                        T32[:, :]), waits=list(x_waits), inc=(C.ch_v, 1))
                pend[("o", mO % 2)] = (C.ch_v, ev)
    for e in range(8):
        expert(e)
    expert(None)
    return [(C.ch_v, C.ch_v.n)]


def emit_rope(C, T, ready):
    P = C.P
    QS = C.att_qs
    vsw = C.ch_sd.bump(64)
    def emit_swap(sy):
        for a, b2 in ((0, 32), (32, 0), (64, 96), (96, 64)):
            sy.dma_start(out=QS[a:a + 32, :], in_=T[b2:b2 + 32, :]).then_inc(C.ch_sd.h, 16)
        return sy.nop()
    P.op("sync", emit_swap, waits=list(ready) + [(C.ch_v, C.ch_v.n)])
    v1 = P.op("vector", lambda v: v.tensor_mul(T[:, :], T[:, :], C.att_rc[:, :]),
              waits=list(ready) + [(C.ch_sd, vsw)], inc=(C.ch_v, 1))
    v2 = P.op("vector", lambda v: v.tensor_mul(QS[:, :], QS[:, :], C.att_rs2[:, :]),
              inc=(C.ch_v, 1))
    v3 = P.op("vector", lambda v: v.tensor_add(T[:, :], T[:, :], QS[:, :]),
              inc=(C.ch_v, 1))
    return [(C.ch_v, v3)]


_MIDS = [0.25, 0.75, 1.25, 1.75, 2.5, 3.5, 5.0]
_DELTAS = [0.5, 0.5, 0.5, 0.5, 1.0, 1.0, 2.0]

def emit_fp4_quant(C, T, QD, W, waits):
    P = C.P
    nb = W // 16
    BM, RS = C.q_bm, C.q_rs
    A, Q, SG = C.q_a, C.q_q, C.q_sg
    A3 = A[:, 0:W].rearrange("p (b s) -> p b s", s=16)
    T3 = T.rearrange("p (b s) -> p b s", s=16)
    P.op("vector", lambda v: v.tensor_reduce(
        BM[:, 0:nb], T3, axis=AX.X,
        op=ALU.max, apply_absolute_value=True), waits=list(waits))
    P.op("vector", lambda v: v.tensor_scalar(BM[:, 0:nb], BM[:, 0:nb],
         1.0 / 6.0, 1e-12, ALU.mult, ALU.max))
    P.op("vector", lambda v: v.reciprocal(RS[:, 0:nb], BM[:, 0:nb]))
    P.op("vector", lambda v: v.tensor_tensor(
        A3, T3, bc0(RS[:, 0:nb], 16), ALU.mult))
    P.op("vector", lambda v: v.tensor_scalar(SG[:, 0:W], A[:, 0:W], 0.0, None, ALU.is_ge))
    P.op("vector", lambda v: v.tensor_scalar(SG[:, 0:W], SG[:, 0:W], 2.0, -1.0,
         ALU.mult, ALU.add))
    sab = P.op("scalar", lambda s: s.activation(out=A[:, 0:W], in_=A[:, 0:W],
         func=AF.Abs), waits=[(C.ch_v, C.ch_v.n)], inc=(C.ch_s, 1))
    P.op("vector", lambda v: v.tensor_scalar(Q[:, 0:W], A[:, 0:W], _MIDS[0],
         _DELTAS[0], ALU.is_gt, ALU.mult), waits=[(C.ch_s, sab)])
    for j in range(1, 7):
        P.op("vector", lambda v, j=j: v.tensor_scalar(SGQ_tmp(C)[:, 0:W], A[:, 0:W],
             _MIDS[j], _DELTAS[j], ALU.is_gt, ALU.mult))
        P.op("vector", lambda v: v.tensor_add(Q[:, 0:W], Q[:, 0:W], SGQ_tmp(C)[:, 0:W]))
    P.op("vector", lambda v: v.tensor_mul(Q[:, 0:W], Q[:, 0:W], SG[:, 0:W]))
    QD3 = QD.rearrange("p (b s) -> p b s", s=16)
    Q3 = Q[:, 0:W].rearrange("p (b s) -> p b s", s=16)
    vv = P.op("vector", lambda v: v.tensor_tensor(
        QD3, Q3, bc0(BM[:, 0:nb], 16), ALU.mult), inc=(C.ch_v, 1))
    return (C.ch_v, vv)

def SGQ_tmp(C):
    return C.q_t

def emit_quant_weightT(C, name, WQT, nrows, ncols, waits):
    P = C.P
    nc_ = C.nc
    STG = C.q_stg
    QD = C.q_qd
    last = None
    pend_t = None
    for pt in range(nrows // 128):
        w0 = list(waits) + ([last] if last else [])
        lv = P.op("gpsimd", lambda g, pt=pt: g.dma_start(
            out=STG[:, 0:ncols], in_=C.inp[name][pt * 128:(pt + 1) * 128, :]),
            waits=w0 + [(C.ch_v, C.ch_v.n)], inc=(C.ch_wd, 16))
        qv = emit_fp4_quant(C, STG[:, 0:ncols], QD[:, 0:ncols], ncols,
                            [(C.ch_wd, lv)])
        for kc in range(ncols // 128):
            mmv = P.op("tensor", lambda t, kc=kc: t.transpose(
                C.PT[:, 0:128], QD[:, kc * 128:(kc + 1) * 128], C.identb_s[:, :]),
                waits=[qv] + ([pend_t] if pend_t else []), inc=(C.ch_t, 1))
            ev = P.op("vector", lambda v, kc=kc, pt=pt: v.tensor_copy(
                WQT[:, kc * nrows + pt * 128: kc * nrows + (pt + 1) * 128],
                C.PT[:, 0:128]), waits=[(C.ch_t, mmv)], inc=(C.ch_v, 1))
            pend_t = (C.ch_v, ev)
        last = pend_t
    return last


def emit_ffn_l3_phase1(C, li, WGQ, WUQ, x_waits):
    P = C.P
    nc = C.nc
    qg = emit_quant_weightT(C, "L3_wg", WGQ, 2816, 1024, list(x_waits))
    qu = emit_quant_weightT(C, "L3_wu", WUQ, 2816, 1024, [qg])
    XT = C.q_stg
    XQT = C.q_qd
    XQ = C.f3_xq
    HT = C.f3_ht
    HQ = C.f3_hq
    hq_done = []
    for tc in range(4):
        for tt in range(4):
            vld = C.ch_sd.bump(16)
            def emit_ld(sy, tc=tc, tt=tt):
                h = P.holder
                for myp in (0, 1):
                    guard = sy.If_eq(h["par"], 0) if myp == 0 else sy.Else()
                    with guard:
                        sy.dma_start(
                            out=C.f3_xn[:, :, :],
                            in_=C.xn_sh[myp, li].rearrange("p (j t) -> p j t", j=DT)[
                                :, :, tc * 512 + tt * 128: tc * 512 + (tt + 1) * 128],
                        ).then_inc(C.ch_sd.h, 16)
                sy.end_ifs()
                return sy.nop()
            P.op("sync", emit_ld, waits=[(C.ch_t, C.ch_t.n), (C.ch_v, C.ch_v.n)])
            pend = None
            for j in range(DT):
                mmv = P.op("tensor", lambda t, j=j: t.transpose(
                    C.PT[:, 0:128], C.f3_xn[:, j, :], C.identb_s[:, :]),
                    waits=[(C.ch_sd, vld)] + ([pend] if pend else []), inc=(C.ch_t, 1))
                ev = P.op("vector", lambda v, j=j: v.tensor_copy(
                    XT[:, j * 128:(j + 1) * 128], C.PT[:, 0:128]),
                    waits=[(C.ch_t, mmv)], inc=(C.ch_v, 1))
                pend = (C.ch_v, ev)
            qv = emit_fp4_quant(C, XT[:, 0:1024], XQT[:, 0:1024], 1024, [pend])
            pend = None
            for j in range(DT):
                mmv = P.op("tensor", lambda t, j=j: t.transpose(
                    C.PT[:, 0:128], XQT[:, j * 128:(j + 1) * 128], C.identb_s[:, :]),
                    waits=[qv] + ([pend] if pend else []), inc=(C.ch_t, 1))
                ev = P.op("vector", lambda v, j=j, tt=tt: v.tensor_copy(
                    XQ[:, j * 512 + tt * 128: j * 512 + (tt + 1) * 128],
                    C.PT[:, 0:128]), waits=[(C.ch_t, mmv)], inc=(C.ch_v, 1))
                pend = (C.ch_v, ev)
        xq_ready = [(C.ch_v, C.ch_v.n), qu]
        for tt in range(4):
            for iI in range(6):
                iw = 512 if iI < 5 else 256
                mmg = None
                for k in range(DT):
                    kw = dict(inc=(C.ch_t, 1)) if k == DT - 1 else {}
                    mmg = P.op("tensor", lambda t, k=k, tt=tt, iI=iI, iw=iw, tc=tc: t.matmul(
                        C.PB[2][:, 0:iw], XQ[:, k * 512 + tt * 128: k * 512 + (tt + 1) * 128],
                        WGQ[:, k * 2816 + iI * 512: k * 2816 + iI * 512 + iw],
                        start=(k == 0), stop=(k == DT - 1)),
                        waits=(xq_ready if k == 0 else ()), **kw)
                mmu = None
                u_w0 = [(C.ch_v, C.ch_v.n)]
                for k in range(DT):
                    kw = dict(inc=(C.ch_t, 1)) if k == DT - 1 else {}
                    mmu = P.op("tensor", lambda t, k=k, tt=tt, iI=iI, iw=iw, tc=tc: t.matmul(
                        C.PB[3][:, 0:iw], XQ[:, k * 512 + tt * 128: k * 512 + (tt + 1) * 128],
                        WUQ[:, k * 2816 + iI * 512: k * 2816 + iI * 512 + iw],
                        start=(k == 0), stop=(k == DT - 1)),
                        waits=(u_w0 if k == 0 else ()), **kw)
                sv = P.op("scalar", lambda s, iw=iw: s.activation(
                    out=HT[:, 0:iw], in_=C.PB[2][:, 0:iw], func=AF.Silu),
                    waits=[(C.ch_t, mmu), (C.ch_v, C.ch_v.n)], inc=(C.ch_s, 1))
                hv = P.op("vector", lambda v, iw=iw: v.tensor_mul(
                    HT[:, 0:iw], HT[:, 0:iw], C.PB[3][:, 0:iw]),
                    waits=[(C.ch_s, sv)], inc=(C.ch_v, 1))
                qv2 = emit_fp4_quant(C, HT[:, 0:iw], HQ[:, 0:iw], iw, [(C.ch_v, hv)])
                pend_pt = None
                for jb in range(iw // 128):
                    mmv = P.op("tensor", lambda t, jb=jb: t.transpose(
                        C.PT[:, 0:128], HQ[:, jb * 128:(jb + 1) * 128], C.identb_s[:, :]),
                        waits=[qv2] + ([pend_pt] if pend_pt else []), inc=(C.ch_t, 1))
                    ib = iI * 4 + jb
                    ev = P.op("vector", lambda v: v.tensor_copy(
                        C.f3_hf[:, 0:128], C.PT[:, 0:128]),
                        waits=[(C.ch_t, mmv)], inc=(C.ch_v, 1))
                    pend_pt = (C.ch_v, ev)
                    dv = P.op("sync", lambda sy, ib=ib, tc=tc, tt=tt: sy.dma_start(
                        out=C.scr_h[:, ib * 2048 + tc * 512 + tt * 128:
                                    ib * 2048 + tc * 512 + (tt + 1) * 128],
                        in_=C.f3_hf[:, 0:128]), waits=[(C.ch_v, ev)],
                        inc=(C.ch_sd, 16))
                    P.op("vector", lambda v: v.engine_nop(), waits=[(C.ch_sd, dv)])
        hq_done.append((C.ch_sd, C.ch_sd.n))
    return hq_done

def emit_ffn_l3_phase2(C, li, WDQ, hq_done, x_waits):
    P = C.P
    qd = emit_quant_weightT(C, "L3_wd", WDQ, 1024, 2816, list(hq_done))
    HQF = C.f3_hqf
    pend = {}
    for tc in range(4):
        vld = P.op("gpsimd", lambda g, tc=tc: g.dma_start(
            out=HQF.rearrange("p (kc e) -> p kc e", e=512)[:, :, :],
            in_=C.scr_h.rearrange("p (kc t) -> p kc t", t=2048)[:, :, tc * 512:(tc + 1) * 512]),
            waits=list(hq_done) + [(C.ch_t, C.ch_t.n)], inc=(C.ch_wd, 16))
        for mO in range(DT):
            ps = C.PB[4 + (mO % 2)]
            w0 = [qd, (C.ch_wd, vld)] + list(x_waits) + ([pend.get(mO % 2)] if pend.get(mO % 2) else [])
            mmv = None
            for k in range(22):
                kw = dict(inc=(C.ch_t, 1)) if k == 21 else {}
                mmv = P.op("tensor", lambda t, k=k, mO=mO, ps=ps: t.matmul(
                    ps[:, 0:512], WDQ[:, k * 1024 + mO * 128: k * 1024 + (mO + 1) * 128],
                    HQF[:, k * 512:(k + 1) * 512],
                    start=(k == 0), stop=(k == 21)),
                    waits=(w0 if k == 0 else ()), **kw)
            ev = P.op("vector", lambda v, mO=mO, tc=tc, ps=ps: v.tensor_add(
                C.X[:, mO * TOK + tc * 512: mO * TOK + (tc + 1) * 512],
                C.X[:, mO * TOK + tc * 512: mO * TOK + (tc + 1) * 512],
                ps[:, 0:512]), waits=[(C.ch_t, mmv)], inc=(C.ch_v, 1))
            pend[mO % 2] = (C.ch_v, ev)
    return [(C.ch_v, C.ch_v.n)]



_CACHE = {}

def kernel(**inputs):
    if "C" not in _CACHE:
        _CACHE["C"] = build_full([0, 1, 2, 3], stage="full")
    C = _CACHE["C"]
    ep = int(np.random.randint(1, 1 << 30))
    in_maps = [host_prep(inputs, c, ep) for c in range(8)]
    res = run_cores(C, in_maps)
    out = np.zeros((4, TFULL, D), np.float32)
    for c in range(8):
        b, p = c // 2, c % 2
        out[b, p * TOK:(p + 1) * TOK, :] = np.asarray(res.results[c]["x_out"],
                                                      np.float32).T
    return out


# version 07
# speedup vs baseline: 1.0148x; 1.0148x over previous
import numpy as np
import ml_dtypes
from contextlib import ExitStack
import concourse.bass as bass
import concourse.mybir as mybir
from concourse.bass_utils import run_bass_kernel_spmd

F32 = mybir.dt.float32
BF16 = mybir.dt.bfloat16
I32 = mybir.dt.int32
AF = mybir.ActivationFunctionType
ALU = mybir.AluOpType
AX = mybir.AxisListType
ENG = mybir.EngineType
NPBF16 = ml_dtypes.bfloat16

USE_POLL = True

D = 1024
TOK = 2048
TFULL = 4096
DT = 8
NHP = 4
DH = 64


class Chan:
    __slots__ = ("h", "n", "name")
    def __init__(self, prog, name):
        self.h = prog.ctx.enter_context(prog.nc.semaphore(name))
        self.n = 0
        self.name = name
    def bump(self, k):
        self.n += k
        return self.n

class Prog:
    ENGINES = ("tensor", "scalar", "vector", "gpsimd", "sync")
    def __init__(self, nc, ctx):
        self.nc, self.ctx = nc, ctx
        self.q = {e: [] for e in self.ENGINES}
        self.holder = {}
        self.chain = {}
    def chan(self, name):
        return Chan(self, name)
    def op(self, eng, emit, waits=(), inc=None):
        waits = list(waits)
        if eng in self.chain:
            ch = self.chain[eng]
            assert inc is None or inc[0] is ch, f"{eng} op must inc its chain chan"
            if ch.n > 0:
                waits.append((ch, ch.n))
            tgt = ch.bump(1)
            self.q[eng].append((tuple(waits), emit, (ch, 1)))
            return tgt
        tgt = None
        if inc is not None:
            tgt = inc[0].bump(inc[1])
        self.q[eng].append((tuple(waits), emit, inc))
        return tgt
    def emit_all(self):
        self.ctx.enter_context(self.nc.allow_low_precision(
            reason="bf16 matmul operands + fp32 accumulation per problem tolerance"))
        block = self.ctx.enter_context(self.nc.Block())
        def mk(name):
            ops = self.q[name]
            def body(eng):
                for waits, emit, inc in ops:
                    for ch, v in waits:
                        eng.wait_ge(ch.h, v)
                    inst = emit(eng)
                    if inc is not None:
                        assert inst is not None, f"op needs inst for inc on {inc[0].name}"
                        inst.then_inc(inc[0].h, inc[1])
            return body
        block.tensor(mk("tensor"))
        block.scalar(mk("scalar"))
        block.vector(mk("vector"))
        block.gpsimd(mk("gpsimd"))
        block.sync(mk("sync"))


def ceil_div(a, b):
    return (a + b - 1) // b

class Ctx:
    pass

def linear_fm(C, *, nK, mlist, tlist, TQ, lhsT, rhs, banks, evac,
              rhs_waits=lambda tc: (), w_waits=lambda m: (), pend0=None):
    P = C.P
    pend = list(pend0) if pend0 else [None] * len(banks)
    i = 0
    for m in mlist:
        for tc in tlist:
            bi = i % len(banks)
            ps = banks[bi]
            w0 = list(rhs_waits(tc)) + list(w_waits(m))
            if pend[bi] is not None:
                w0.append(pend[bi])
            for k in range(nK):
                def mm(t, ps=ps, k=k, m=m, tc=tc, first=(k == 0), last=(k == nK - 1)):
                    return t.matmul(ps[:, 0:TQ], lhsT(k, m), rhs(k, tc),
                                    start=first, stop=last)
                if k == nK - 1:
                    mmv = P.op("tensor", mm, waits=(w0 if k == 0 else ()),
                               inc=(C.ch_t, 1))
                else:
                    P.op("tensor", mm, waits=(w0 if k == 0 else ()))
            pend[bi] = evac(m, tc, ps, [(C.ch_t, mmv)])
            i += 1
    return pend


def host_prep(inp, core, epoch=None):
    b, p = core // 2, core % 2
    g = lambda k: np.asarray(inp[k], np.float32)
    o = {}
    x = g("L0_attn_x")[b]
    o["x_fm"] = np.ascontiguousarray(x[p * TOK:(p + 1) * TOK].T)
    hs = slice(p * 512, (p + 1) * 512)

    for nm in ("wq", "wk", "wv"):
        o[f"L0_{nm}T"] = np.ascontiguousarray(g(f"L0_attn_{nm}_w").T[:, hs])
        o[f"L0_{nm}b"] = np.ascontiguousarray(g(f"L0_attn_{nm}_b")[hs])
    woT = g("L0_attn_wo_w").T
    o["L0_woT"] = np.ascontiguousarray(
        np.concatenate([woT[p * 512:(p + 1) * 512], woT[(1 - p) * 512:(2 - p) * 512]], 0))
    o["L0_wob"] = g("L0_attn_wo_b")
    o["L0_normw"] = g("L0_ffn_norm_weight")
    o["L0_routerT"] = np.ascontiguousarray(g("L0_ffn_router_w").T)
    o["L0_sw1"] = g("L0_ffn_sw1"); o["L0_sb1"] = g("L0_ffn_sb1")
    o["L0_sw2"] = g("L0_ffn_sw2"); o["L0_sb2"] = g("L0_ffn_sb2")
    o["L0_ew1"] = g("L0_ffn_ew1"); o["L0_eb1"] = g("L0_ffn_eb1")
    o["L0_ew2"] = g("L0_ffn_ew2"); o["L0_eb2"] = g("L0_ffn_eb2")
    for nm in ("W_Q", "W_K", "W_V", "W_K_prime"):
        o[f"L1_{nm}T"] = np.ascontiguousarray(g(f"L1_attn_{nm}").T[:, hs])
    w1o = g("L1_attn_W_O").T
    o["L1_woT"] = np.ascontiguousarray(
        np.concatenate([w1o[p * 512:(p + 1) * 512], w1o[(1 - p) * 512:(2 - p) * 512]], 0))
    o["L1_alpha"] = np.asarray(inp["L1_attn_alpha"], np.float32).reshape(1, 1)
    o["L1_routerT"] = np.ascontiguousarray(g("L1_ffn_w_router").T)
    o["L1_cent"] = g("L1_ffn_head_centroids")
    o["L1_temp"] = g("L1_ffn_temperature").reshape(1, 1)
    o["L1_ie"] = g("L1_ffn_input_experts")
    o["L1_oe"] = g("L1_ffn_output_experts")
    qkvT = g("L2_attn_w_qkv").T
    o["L2_qT"] = np.ascontiguousarray(qkvT[:, 0 * D + p * 512:0 * D + (p + 1) * 512])
    o["L2_kT"] = np.ascontiguousarray(qkvT[:, 1 * D + p * 512:1 * D + (p + 1) * 512])
    o["L2_vT"] = np.ascontiguousarray(qkvT[:, 2 * D + p * 512:2 * D + (p + 1) * 512])
    w2o = g("L2_attn_w_proj").T
    o["L2_projT"] = np.ascontiguousarray(
        np.concatenate([w2o[p * 512:(p + 1) * 512], w2o[(1 - p) * 512:(2 - p) * 512]], 0))
    o["L2_gateT"] = np.ascontiguousarray(g("L2_ffn_gate_w").T)
    o["L2_wg"] = g("L2_ffn_w_gate"); o["L2_wu"] = g("L2_ffn_w_up"); o["L2_wd"] = g("L2_ffn_w_down")
    o["L2_swg"] = g("L2_ffn_sw_gate"); o["L2_swu"] = g("L2_ffn_sw_up"); o["L2_swd"] = g("L2_ffn_sw_down")
    o["L2_sgT"] = np.ascontiguousarray(g("L2_ffn_shared_gate_w").T)
    Wq = g("L3_attn_Wq")
    base = p * 8
    qperm = []
    for i in range(4):
        qperm += [base + i, base + i + 4]
    qcols = np.concatenate([np.arange(h * 64, (h + 1) * 64) for h in qperm])
    o["L3_wq"] = np.ascontiguousarray(Wq[:, qcols])
    Wkv = g("L3_attn_Wkv")
    o["L3_wk"] = np.ascontiguousarray(Wkv[:, p * 128:(p + 1) * 128])
    o["L3_wv"] = np.ascontiguousarray(Wkv[:, 256 + p * 128:256 + (p + 1) * 128])
    operm = []
    for pp in (p, 1 - p):
        bb = pp * 8
        for i in range(4):
            operm += [bb + i, bb + i + 4]
    orows = np.concatenate([np.arange(h * 64, (h + 1) * 64) for h in operm])
    o["L3_wo"] = np.ascontiguousarray(g("L3_attn_Wo")[orows, :])
    o["L3_bo"] = g("L3_attn_bo")
    o["L3_wg"] = g("L3_ffn_w_gate")
    o["L3_wu"] = g("L3_ffn_w_up")
    o["L3_wd"] = g("L3_ffn_w_down")
    ep = int(np.random.randint(1, 1 << 30)) if epoch is None else int(epoch)
    o["epoch"] = np.asarray([[ep]], np.int32)
    pay = np.zeros((1, 34), np.int32)
    pay[0, 0::2] = ep
    pay[0, 1::2][:16] = np.arange(1, 17)
    o["flagpay"] = pay
    return o


def _tril_maskT():
    m = np.zeros((128, 128), np.float32)
    for k in range(128):
        m[k, k:] = 1.0
    return m.astype(NPBF16)

def _diag_masks_l2():
    out = np.zeros((128, 4 * 512), np.float32)
    for i in range(4):
        for k in range(128):
            out[k, i * 512 + i * 128 + k: (i + 1) * 512] = 1.0
    return out.astype(NPBF16)

def _rope_tables():
    n, d = TFULL, DH
    inv = 1.0 / 10000.0 ** (np.arange(0, d, 2, dtype=np.float32) / d)
    fr = np.einsum("i,j->ij", np.arange(n, dtype=np.float32), inv)
    fr = np.concatenate([fr, fr], -1)
    cos = np.cos(fr).T
    sin = np.sin(fr).T
    sgn = np.ones((64, 1), np.float32)
    sgn[0:32] = -1.0
    cos2 = np.tile(cos, (2, 1))
    sin2 = np.tile(sin * sgn, (2, 1))
    return cos2.astype(NPBF16), sin2.astype(NPBF16)


def build_program(layers, dbg=None):
    nc = bass.Bass("TRN2", target_bir_lowering=False, detect_race_conditions=False,
                   dynamic_dma_scratch_size=16384)
    NL = len(layers)
    ctx = ExitStack()
    P = Prog(nc, ctx)
    C = Ctx()
    C.nc, C.P, C.ctx = nc, P, ctx
    C.dbg = dbg

    def din(name, shape, dt=F32):
        return nc.dram_tensor(name, list(shape), dt, kind="ExternalInput").ap()
    inp = {}
    names = {
        "x_fm": (D, TOK), "epoch": (1, 1), "flagpay": (1, 34),
        "L0_wqT": (D, 512), "L0_wkT": (D, 512), "L0_wvT": (D, 512),
        "L0_wqb": (512,), "L0_wkb": (512,), "L0_wvb": (512,),
        "L0_woT": (D, D), "L0_wob": (D,),
        "L0_normw": (D,), "L0_routerT": (D, 8),
        "L0_sw1": (D, 4096), "L0_sb1": (4096,), "L0_sw2": (4096, D), "L0_sb2": (D,),
        "L0_ew1": (8, D, 4096), "L0_eb1": (8, 4096),
        "L0_ew2": (8, 4096, D), "L0_eb2": (8, D),
        "L1_W_QT": (D, 512), "L1_W_KT": (D, 512), "L1_W_VT": (D, 512),
        "L1_W_K_primeT": (D, 512), "L1_woT": (D, D), "L1_alpha": (1, 1),
        "L1_routerT": (D, 128), "L1_cent": (16, 128), "L1_temp": (1, 1),
        "L1_ie": (16, D, 512), "L1_oe": (16, 512, 64),
        "L2_qT": (D, 512), "L2_kT": (D, 512), "L2_vT": (D, 512),
        "L2_projT": (D, D),
        "L2_gateT": (D, 8), "L2_wg": (8, D, 1408), "L2_wu": (8, D, 1408),
        "L2_wd": (8, 1408, D), "L2_swg": (D, 1408), "L2_swu": (D, 1408),
        "L2_swd": (1408, D), "L2_sgT": (D, 1),
        "L3_wq": (D, 512), "L3_wk": (D, 128), "L3_wv": (D, 128),
        "L3_wo": (D, D), "L3_bo": (D,),
        "L3_wg": (2816, D), "L3_wu": (2816, D), "L3_wd": (D, 2816),
    }
    need = {"x_fm", "epoch", "flagpay"}
    for l in layers:
        for k in names:
            if k.startswith(f"L{l}_"):
                need.add(k)
    for k in sorted(need):
        dt = I32 if k in ("epoch", "flagpay") else F32
        inp[k] = din(k, names[k], dt)
    x_out = nc.dram_tensor("x_out", [D, TOK], F32, kind="ExternalOutput").ap()
    dbg_out = None
    if dbg is not None:
        dbg_out = nc.dram_tensor("dbg_out", list(dbg[1]), F32, kind="ExternalOutput").ap()
    C.inp, C.x_out, C.dbg_out = inp, x_out, dbg_out

    xn_sh = nc.dram_tensor("xn_sh", [2, NL, 128, DT * TOK], BF16, addr_space="Shared").ap()
    oh_sh = nc.dram_tensor("oh_sh", [2, NL, NHP, 128, TOK], BF16, addr_space="Shared").ap()
    flg = nc.dram_tensor("flg", [2, 64, 2], I32, addr_space="Shared").ap()
    C.xn_sh, C.oh_sh, C.flg = xn_sh, oh_sh, flg
    C.scr = nc.dram_tensor("scr", [64, 128, 4096], F32).ap()

    C.identb = nc.inline_tensor(np.eye(128, dtype=np.float32).astype(NPBF16), "identb").ap()
    C.identf = nc.inline_tensor(np.eye(128, dtype=np.float32), "identf").ap()
    C.trilT = nc.inline_tensor(_tril_maskT(), "trilT").ap()
    dm = _diag_masks_l2()
    C.diagm = nc.inline_tensor(dm, "diagm").ap()
    cos, sin = _rope_tables()
    C.ropec = nc.inline_tensor(cos, "ropec").ap()
    C.ropes = nc.inline_tensor(sin, "ropes").ap()
    sel8 = np.zeros((8, 8 * 128), np.float32)
    for e in range(8):
        sel8[e, e * 128:(e + 1) * 128] = 1.0
    C.sel8 = nc.inline_tensor(sel8.astype(NPBF16), "sel8").ap()
    sel16 = np.zeros((16, 16 * 128), np.float32)
    for e in range(16):
        sel16[e, e * 128:(e + 1) * 128] = 1.0
    C.sel16 = nc.inline_tensor(sel16.astype(NPBF16), "sel16").ap()
    onesf = np.ones((128, 1), np.float32)
    C.onescol_f = nc.inline_tensor(onesf, "onescol_f").ap()
    C.onescol_b = nc.inline_tensor(onesf.astype(NPBF16), "onescol_b").ap()
    C.onesrow_f = nc.inline_tensor(np.ones((1, 128), np.float32), "onesrow_f").ap()
    C.onesrow_b = nc.inline_tensor(np.ones((1, 128), np.float32).astype(NPBF16), "onesrow_b").ap()
    sel2h = np.zeros((128, 2), np.float32)
    sel2h[0:64, 0] = 1.0
    sel2h[64:128, 1] = 1.0
    C.sel2h = nc.inline_tensor(sel2h.astype(NPBF16), "sel2h").ap()

    C.ch_t = P.chan("ch_t")
    C.ch_s = P.chan("ch_s")
    C.ch_v = P.chan("ch_v")
    C.ch_g = P.chan("ch_g")
    C.ch_wd = P.chan("ch_wd")
    C.ch_sd = P.chan("ch_sd")
    C.ch_ad = P.chan("ch_ad")
    C.ch_vd = P.chan("ch_vd")
    P.chain = {"vector": C.ch_v, "scalar": C.ch_s}
    C.psem = ctx.enter_context(nc.semaphore("psem"))

    sb = lambda name, shape, dt: ctx.enter_context(nc.sbuf_tensor(name, list(shape), dt))
    C.sb = sb

    C.PB = [ctx.enter_context(nc.psum_tensor(f"pb{i}", [128, 512], F32)) for i in range(7)]
    C.pend45 = {}
    C.PT = ctx.enter_context(nc.psum_tensor("pt", [128, 1024], BF16))

    C.X = sb("X", [128, DT * TOK], F32)
    C.CONST = {}
    C.flagv = sb("flagv", [1, 34], I32)
    C.flagr = sb("flagr", [1, 2], I32)
    C.identb_s = sb("identb_s", [128, 128], BF16)
    C.trilT_s = sb("trilT_s", [128, 128], BF16)
    C.onescol_f_s = sb("onescol_f_s", [128, 1], F32)
    C.onescol_b_s = sb("onescol_b_s", [128, 1], BF16)
    C.onesrow_f_s = sb("onesrow_f_s", [1, 128], F32)
    C.onesrow_b_s = sb("onesrow_b_s", [1, 128], BF16)
    C.sel8_s = sb("sel8_s", [8, 8 * 128], BF16)
    C.sel16_s = sb("sel16_s", [16, 16 * 128], BF16)
    C.sel2h_s = sb("sel2h_s", [128, 2], BF16)
    C.ln_ms = sb("ln_ms", [1, 512], F32)
    C.ln_rv = sb("ln_rv", [1, 512], F32)
    C.ln_rs = C.ln_rv
    C.ln_b0 = sb("ln_b0", [1, 512], F32)
    C.ln_t2 = sb("ln_t2", [128, 512], BF16)
    C.ffn_srow = sb("ffn_srow", [128, 8], F32)
    C.ffn_trow = sb("ffn_trow", [128, 16], F32)
    C.att_nr = sb("att_nr", [1, 16], F32)
    C.scr_oh = nc.dram_tensor("scr_oh", [NHP, 128, TFULL], BF16).ap()
    C.scr_h = nc.dram_tensor("scr_h", [128, 22 * 2048], BF16).ap()

    def init_sync(sy):
        h = P.holder
        pid = sy.compute_val(nc.partition_id(engines=[ENG.SP]))
        h["par"] = nc.alloc_register(ENG.SP, "par")
        sy.reg_alu(h["par"], pid, 2, ALU.mod)
        h["pcnt"] = nc.alloc_register(ENG.SP, "pcnt")
        sy.reg_mov(h["pcnt"], 0)
        h["repoch"] = nc.alloc_register(ENG.SP, "repoch")
        h["rflag_e"] = nc.alloc_register(ENG.SP, "rflag_e")
        h["rflag_k"] = nc.alloc_register(ENG.SP, "rflag_k")
        h["t1"] = nc.alloc_register(ENG.SP, "t1")
        h["t2"] = nc.alloc_register(ENG.SP, "t2")
        h["t3"] = nc.alloc_register(ENG.SP, "t3")
        return sy.nop()
    P.op("sync", init_sync)
    P.op("sync", lambda sy: sy.dma_start(out=C.flagv[0:1, 0:34], in_=inp["flagpay"][0:1, 0:34]),
         inc=(C.ch_sd, 16))
    ep_ready = C.ch_sd.n
    if USE_POLL:
        P.op("sync", lambda sy: sy.reg_load(P.holder["repoch"], C.flagv[0:1, 0:1]),
             waits=[(C.ch_sd, ep_ready)])
    consts = [(C.identb_s, C.identb), (C.trilT_s, C.trilT),
              (C.onescol_f_s, C.onescol_f), (C.onescol_b_s, C.onescol_b),
              (C.onesrow_f_s, C.onesrow_f), (C.onesrow_b_s, C.onesrow_b),
              (C.sel8_s, C.sel8), (C.sel16_s, C.sel16),
              (C.sel2h_s, C.sel2h)]
    for dst, src in consts:
        P.op("sync", lambda sy, dst=dst, src=src: sy.dma_start(out=dst[:, :], in_=src[:, :]),
             inc=(C.ch_sd, 16))
    P.op("sync", lambda sy: sy.dma_start(
        out=C.X.rearrange("p (j t) -> p j t", j=DT)[:, :, :],
        in_=inp["x_fm"].rearrange("(j p) t -> p j t", p=128)),
        inc=(C.ch_sd, 16))
    C.x_ready = [(C.ch_sd, C.ch_sd.n)]
    C.const_ready = [(C.ch_sd, C.ch_sd.n)]
    return C


def sh_write(C, dst2, src_ap, fidx, data_waits):
    P = C.P
    v1 = C.ch_sd.bump(16)
    v2 = C.ch_sd.bump(16)
    def emit(sy):
        h = P.holder
        with sy.If_eq(h["par"], 0):
            sy.dma_start(out=dst2(0), in_=src_ap).then_inc(C.ch_sd.h, 16)
        with sy.Else():
            sy.dma_start(out=dst2(1), in_=src_ap).then_inc(C.ch_sd.h, 16)
        sy.end_ifs()
        sy.wait_ge(C.ch_sd.h, v1)
        with sy.If_eq(h["par"], 0):
            sy.dma_start(out=C.flg[0, fidx, 0:2], in_=C.flagv[0:1, 2 * fidx:2 * fidx + 2]).then_inc(C.ch_sd.h, 16)
        with sy.Else():
            sy.dma_start(out=C.flg[1, fidx, 0:2], in_=C.flagv[0:1, 2 * fidx:2 * fidx + 2]).then_inc(C.ch_sd.h, 16)
        sy.end_ifs()
        return sy.nop()
    P.op("sync", emit, waits=data_waits)
    return v2

def sh_poll_read(C, fidx, reads, extra_waits=()):
    P = C.P
    ktar = fidx + 1
    vend = C.ch_sd.bump(16 * len(reads))
    def emit(sy):
        h = P.holder
        def mkcond(slot):
            def cond():
                sy.reg_add(h["pcnt"], h["pcnt"], 16)
                sy.dma_start(out=C.flagr[0:1, 0:2], in_=C.flg[slot, fidx, 0:2]).then_inc(C.psem, 16)
                sy.wait_ge(C.psem, h["pcnt"])
                sy.reg_load(h["rflag_e"], C.flagr[0:1, 0:1])
                sy.reg_load(h["rflag_k"], C.flagr[0:1, 1:2])
                sy.reg_alu(h["t1"], h["rflag_e"], h["repoch"], ALU.subtract)
                sy.reg_alu(h["t2"], h["rflag_k"], ktar, ALU.is_lt)
                sy.reg_alu(h["t3"], h["t1"], h["t2"], ALU.bitwise_or)
                return h["t3"]
            return cond
        if not USE_POLL:
            for _ in range(100):
                sy.nop(cycle_cnt=10000, nofuse=True)
        for slot_if, slot in ((0, 1), (1, 0)):
            guard = sy.If_eq(P.holder["par"], slot_if) if slot_if == 0 else sy.Else()
            with guard:
                if USE_POLL:
                    with sy.While(mkcond(slot)):
                        sy.nop()
                for dst, src2 in reads:
                    sy.dma_start(out=dst, in_=src2(slot)).then_inc(C.ch_sd.h, 16)
        sy.end_ifs()
        return sy.nop()
    P.op("sync", emit, waits=list(extra_waits))
    return vend


def emit_ln(C, xno, sq, x_waits):
    P = C.P
    ms, rv, rs, b0, t2 = (C.ln_ms, C.ln_rv, C.ln_rs, C.ln_b0, C.ln_t2)
    t1 = b0
    PB = C.PB
    X = C.X

    def xsl(j, c):
        return X[:, j * TOK + c * 512: j * TOK + (c + 1) * 512]

    prev_chunk_v = None
    mm_hist = {}
    lastv = None
    for c in range(4):
        half = (c % 2) * 4096
        sqv = None
        for j in range(DT):
            w = list(x_waits) if (c == 0 and j == 0) else []
            if j == 0 and c >= 2:
                w.append((C.ch_t, mm_hist[c - 2]))
            sqv = P.op("scalar", lambda s, j=j, c=c, half=half: s.activation(
                out=sq[:, half + j * 512: half + (j + 1) * 512], in_=xsl(j, c),
                func=AF.Square), waits=w, inc=(C.ch_s, 1))
        w0 = list(x_waits) + ([prev_chunk_v] if prev_chunk_v else [])
        for j in range(DT):
            P.op("tensor", lambda t, j=j, c=c: t.matmul(
                PB[0][0:1, 0:512], C.onescol_f_s[:, 0:1], xsl(j, c),
                start=(j == 0), stop=(j == DT - 1)), waits=(w0 if j == 0 else ()))
        mmv = None
        for j in range(DT):
            kwargs = {}
            if j == DT - 1:
                kwargs = dict(inc=(C.ch_t, 1))
            mmv = P.op("tensor", lambda t, j=j, half=half: t.matmul(
                PB[1][0:1, 0:512], C.onescol_b_s[:, 0:1],
                sq[:, half + j * 512: half + (j + 1) * 512],
                start=(j == 0), stop=(j == DT - 1)),
                waits=([(C.ch_s, sqv)] if j == 0 else ()), **kwargs)
        mm_hist[c] = mmv
        P.op("vector", lambda v: v.tensor_scalar(ms[0:1, :], PB[0][0:1, 0:512],
             1.0 / 1024.0, None, ALU.mult), waits=[(C.ch_t, mmv)])
        P.op("vector", lambda v: v.tensor_mul(t1[0:1, :], ms[0:1, :], ms[0:1, :]))
        P.op("vector", lambda v: v.scalar_tensor_tensor(rv[0:1, :], PB[1][0:1, 0:512],
             1.0 / 1024.0, t1[0:1, :], ALU.mult, ALU.subtract))
        P.op("vector", lambda v: v.tensor_scalar(rv[0:1, :], rv[0:1, :], 1e-5, None, ALU.add))
        vv = P.op("vector", lambda v: v.reciprocal(rv[0:1, :], rv[0:1, :]), inc=(C.ch_v, 1))
        sv = P.op("scalar", lambda s: s.activation(out=rs[0:1, :], in_=rv[0:1, :],
             func=AF.Sqrt), waits=[(C.ch_v, vv)], inc=(C.ch_s, 1))
        bv = P.op("vector", lambda v: v.scalar_tensor_tensor(b0[0:1, :], ms[0:1, :],
             -1.0, rs[0:1, :], ALU.mult, ALU.mult), waits=[(C.ch_s, sv)], inc=(C.ch_v, 1))
        P.op("tensor", lambda t: t.matmul(PB[2][:, 0:512], C.onesrow_f_s[0:1, :],
             rs[0:1, :], start=True, stop=True), waits=[(C.ch_v, bv)])
        mmb = P.op("tensor", lambda t: t.matmul(PB[3][:, 0:512], C.onesrow_f_s[0:1, :],
             b0[0:1, :], start=True, stop=True), inc=(C.ch_t, 1))
        for j in range(DT):
            P.op("vector", lambda v, j=j, c=c: v.tensor_mul(t2[:, :], xsl(j, c),
                 PB[2][:, 0:512]), waits=([(C.ch_t, mmb)] if j == 0 else ()))
            lastv = P.op("vector", lambda v, j=j, c=c: v.tensor_add(
                 xno[:, j * TOK + c * 512: j * TOK + (c + 1) * 512],
                 t2[:, :], PB[3][:, 0:512]), inc=(C.ch_v, 1))
        prev_chunk_v = (C.ch_v, lastv)
    return [(C.ch_v, lastv)]

def phase_barrier(C):
    vals = [(ch, ch.n) for ch in (C.ch_t, C.ch_s, C.ch_v, C.ch_g, C.ch_wd,
                                  C.ch_sd, C.ch_ad, C.ch_vd) if ch.n > 0]
    for e in Prog.ENGINES:
        C.P.op(e, lambda eng: eng.nop(), waits=vals)

def emit_xn_assemble(C, li, XN, xno_written_v, poll=True, poll_own=False):
    P = C.P
    vown = C.ch_sd.bump(16)
    def emit_own(sy):
        h = P.holder
        for pv in (0, 1):
            guard = sy.If_eq(h["par"], 0) if pv == 0 else sy.Else()
            with guard:
                sy.dma_start(
                    out=XN.rearrange("p (j t) -> p j t", j=DT)[:, :, pv * TOK:(pv + 1) * TOK],
                    in_=C.xn_sh[pv, li].rearrange("p (j t) -> p j t", j=DT)[:, :, :],
                ).then_inc(C.ch_sd.h, 16)
        sy.end_ifs()
        return sy.nop()
    P.op("sync", emit_own, waits=[(C.ch_sd, xno_written_v)])
    ktar = 2 * li + 1
    fidx = 2 * li
    vend = C.ch_sd.bump(16)
    def emit_poll(sy):
        h = P.holder
        def mkcond(slot):
            def cond():
                sy.reg_add(h["pcnt"], h["pcnt"], 16)
                sy.dma_start(out=C.flagr[0:1, 0:2], in_=C.flg[slot, fidx, 0:2]).then_inc(C.psem, 16)
                sy.wait_ge(C.psem, h["pcnt"])
                sy.reg_load(h["rflag_e"], C.flagr[0:1, 0:1])
                sy.reg_load(h["rflag_k"], C.flagr[0:1, 1:2])
                sy.reg_alu(h["t1"], h["rflag_e"], h["repoch"], ALU.subtract)
                sy.reg_alu(h["t2"], h["rflag_k"], ktar, ALU.is_lt)
                sy.reg_alu(h["t3"], h["t1"], h["t2"], ALU.bitwise_or)
                return h["t3"]
            return cond
        if (not poll) or not USE_POLL:
            for _ in range(400):
                sy.nop(cycle_cnt=10000, nofuse=True)
        for myp in (0, 1):
            slot = 1 - myp
            guard = sy.If_eq(h["par"], 0) if myp == 0 else sy.Else()
            with guard:
                if poll and USE_POLL:
                    with sy.While(mkcond(slot)):
                        sy.nop()
                elif poll_own:
                    with sy.While(mkcond(myp)):
                        sy.nop()
                sy.dma_start(
                    out=XN.rearrange("p (j t) -> p j t", j=DT)[:, :, slot * TOK:(slot + 1) * TOK],
                    in_=C.xn_sh[slot, li].rearrange("p (j t) -> p j t", j=DT)[:, :, :],
                ).then_inc(C.ch_sd.h, 16)
        sy.end_ifs()
        return sy.nop()
    P.op("sync", emit_poll)
    return [(C.ch_sd, vown), (C.ch_sd, vend)]

def emit_dbg_dump(C, src_ap, waits):
    v = C.P.op("gpsimd", lambda g: g.dma_start(out=C.dbg_out[:, :], in_=src_ap),
               waits=waits, inc=(C.ch_wd, 16))
    return [(C.ch_wd, v)]

def finish_program(C, x_done_waits):
    P = C.P
    v = P.op("sync", lambda sy: sy.dma_start(
        out=C.x_out.rearrange("(j p) t -> p j t", p=128),
        in_=C.X.rearrange("p (j t) -> p j t", j=DT)[:, :, :]),
        waits=x_done_waits, inc=(C.ch_sd, 16))
    P.op("sync", lambda sy: sy.nop(), waits=[(C.ch_sd, v)])
    phase_barrier(C)


def build_full(layers, stage="full"):
    dbg = None
    if stage in ("ln", "lnw", "lnd", "lnp1"):
        dbg = ("xn", (128, DT * TFULL))
    elif stage in ("ln_local", "l0norm"):
        dbg = ("xno", (128, DT * TOK))
    elif stage == "l0gates":
        dbg = ("g", (8, TOK))
    elif stage == "l0shared":
        dbg = ("sh", (1, 1))
    C = build_program(layers, dbg=dbg)
    nc = C.nc
    x_waits = list(C.x_ready) + list(C.const_ready)
    if stage == "pass":
        finish_program(C, x_waits)
        C.P.emit_all()
        C.ctx.close()
        return C
    stop = False
    for li, l in enumerate(layers):
        phase_barrier(C)
        dbgstage = stage in ("ln", "lnw", "lnd", "lnp1", "ln_local")
        if dbgstage:
            with nc.sbuf_tensor(f"xno{li}", [128, DT * TOK], BF16) as xno:
                with nc.sbuf_tensor(f"sq{li}", [128, 2 * 4096], BF16) as sq:
                    ln_done = emit_ln(C, xno, sq, x_waits)
                if stage == "ln_local":
                    dv = emit_dbg_dump(C, xno[:, :], ln_done)
                    finish_program(C, dv)
                    break
                vw = sh_write(C, lambda slot, li=li: C.xn_sh[slot, li][:, :],
                              xno[:, :], 2 * li, ln_done)
            with nc.sbuf_tensor(f"XN_{li}", [128, DT * TFULL], BF16) as XN:
                xn_ready = emit_xn_assemble(C, li, XN, vw, poll=(stage == "ln"),
                                            poll_own=(stage == "lnp1"))
                dv = emit_dbg_dump(C, XN[:, :], xn_ready)
                finish_program(C, dv)
            stop = True
            break
        x_waits = emit_layer(C, li, l, x_waits, stage)
    if not stop:
        if stage in ("full", "l0ffn", "l0attn", "l0shared", "ffnonly", "attnonly"):
            finish_program(C, x_waits)
    C.P.emit_all()
    C.ctx.close()
    return C

def emit_layer(C, li, l, x_waits, stage="full"):
    nc = C.nc
    if l == 0:
        with nc.sbuf_tensor(f"f0_norm_{li}", [128, DT * TOK], BF16) as NORM:
            C.f0_norm = NORM
            with nc.sbuf_tensor(f"xno{li}", [128, DT * TOK], BF16) as xno:
                with nc.sbuf_tensor(f"sq{li}", [128, 2 * 4096], BF16) as sq:
                    ln_done = emit_ln(C, xno, sq, x_waits)
                vw = sh_write(C, lambda slot, li=li: C.xn_sh[slot, li][:, :],
                              xno[:, :], 2 * li, ln_done)
                with nc.sbuf_tensor(f"f0_rsq_{li}", [128, DT * 512], BF16) as RSQ, \
                     nc.sbuf_tensor(f"f0_nwt_{li}", [128, 8], F32) as NWT:
                    C.f0_rsq = RSQ
                    phase_barrier(C)
                    norm_done = emit_ffn_l0_norm(C, xno, ln_done, NWT)
                if stage == "l0norm":
                    dv = emit_dbg_dump(C, NORM[:, :], norm_done)
                    finish_program(C, dv)
                    return norm_done
            with nc.sbuf_tensor(f"f0_wb1_{li}", [128, 8 * 2048], BF16) as WB1, \
                 nc.sbuf_tensor(f"f0_wb2_{li}", [128, 16 * 1024], BF16) as WB2, \
                 nc.sbuf_tensor(f"f0_ht_{li}", [128, 16 * 512], BF16) as HT, \
                 nc.sbuf_tensor(f"f0_prb_{li}", [128, 16 * 8], F32) as PRB, \
                 nc.sbuf_tensor(f"f0_gt_{li}", [128, 16 * 8], BF16) as GT, \
                 nc.sbuf_tensor(f"f0_gf_{li}", [8, TOK], BF16) as GF, \
                 nc.sbuf_tensor(f"f0_eb2_{li}", [8, 1024], BF16) as EB2, \
                 nc.sbuf_tensor(f"f0_sb1_{li}", [128, 32], F32) as SB1, \
                 nc.sbuf_tensor(f"f0_sb2_{li}", [128, 8], F32) as SB2, \
                 nc.sbuf_tensor(f"f0_eb1_{li}", [128, 32], F32) as EB1, \
                 nc.sbuf_tensor(f"f0_rw_{li}", [128, 8 * 8], BF16) as RW:
                for nm, tt in (("f0_wb1", WB1), ("f0_wb2", WB2), ("f0_ht", HT),
                               ("f0_prb", PRB), ("f0_gt", GT), ("f0_gf", GF),
                               ("f0_eb2", EB2), ("f0_sb1", SB1),
                               ("f0_sb2", SB2), ("f0_eb1", EB1), ("f0_rw", RW)):
                    setattr(C, nm, tt)
                phase_barrier(C)
                if stage == "l0attn":
                    ffn_done = norm_done
                else:
                    ffn_done = emit_ffn_l0(C, norm_done, ln_done)
        if stage in ("l0ffn", "l0shared"):
            return ffn_done
        with nc.sbuf_tensor(f"XN_{li}", [128, DT * TFULL], BF16) as XN, \
             nc.sbuf_tensor(f"a_wq_{li}", [128, 8 * 512], BF16) as WQ, \
             nc.sbuf_tensor(f"a_wk_{li}", [128, 8 * 512], BF16) as WK, \
             nc.sbuf_tensor(f"a_wv_{li}", [128, 8 * 512], BF16) as WV, \
             nc.sbuf_tensor(f"a_qf_{li}", [128, TFULL], BF16) as QF, \
             nc.sbuf_tensor(f"a_kf_{li}", [128, TFULL], BF16) as KF, \
             nc.sbuf_tensor(f"a_kt_{li}", [128, TFULL], BF16) as KT, \
             nc.sbuf_tensor(f"a_vt_{li}", [128, 32 * 130], BF16) as VT, \
             nc.sbuf_tensor(f"a_ohq_{li}", [128, 2 * 512], BF16) as OHQ, \
             nc.sbuf_tensor(f"a_ss_{li}", [128, 64], BF16) as SS, \
             nc.sbuf_tensor(f"a_ssf_{li}", [128, 64], F32) as SSF, \
             nc.sbuf_tensor(f"a_pti_{li}", [128, 2 * 128], BF16) as PTI, \
             nc.sbuf_tensor(f"a_bq_{li}", [128, 12], F32) as BQ, \
             nc.sbuf_tensor(f"a_bkb_{li}", [128, 256], BF16) as BKB, \
             nc.sbuf_tensor(f"a_br_{li}", [1, 128], BF16) as BR:
            for nm, tt in (("att_wq", WQ), ("att_wk", WK), ("att_wv", WV),
                           ("att_qf", QF), ("att_kf", KF), ("att_kt", KT),
                           ("att_vt", VT), ("att_ohq", OHQ), ("att_ss", SS),
                           ("att_ssf", SSF),
                           ("att_pti", PTI), ("att_bq", BQ), ("att_bkb", BKB),
                           ("att_br", BR)):
                setattr(C, nm, tt)
            phase_barrier(C)
            xn_ready = emit_xn_assemble(C, li, XN, C.ch_sd.n)
            oh_done, vfl = emit_attn_l0(C, li, XN, xn_ready, C.scr_oh)
        with nc.sbuf_tensor(f"a_wo_{li}", [128, 8 * 1024], BF16) as WO, \
             nc.sbuf_tensor(f"a_ohs_{li}", [128, 8 * 512], BF16) as OHS, \
             nc.sbuf_tensor(f"a_bo_{li}", [128, 8], F32) as BO:
            C.att_ohs = OHS
            phase_barrier(C)
            wov = load_w_fm(C, "L0_woT", WO, D, D)
            bov = load_bias_fm(C, "L0_wob", BO, D)
            x_done = emit_wo(C, li, l, C.scr_oh, list(oh_done) + list(ffn_done) + [wov, bov],
                             vfl, WO, wov, bias_tile=BO, x_waits=list(ffn_done))
        return x_done
    if l == 1:
        with nc.sbuf_tensor(f"xno{li}", [128, DT * TOK], BF16) as xno:
            with nc.sbuf_tensor(f"sq{li}", [128, 2 * 4096], BF16) as sq:
                ln_done = emit_ln(C, xno, sq, x_waits)
            vw = sh_write(C, lambda slot, li=li: C.xn_sh[slot, li][:, :],
                          xno[:, :], 2 * li, ln_done)
            with nc.sbuf_tensor(f"f1_rw_{li}", [128, 8 * 128], BF16) as RW1, \
                 nc.sbuf_tensor(f"f1_zf_{li}", [128, TOK], BF16) as ZF, \
                 nc.sbuf_tensor(f"f1_sqz_{li}", [128, TOK], BF16) as SQZ, \
                 nc.sbuf_tensor(f"f1_rsd_{li}", [1, TOK], F32) as RSD, \
                 nc.sbuf_tensor(f"f1_ct_{li}", [16, 128], F32) as CTt, \
                 nc.sbuf_tensor(f"f1_cs_{li}", [16, 128], BF16) as CS, \
                 nc.sbuf_tensor(f"f1_cnt_{li}", [128, 16], BF16) as CNT, \
                 nc.sbuf_tensor(f"f1_lt_{li}", [128, 16 * 16], F32) as LT, \
                 nc.sbuf_tensor(f"f1_prb_{li}", [128, 16 * 16], F32) as PRB, \
                 nc.sbuf_tensor(f"f1_gt_{li}", [128, 16 * 16], BF16) as GT, \
                 nc.sbuf_tensor(f"f1_gf_{li}", [16, TOK], BF16) as GF1, \
                 nc.sbuf_tensor(f"f1_wie_{li}", [128, 8 * 512], BF16) as WIE, \
                 nc.sbuf_tensor(f"f1_woe_{li}", [128, 4 * 64], BF16) as WOE, \
                 nc.sbuf_tensor(f"f1_ht_{li}", [128, 4 * 512], BF16) as HT1, \
                 nc.sbuf_tensor(f"f1_t32_{li}", [128, 512], F32) as T32, \
                 nc.sbuf_tensor(f"f1_etb_{li}", [128, 1], F32) as ETB:
                for nm, tt in (("f1_rw", RW1), ("f1_zf", ZF), ("f1_sqz", SQZ),
                               ("f1_rsd", RSD), ("f1_ct", CTt), ("f1_cs", CS),
                               ("f1_cnt", CNT), ("f1_lt", LT), ("f1_prb", PRB),
                               ("f1_gt", GT), ("f1_gf", GF1), ("f1_wie", WIE),
                               ("f1_woe", WOE), ("f1_ht", HT1), ("f1_t32", T32),
                               ("f1_etb", ETB)):
                    setattr(C, nm, tt)
                phase_barrier(C)
                if stage == "attnonly":
                    ffn_done = ln_done
                else:
                    ffn_done = emit_ffn_l1(C, xno, ln_done, ln_done)
        if stage == "ffnonly":
            return ffn_done
        with nc.sbuf_tensor(f"XN_{li}", [128, DT * TFULL], BF16) as XN, \
             nc.sbuf_tensor(f"a_wq_{li}", [128, 8 * 128], BF16) as WQ, \
             nc.sbuf_tensor(f"a_wk_{li}", [128, 8 * 128], BF16) as WK, \
             nc.sbuf_tensor(f"a_wv_{li}", [128, 8 * 128], BF16) as WV, \
             nc.sbuf_tensor(f"a_qf_{li}", [128, TFULL], BF16) as QF, \
             nc.sbuf_tensor(f"a_kf_{li}", [128, TFULL], BF16) as KF, \
             nc.sbuf_tensor(f"a_vt_{li}", [128, 32 * 130], BF16) as VT, \
             nc.sbuf_tensor(f"a_ohq_{li}", [128, 2 * 512], BF16) as OHQ, \
             nc.sbuf_tensor(f"a_pt_{li}", [128, 4 * 512], BF16) as PT4, \
             nc.sbuf_tensor(f"a_sqb_{li}", [128, TFULL], BF16) as SQB, \
             nc.sbuf_tensor(f"a_o32_{li}", [128, TFULL], BF16) as O32, \
             nc.sbuf_tensor(f"a_t32_{li}", [128, 512], F32) as T32a, \
             nc.sbuf_tensor(f"a_bh_{li}", [128, 2], F32) as BH, \
             nc.sbuf_tensor(f"a_r0_{li}", [1, 2 * 512], BF16) as R0, \
             nc.sbuf_tensor(f"a_al_{li}", [1, 2], F32) as AL, \
             nc.sbuf_tensor(f"a_alb_{li}", [128, 2], F32) as ALB:
            for nm, tt in (("att_wq", WQ), ("att_wk", WK), ("att_wv", WV),
                           ("att_qf", QF), ("att_kf", KF), ("att_vt", VT),
                           ("att_ohq", OHQ), ("att_pt", PT4), ("att_sqb", SQB),
                           ("att_o32", O32), ("att_t32", T32a), ("att_bh", BH),
                           ("att_r0", R0), ("att_al", AL), ("att_alb", ALB)):
                setattr(C, nm, tt)
            phase_barrier(C)
            av = C.P.op("gpsimd", lambda g: g.dma_start(out=AL[0:1, 0:1],
                 in_=C.inp["L1_alpha"][0:1, 0:1]), inc=(C.ch_wd, 16))
            a1 = C.P.op("vector", lambda v: v.tensor_scalar(AL[0:1, 1:2], AL[0:1, 0:1],
                 -1.0, 1.0, ALU.mult, ALU.add), waits=[(C.ch_wd, av)],
                 inc=(C.ch_v, 1))
            mma = C.P.op("tensor", lambda t: t.matmul(C.PB[6][:, 0:2],
                 C.onesrow_f_s[0:1, :], AL[0:1, 0:2], start=True, stop=True),
                 waits=[(C.ch_v, a1)] + list(C.const_ready), inc=(C.ch_t, 1))
            C.P.op("vector", lambda v: v.tensor_copy(ALB[:, 0:2], C.PB[6][:, 0:2]),
                 waits=[(C.ch_t, mma)], inc=(C.ch_v, 1))
            cfg = {
                "load_q": lambda hp: load_w_fm_slice(C, "L1_W_QT", WQ, D, 512, hp * 128, 128),
                "load_k": lambda hp: load_w_fm_slice(C, "L1_W_KT", WK, D, 512, hp * 128, 128),
                "load_v": lambda hp: load_w_fm_slice(C, "L1_W_VT", WV, D, 512, hp * 128, 128),
                "load_k2": lambda hp: load_w_fm_slice(
                    C, "L1_W_K_primeT", WK, D, 512, hp * 128, 128,
                    waits=[(C.ch_s, C.ch_s.n)]),
                "passes": 2,
            }
            xn_ready = emit_xn_assemble(C, li, XN, C.ch_sd.n)
            oh_done, vfl = emit_attn_softmax_layer(C, li, l, XN, xn_ready,
                                                   C.scr_oh, cfg)
        with nc.sbuf_tensor(f"a_wo_{li}", [128, 8 * 1024], BF16) as WO, \
             nc.sbuf_tensor(f"a_ohs_{li}", [128, 8 * 512], BF16) as OHS:
            C.att_ohs = OHS
            phase_barrier(C)
            wov = load_w_fm(C, "L1_woT", WO, D, D)
            x_done = emit_wo(C, li, l, C.scr_oh, list(oh_done) + list(ffn_done) + [wov],
                             vfl, WO, wov, bias_tile=None, x_waits=list(ffn_done))
        return x_done
    if l == 2:
        with nc.sbuf_tensor(f"xno{li}", [128, DT * TOK], BF16) as xno:
            with nc.sbuf_tensor(f"sq{li}", [128, 2 * 4096], BF16) as sq:
                ln_done = emit_ln(C, xno, sq, x_waits)
            vw = sh_write(C, lambda slot, li=li: C.xn_sh[slot, li][:, :],
                          xno[:, :], 2 * li, ln_done)
            with nc.sbuf_tensor(f"f2_gw_{li}", [128, 8 * 8], BF16) as GW, \
                 nc.sbuf_tensor(f"f2_lt_{li}", [128, 16 * 8], F32) as LT, \
                 nc.sbuf_tensor(f"f2_prb_{li}", [128, 16 * 8], F32) as PRB, \
                 nc.sbuf_tensor(f"f2_gt_{li}", [128, 16 * 8], BF16) as GT, \
                 nc.sbuf_tensor(f"f2_gf_{li}", [8, TOK], BF16) as GF2, \
                 nc.sbuf_tensor(f"f2_wbg_{li}", [128, 8 * 1408], BF16) as WBg, \
                 nc.sbuf_tensor(f"f2_wbu_{li}", [128, 8 * 1408], BF16) as WBu, \
                 nc.sbuf_tensor(f"f2_wbd_{li}", [128, 11 * 1024], BF16) as WBd, \
                 nc.sbuf_tensor(f"f2_hti_{li}", [128, 11 * 512], BF16) as HTI, \
                 nc.sbuf_tensor(f"f2_sgf_{li}", [1, TOK], BF16) as SGF, \
                 nc.sbuf_tensor(f"f2_sgw_{li}", [128, 8 * 1], BF16) as SGW, \
                 nc.sbuf_tensor(f"f2_t32_{li}", [128, 512], F32) as T32:
                for nm, tt in (("f2_gw", GW), ("f2_lt", LT), ("f2_prb", PRB),
                               ("f2_gt", GT), ("f2_gf", GF2), ("f2_wbg", WBg),
                               ("f2_wbu", WBu), ("f2_wbd", WBd),
                               ("f2_hti", HTI), ("f2_sgf", SGF), ("f2_sgw", SGW),
                               ("f2_t32", T32)):
                    setattr(C, nm, tt)
                phase_barrier(C)
                if stage == "attnonly":
                    ffn_done = ln_done
                else:
                    ffn_done = emit_ffn_l2(C, xno, ln_done, ln_done)
        if stage == "ffnonly":
            return ffn_done
        with nc.sbuf_tensor(f"XN_{li}", [128, DT * TFULL], BF16) as XN, \
             nc.sbuf_tensor(f"a_wq_{li}", [128, 8 * 128], BF16) as WQ, \
             nc.sbuf_tensor(f"a_wk_{li}", [128, 8 * 128], BF16) as WK, \
             nc.sbuf_tensor(f"a_wv_{li}", [128, 8 * 128], BF16) as WV, \
             nc.sbuf_tensor(f"a_qf_{li}", [128, TFULL], BF16) as QF, \
             nc.sbuf_tensor(f"a_kf_{li}", [128, TFULL], BF16) as KF, \
             nc.sbuf_tensor(f"a_vt_{li}", [128, 32 * 130], BF16) as VT, \
             nc.sbuf_tensor(f"a_ohq_{li}", [128, 2 * 512], BF16) as OHQ, \
             nc.sbuf_tensor(f"a_pt_{li}", [128, 4 * 512], BF16) as PT4, \
             nc.sbuf_tensor(f"a_sqb_{li}", [128, TFULL], BF16) as SQB, \
             nc.sbuf_tensor(f"a_dm_{li}", [128, 4 * 512], BF16) as DM, \
             nc.sbuf_tensor(f"a_bh_{li}", [128, 2], F32) as BH, \
             nc.sbuf_tensor(f"a_t32_{li}", [128, 512], F32) as T32a, \
             nc.sbuf_tensor(f"a_r0_{li}", [1, 2 * 512], BF16) as R0:
            for nm, tt in (("att_wq", WQ), ("att_wk", WK), ("att_wv", WV),
                           ("att_qf", QF), ("att_kf", KF), ("att_vt", VT),
                           ("att_ohq", OHQ), ("att_pt", PT4), ("att_sqb", SQB),
                           ("att_bh", BH), ("att_r0", R0), ("att_t32", T32a)):
                setattr(C, nm, tt)
            C.diagm_s = DM
            phase_barrier(C)
            C.P.op("sync", lambda sy: sy.dma_start(out=DM[:, :], in_=C.diagm[:, :]),
                   inc=(C.ch_sd, 16))
            cfg = {
                "load_q": lambda hp: load_w_fm_slice(C, "L2_qT", WQ, D, 512, hp * 128, 128),
                "load_k": lambda hp: load_w_fm_slice(C, "L2_kT", WK, D, 512, hp * 128, 128),
                "load_v": lambda hp: load_w_fm_slice(C, "L2_vT", WV, D, 512, hp * 128, 128),
                "causal": True,
            }
            xn_ready = emit_xn_assemble(C, li, XN, C.ch_sd.n)
            oh_done, vfl = emit_attn_softmax_layer(C, li, l, XN, xn_ready,
                                                   C.scr_oh, cfg)
        with nc.sbuf_tensor(f"a_wo_{li}", [128, 8 * 1024], BF16) as WO, \
             nc.sbuf_tensor(f"a_ohs_{li}", [128, 8 * 512], BF16) as OHS:
            C.att_ohs = OHS
            phase_barrier(C)
            wov = load_w_fm(C, "L2_projT", WO, D, D)
            x_done = emit_wo(C, li, l, C.scr_oh, list(oh_done) + list(ffn_done) + [wov],
                             vfl, WO, wov, bias_tile=None, x_waits=list(ffn_done))
        return x_done
    if l == 3:
        with nc.sbuf_tensor(f"xno{li}", [128, DT * TOK], BF16) as xno:
            with nc.sbuf_tensor(f"sq{li}", [128, 2 * 4096], BF16) as sq:
                ln_done = emit_ln(C, xno, sq, x_waits)
            vw = sh_write(C, lambda slot, li=li: C.xn_sh[slot, li][:, :],
                          xno[:, :], 2 * li, ln_done)
        with nc.sbuf_tensor(f"XN_{li}", [128, DT * TFULL], BF16) as XN, \
             nc.sbuf_tensor(f"a_wq_{li}", [128, 8 * 128], BF16) as WQ, \
             nc.sbuf_tensor(f"a_wk_{li}", [128, 8 * 128], BF16) as WK, \
             nc.sbuf_tensor(f"a_wv_{li}", [128, 8 * 128], BF16) as WV, \
             nc.sbuf_tensor(f"a_qf_{li}", [128, TFULL], BF16) as QF, \
             nc.sbuf_tensor(f"a_kf_{li}", [128, TFULL], BF16) as KF, \
             nc.sbuf_tensor(f"a_vt_{li}", [128, 32 * 130], BF16) as VT, \
             nc.sbuf_tensor(f"a_ohq_{li}", [128, 2 * 512], BF16) as OHQ, \
             nc.sbuf_tensor(f"a_pt_{li}", [128, 4 * 512], BF16) as PT4, \
             nc.sbuf_tensor(f"a_rc_{li}", [128, TFULL], BF16) as RC, \
             nc.sbuf_tensor(f"a_rs2_{li}", [128, TFULL], BF16) as RS2, \
             nc.sbuf_tensor(f"a_qs_{li}", [128, TFULL], BF16) as QS, \
             nc.sbuf_tensor(f"a_bh_{li}", [128, 2], F32) as BH, \
             nc.sbuf_tensor(f"a_t32_{li}", [128, 512], F32) as T32a, \
             nc.sbuf_tensor(f"a_r0_{li}", [1, 2 * 512], BF16) as R0:
            for nm, tt in (("att_wq", WQ), ("att_wk", WK), ("att_wv", WV),
                           ("att_qf", QF), ("att_kf", KF), ("att_vt", VT),
                           ("att_ohq", OHQ), ("att_pt", PT4),
                           ("att_rc", RC), ("att_rs2", RS2), ("att_qs", QS),
                           ("att_bh", BH), ("att_r0", R0), ("att_t32", T32a)):
                setattr(C, nm, tt)
            C.att_sqb = QS
            phase_barrier(C)
            C.P.op("sync", lambda sy: sy.dma_start(out=RC[:, :], in_=C.ropec[:, :]),
                   inc=(C.ch_sd, 16))
            C.P.op("sync", lambda sy: sy.dma_start(out=RS2[:, :], in_=C.ropes[:, :]),
                   inc=(C.ch_sd, 16))
            rope_ready = [(C.ch_sd, C.ch_sd.n)]
            def ropefn(C2, QF2, KF2, ready, do_k):
                r = emit_rope(C2, QF2, list(ready) + rope_ready)
                if do_k:
                    r2 = emit_rope(C2, KF2, list(ready) + rope_ready + r)
                    C2._l3_krope = r2
                return list(r) + list(C2._l3_krope)
            cfg = {
                "load_q": lambda hp: load_w_fm_slice(C, "L3_wq", WQ, D, 512, hp * 128, 128),
                "load_k": lambda hp: load_w_fm(C, "L3_wk", WK, D, 128,
                                               waits=[(C.ch_t, C.ch_t.n)]),
                "load_v": lambda hp: load_w_fm(C, "L3_wv", WV, D, 128,
                                               waits=[(C.ch_t, C.ch_t.n)]),
                "kv_shared": True,
                "rope": ropefn,
            }
            xn_ready = emit_xn_assemble(C, li, XN, C.ch_sd.n)
            oh_done, vfl = emit_attn_softmax_layer(C, li, l, XN, xn_ready,
                                                   C.scr_oh, cfg)
        with nc.sbuf_tensor(f"a_wo_{li}", [128, 8 * 1024], BF16) as WO, \
             nc.sbuf_tensor(f"a_ohs_{li}", [128, 8 * 512], BF16) as OHS, \
             nc.sbuf_tensor(f"a_bo_{li}", [128, 8], F32) as BO:
            C.att_ohs = OHS
            phase_barrier(C)
            wov = load_w_fm(C, "L3_wo", WO, D, D)
            bov = load_bias_fm(C, "L3_bo", BO, D)
            attn_x_done = emit_wo(C, li, l, C.scr_oh, list(oh_done) + [wov, bov],
                                  vfl, WO, wov, bias_tile=BO, x_waits=list(ln_done))
        with nc.sbuf_tensor(f"q_bm_{li}", [128, 176], F32) as QBM, \
             nc.sbuf_tensor(f"q_rs_{li}", [128, 176], F32) as QRS, \
             nc.sbuf_tensor(f"q_stg_{li}", [128, 2816], BF16) as QSTG, \
             nc.sbuf_tensor(f"q_qd_{li}", [128, 2816], BF16) as QQD, \
             nc.sbuf_tensor(f"q_a_{li}", [128, 2816], BF16) as QA, \
             nc.sbuf_tensor(f"q_q_{li}", [128, 2816], BF16) as QQ, \
             nc.sbuf_tensor(f"q_sg_{li}", [128, 2816], BF16) as QSG, \
             nc.sbuf_tensor(f"f3_xn_{li}", [128, 8, 128], BF16) as F3XN, \
             nc.sbuf_tensor(f"f3_xq_{li}", [128, 8 * 512], BF16) as F3XQ, \
             nc.sbuf_tensor(f"f3_ht_{li}", [128, 512], BF16) as F3HT, \
             nc.sbuf_tensor(f"f3_hq_{li}", [128, 512], BF16) as F3HQ, \
             nc.sbuf_tensor(f"f3_hf_{li}", [128, 128], BF16) as F3HF:
            for nm, tt in (("q_bm", QBM), ("q_rs", QRS), ("q_stg", QSTG),
                           ("q_qd", QQD), ("q_a", QA), ("q_q", QQ), ("q_sg", QSG),
                           ("f3_xn", F3XN), ("f3_xq", F3XQ), ("f3_ht", F3HT),
                           ("f3_hq", F3HQ), ("f3_hf", F3HF)):
                setattr(C, nm, tt)
            C.q_t = QQD
            phase_barrier(C)
            with nc.sbuf_tensor(f"f3_wgq_{li}", [128, 8 * 2816], BF16) as WGQ, \
                 nc.sbuf_tensor(f"f3_wuq_{li}", [128, 8 * 2816], BF16) as WUQ:
                hq_done = emit_ffn_l3_phase1(C, li, WGQ, WUQ, attn_x_done)
            phase_barrier(C)
            with nc.sbuf_tensor(f"f3_wdq_{li}", [128, 22 * 1024], BF16) as WDQ, \
                 nc.sbuf_tensor(f"f3_hqf_{li}", [128, 22 * 512], BF16) as HQF:
                C.f3_hqf = HQF
                x_done = emit_ffn_l3_phase2(C, li, WDQ, hq_done, attn_x_done)
        return x_done
    raise NotImplementedError(f"layer {l}")

def emit_ffn_l0_norm(C, XNo, xn_ready, NW):
    P = C.P
    NORM, RSQ = C.f0_norm, C.f0_rsq
    nwv = load_bias_fm(C, "L0_normw", NW, D)
    nv = None
    prevst = None
    for c in range(4):
        sv = None
        for j in range(DT):
            sv = P.op("scalar", lambda s, j=j, c=c: s.activation(
                out=RSQ[:, j * 512:(j + 1) * 512],
                in_=XNo[:, j * TOK + c * 512: j * TOK + (c + 1) * 512],
                func=AF.Square), waits=(list(xn_ready) + [nwv] if (c == 0 and j == 0) else ()),
                inc=(C.ch_s, 1))
        mmv = None
        for j in range(DT):
            kw = dict(inc=(C.ch_t, 1)) if j == DT - 1 else {}
            w0 = [(C.ch_s, sv)] + ([prevst] if prevst else [])
            mmv = P.op("tensor", lambda t, j=j: t.matmul(
                C.PB[0][0:1, 0:512], C.onescol_b_s[:, 0:1],
                RSQ[:, j * 512:(j + 1) * 512], start=(j == 0), stop=(j == DT - 1)),
                waits=(w0 if j == 0 else ()), **kw)
        P.op("vector", lambda v: v.tensor_scalar(C.ln_ms[0:1, :], C.PB[0][0:1, 0:512],
             1.0 / 1024.0, 1e-8, ALU.mult, ALU.add), waits=[(C.ch_t, mmv)])
        vv = P.op("vector", lambda v: v.reciprocal(C.ln_rv[0:1, :], C.ln_ms[0:1, :]),
                  inc=(C.ch_v, 1))
        sv2 = P.op("scalar", lambda s: s.activation(out=C.ln_rs[0:1, :],
              in_=C.ln_rv[0:1, :], func=AF.Sqrt), waits=[(C.ch_v, vv)], inc=(C.ch_s, 1))
        mmb = P.op("tensor", lambda t: t.matmul(C.PB[1][:, 0:512], C.onesrow_f_s[0:1, :],
              C.ln_rs[0:1, :], start=True, stop=True), waits=[(C.ch_s, sv2)],
              inc=(C.ch_t, 1))
        for j in range(DT):
            P.op("vector", lambda v, j=j, c=c: v.tensor_mul(C.ln_t2[:, :],
                 XNo[:, j * TOK + c * 512: j * TOK + (c + 1) * 512], C.PB[1][:, 0:512]),
                 waits=([(C.ch_t, mmb)] if j == 0 else ()))
            nv = P.op("vector", lambda v, j=j, c=c: v.tensor_scalar(
                 NORM[:, j * TOK + c * 512: j * TOK + (c + 1) * 512],
                 C.ln_t2[:, :], NW[:, j:j + 1], None, ALU.mult), inc=(C.ch_v, 1))
        prevst = (C.ch_v, nv)
    return [(C.ch_v, nv)]

def run_cores(C, in_maps, **kw):
    return run_bass_kernel_spmd(C.nc, in_maps, core_ids=list(range(8)), **kw)


def load_w_fm(C, name, dst, rows, cols, waits=(), slow=False):
    src = C.inp[name].rearrange("(kc p) e -> p kc e", p=128)
    v = C.P.op("gpsimd", lambda g: g.dma_start(
        out=dst.rearrange("p (kc e) -> p kc e", e=cols)[:, :, :], in_=src,
        allow_slow_non_contiguous=slow),
        waits=waits, inc=(C.ch_wd, 16))
    return (C.ch_wd, v)

def load_w_fm3(C, name, idx, dst, rows, cols, waits=()):
    src = C.inp[name][idx].rearrange("(kc p) e -> p kc e", p=128)
    v = C.P.op("gpsimd", lambda g: g.dma_start(
        out=dst.rearrange("p (kc e) -> p kc e", e=cols)[:, :, :], in_=src),
        waits=waits, inc=(C.ch_wd, 16))
    return (C.ch_wd, v)

def load_w_fm_slice(C, name, dst, rows, cols, c0, cw, waits=()):
    src = C.inp[name].rearrange("(kc p) e -> p kc e", p=128)[:, :, c0:c0 + cw]
    v = C.P.op("gpsimd", lambda g: g.dma_start(
        out=dst.rearrange("p (kc e) -> p kc e", e=cw)[:, :, :], in_=src),
        waits=list(waits) + [(C.ch_t, C.ch_t.n)], inc=(C.ch_wd, 16))
    return (C.ch_wd, v)

def load_bias_fm(C, name, dst, n, waits=(), col0=0):
    v = C.P.op("gpsimd", lambda g: g.dma_start(
        out=dst[:, col0:col0 + n // 128],
        in_=C.inp[name].rearrange("(j p) -> p j", p=128),
        allow_slow_non_contiguous=True), waits=waits, inc=(C.ch_wd, 16))
    return (C.ch_wd, v)


def XNv(C, XN, j):
    return XN.rearrange("p (j t) -> p j t", j=DT)[:, j, :]

def emit_proj_fm(C, XN, xn_ready, wbuf, wcols, wc0, wready, dst, bias=None,
                 dst_col0=0, nrows=128):
    P = C.P
    pend = C.pend45
    for tc in range(TFULL // 512):
        ps = C.PB[4 + (tc % 2)]
        w0 = list(xn_ready) + [wready]
        if pend.get(tc % 2):
            w0.append(pend[tc % 2])
        mmv = None
        for k in range(DT):
            kw = dict(inc=(C.ch_t, 1)) if k == DT - 1 else {}
            mmv = P.op("tensor", lambda t, k=k, tc=tc, ps=ps: t.matmul(
                ps[0:nrows, 0:512],
                wbuf[:, k * wcols + wc0: k * wcols + wc0 + nrows],
                XNv(C, XN, k)[:, tc * 512:(tc + 1) * 512],
                start=(k == 0), stop=(k == DT - 1)),
                waits=(w0 if k == 0 else ()), **kw)
        if bias is not None:
            ev = P.op("scalar", lambda s, tc=tc, ps=ps: s.activation(
                out=dst[0:nrows, dst_col0 + tc * 512: dst_col0 + (tc + 1) * 512],
                in_=ps[0:nrows, 0:512], func=AF.Identity, bias=bias, scale=1.0),
                inc=(C.ch_s, 1), waits=[(C.ch_t, mmv)])
        else:
            ev = P.op("scalar", lambda s, tc=tc, ps=ps: s.copy(
                out=dst[0:nrows, dst_col0 + tc * 512: dst_col0 + (tc + 1) * 512],
                in_=ps[0:nrows, 0:512]), inc=(C.ch_s, 1), waits=[(C.ch_t, mmv)])
        pend[tc % 2] = (C.ch_s, ev)
    return [(C.ch_s, C.ch_s.n)]

def emit_proj_tm(C, XN, xn_ready, wbuf, wcols, wc0, wready, dstv, ncols,
                 bias_tile=None, src_view=None):
    P = C.P
    pend = C.pend45
    for tt in range(TFULL // 128):
        ps = C.PB[4 + (tt % 2)]
        w0 = list(xn_ready) + [wready]
        if pend.get(tt % 2):
            w0.append(pend[tt % 2])
        mmv = None
        for k in range(DT):
            kw = dict(inc=(C.ch_t, 1)) if k == DT - 1 else {}
            mmv = P.op("tensor", lambda t, k=k, tt=tt, ps=ps: t.matmul(
                ps[:, 0:ncols],
                XNv(C, XN, k)[:, tt * 128:(tt + 1) * 128],
                wbuf[:, k * wcols + wc0: k * wcols + wc0 + ncols],
                start=(k == 0), stop=(k == DT - 1)),
                waits=(w0 if k == 0 else ()), **kw)
        sview = (lambda p_: p_[:, 0:ncols]) if src_view is None else src_view
        if bias_tile is not None:
            ev = P.op("vector", lambda v, tt=tt, ps=ps: v.tensor_add(
                dstv(tt), sview(ps), bias_tile),
                inc=(C.ch_v, 1), waits=[(C.ch_t, mmv)])
            pend[tt % 2] = (C.ch_v, ev)
        else:
            ev = P.op("scalar", lambda s, tt=tt, ps=ps: s.copy(
                out=dstv(tt), in_=sview(ps)), inc=(C.ch_s, 1),
                waits=[(C.ch_t, mmv)])
            pend[tt % 2] = (C.ch_s, ev)
    return [(C.ch_s, C.ch_s.n), (C.ch_v, C.ch_v.n)]

def emit_head_bounds(C, QF, KF, ready, bh, scale=0.125):
    P = C.P
    SQB = C.att_sqb
    NR = C.att_nr
    mx = {}
    for si, src in enumerate((QF, KF)):
        sqw = list(ready)
        if si == 1:
            sqw.append((C.ch_t, C.ch_t.n))
        sv = P.op("scalar", lambda s, src=src: s.activation(
            out=SQB[:, :], in_=src[:, :], func=AF.Square), inc=(C.ch_s, 1),
            waits=sqw)
        for h in range(2):
            prev = {(0, 1): (0, 0), (1, 0): (0, 1), (1, 1): (1, 0)}.get((si, h))
            redv = {}
            for tc in range(TFULL // 512):
                bk = tc % 4
                mmw = [(C.ch_s, sv)]
                if bk in redv:
                    mmw.append(redv[bk])
                elif prev is not None:
                    mmw.append(mx[prev])
                mmv = P.op("tensor", lambda t, tc=tc, h=h, bk=bk: t.matmul(
                    C.PB[bk][0:1, 0:512], C.sel2h_s[:, h:h + 1],
                    SQB[:, tc * 512:(tc + 1) * 512],
                    start=True, stop=True),
                    waits=mmw, inc=(C.ch_t, 1))
                rv = P.op("vector", lambda v, bk=bk: v.tensor_reduce(
                    NR[0:1, 12 + bk:13 + bk], C.PB[bk][0:1, 0:512],
                    axis=AX.X, op=ALU.max),
                    waits=[(C.ch_t, mmv)], inc=(C.ch_v, 1))
                redv[bk] = (C.ch_v, rv)
                if tc == 0:
                    vv = P.op("vector", lambda v, si=si, h=h, bk=bk: v.tensor_copy(
                        NR[0:1, si * 2 + h: si * 2 + h + 1], NR[0:1, 12 + bk:13 + bk]),
                        inc=(C.ch_v, 1))
                else:
                    vv = P.op("vector", lambda v, si=si, h=h, bk=bk: v.tensor_max(
                        NR[0:1, si * 2 + h: si * 2 + h + 1],
                        NR[0:1, si * 2 + h: si * 2 + h + 1],
                        NR[0:1, 12 + bk:13 + bk]),
                        inc=(C.ch_v, 1))
                mx[(si, h)] = (C.ch_v, vv)
    out_waits = []
    for h in range(2):
        vmu = P.op("vector", lambda v, h=h: v.tensor_mul(
            NR[0:1, 9:10], NR[0:1, 0 + h:1 + h], NR[0:1, 2 + h:3 + h]),
            waits=[mx[(0, h)], mx[(1, h)]], inc=(C.ch_v, 1))
        sv = P.op("scalar", lambda s: s.activation(
            out=NR[0:1, 10:11], in_=NR[0:1, 9:10], func=AF.Sqrt),
            waits=[(C.ch_v, vmu)], inc=(C.ch_s, 1))
        vv = P.op("vector", lambda v: v.tensor_scalar(
            NR[0:1, 11:12], NR[0:1, 10:11], -scale, None, ALU.mult),
            waits=[(C.ch_s, sv)], inc=(C.ch_v, 1))
        mmv = P.op("tensor", lambda t: t.matmul(
            C.PB[6][:, 0:1], C.onesrow_f_s[0:1, :], NR[0:1, 11:12],
            start=True, stop=True), waits=[(C.ch_v, vv)], inc=(C.ch_t, 1))
        ev = P.op("vector", lambda v, h=h: v.tensor_copy(bh[:, h:h + 1],
            C.PB[6][:, 0:1]), waits=[(C.ch_t, mmv)], inc=(C.ch_v, 1))
        out_waits.append((C.ch_v, ev))
    return out_waits

def emit_softmax_pass(C, li, l, QF, KF, VT, bh, ready, qt_sink, causal=False):
    P = C.P
    PT = C.att_pt
    pend_pt = {}
    pend_out = None
    for qt in range(TFULL // 512):
        kts = list(range((qt + 1) * 4)) if causal else list(range(32))
        mm_pv_last = None
        for ki, kt in enumerate(kts):
            b = ki % 2
            scps = (C.PB[2 * b], C.PB[2 * b + 1])
            w0 = list(ready)
            if b in pend_pt:
                w0.append(pend_pt[b])
            if ki == 0 and pend_out is not None:
                w0.append(pend_out)
            for h in range(2):
                base = h * 64
                kw = dict(inc=(C.ch_t, 1)) if h == 1 else {}
                mmsc = P.op("tensor", lambda t, h=h, base=base, kt=kt, qt=qt, ps=scps[h]: t.matmul(
                    ps[:, 0:512], KF[base:base + 64, kt * 128:(kt + 1) * 128],
                    QF[base:base + 64, qt * 512:(qt + 1) * 512],
                    start=True, stop=True,
                    tile_position=(base, 0)),
                    waits=(w0 if h == 0 else ()), **kw)
            ev = None
            for h in range(2):
                ev = P.op("scalar", lambda s, h=h, b=b, ps=scps[h]: s.activation(
                    out=PT[:, (2 * b + h) * 512:(2 * b + h + 1) * 512],
                    in_=ps[:, 0:512], func=AF.Exp, scale=0.125,
                    bias=bh[:, h:h + 1]), waits=[(C.ch_t, mmsc)], inc=(C.ch_s, 1))
            ptv = (C.ch_s, ev)
            if causal and kt >= qt * 4:
                dmi = kt - qt * 4
                for h in range(2):
                    ev2 = P.op("vector", lambda v, h=h, b=b, dmi=dmi: v.tensor_mul(
                        PT[:, (2 * b + h) * 512:(2 * b + h + 1) * 512],
                        PT[:, (2 * b + h) * 512:(2 * b + h + 1) * 512],
                        C.diagm_s[:, dmi * 512:(dmi + 1) * 512]),
                        waits=[ptv], inc=(C.ch_v, 1))
                ptv = (C.ch_v, ev2)
            for h in range(2):
                ops = C.PB[4 + h]
                kw = dict(inc=(C.ch_t, 1)) if h == 1 else {}
                mm_pv_last = P.op("tensor", lambda t, h=h, b=b, kt=kt, ops=ops,
                                  first=(ki == 0), last=(ki == len(kts) - 1): t.matmul(
                    ops[0:65, 0:512], VT[:, kt * 130 + h * 65: kt * 130 + (h + 1) * 65],
                    PT[:, (2 * b + h) * 512:(2 * b + h + 1) * 512],
                    start=first, stop=last),
                    waits=([ptv] if h == 0 else ()), **kw)
            pend_pt[b] = (C.ch_t, mm_pv_last)
        pend_out = qt_sink(qt, C.PB[4], C.PB[5], [(C.ch_t, mm_pv_last)])
    return pend_out

def emit_attn_softmax_layer(C, li, l, XN, xn_ready, scr_oh, cfg):
    P = C.P
    WQ = C.att_wq; WK = C.att_wk; WV = C.att_wv
    QF = C.att_qf; KF = C.att_kf; VT = C.att_vt
    OHQ = C.att_ohq
    bh = C.att_bh
    R0 = C.att_r0
    msv = P.op("gpsimd", lambda g: g.memset(VT[:, :], 1.0), inc=(C.ch_g, 1))
    oh_done = []
    for hp in range(NHP):
        xn_ready2 = list(xn_ready) + [(C.ch_t, C.ch_t.n), (C.ch_s, C.ch_s.n),
                                      (C.ch_v, C.ch_v.n), (C.ch_g, msv)]
        wvq = cfg["load_q"](hp)
        pq = emit_proj_fm(C, XN, xn_ready2, WQ, 128, 0, wvq, QF)
        if cfg.get("kv_shared") and hp > 0:
            pk, pv = C._l3_pk, C._l3_pv
        else:
            wvk = cfg["load_k"](hp)
            wvv = cfg["load_v"](hp)
            pk = emit_proj_fm(C, XN, xn_ready2, WK, 128, 0, wvk, KF)
            def vdst(tt):
                return VT.rearrange("p (kt hb e) -> p kt hb e", hb=2, e=65)[:, tt, :, 0:64]
            pv = emit_proj_tm(C, XN, xn_ready2, WV, 128, 0, wvv,
                              vdst, 128,
                              src_view=lambda p_: p_[:, 0:128].rearrange(
                                  "p (hb e) -> p hb e", e=64))
            if cfg.get("kv_shared"):
                C._l3_pk, C._l3_pv = pk, pv
        ready = pq + pk + pv
        if cfg.get("rope"):
            ready = cfg["rope"](C, QF, KF, ready, do_k=(hp == 0))
        passes = cfg.get("passes", 1)
        oh_slot_pend = [None, None]
        for pi in range(passes):
            if pi == 1:
                wk2 = cfg["load_k2"](hp)
                pk2 = emit_proj_fm(C, XN, list(xn_ready2) + [(C.ch_t, C.ch_t.n)],
                                   WK, 128, 0, wk2, KF)
                ready = pq + pk2
            bw = emit_head_bounds(C, QF, KF, ready, bh)
            ready2 = list(ready) + bw

            def qt_sink(qt, ps0, ps1, mmw, pi=pi, hp=hp):
                vv = None
                for h, ps in ((0, ps0), (1, ps1)):
                    vv = P.op("vector", lambda v, h=h, ps=ps: v.reciprocal(
                        R0[0:1, h * 512:(h + 1) * 512], ps[64:65, 0:512]),
                        waits=mmw, inc=(C.ch_v, 1))
                slot = qt % 2
                out_free = []
                prev_ev = None
                for h, ps in ((0, ps0), (1, ps1)):
                    bw2 = [(C.ch_v, vv)] + ([prev_ev] if prev_ev else [])
                    if pi == passes - 1 and h == 0 and oh_slot_pend[slot] is not None:
                        bw2.append(oh_slot_pend[slot])
                    mmb = P.op("tensor", lambda t, h=h: t.matmul(
                        C.PB[6][0:64, 0:512], C.onesrow_b_s[0:1, 0:64],
                        R0[0:1, h * 512:(h + 1) * 512], start=True, stop=True),
                        waits=bw2, inc=(C.ch_t, 1))
                    if passes == 1:
                        b2 = h * 64
                        sc = P.op("scalar", lambda s, ps=ps, b2=b2: s.copy(
                            out=C.att_t32[b2:b2 + 64, 0:512], in_=ps[0:64, 0:512]),
                            waits=[(C.ch_t, mmb), (C.ch_v, C.ch_v.n)], inc=(C.ch_s, 1))
                        ev = P.op("vector", lambda v, h=h, slot=slot, b2=b2: v.tensor_mul(
                            OHQ[h * 64:(h + 1) * 64, slot * 512:(slot + 1) * 512],
                            C.att_t32[b2:b2 + 64, 0:512], C.PB[6][0:64, 0:512]),
                            waits=[(C.ch_s, sc)], inc=(C.ch_v, 1))
                    elif pi == 0:
                        b2 = h * 64
                        sc0 = P.op("scalar", lambda s, ps=ps, b2=b2: s.copy(
                            out=C.att_t32[b2:b2 + 64, 0:512], in_=ps[0:64, 0:512]),
                            waits=[(C.ch_t, mmb), (C.ch_v, C.ch_v.n)], inc=(C.ch_s, 1))
                        P.op("vector", lambda v, b2=b2: v.tensor_mul(
                            C.att_t32[b2:b2 + 64, 0:512], C.att_t32[b2:b2 + 64, 0:512],
                            C.PB[6][0:64, 0:512]), waits=[(C.ch_s, sc0)])
                        ev = P.op("vector", lambda v, h=h, qt=qt, b2=b2: v.tensor_scalar(
                            C.att_o32[h * 64:(h + 1) * 64, qt * 512:(qt + 1) * 512],
                            C.att_t32[b2:b2 + 64, 0:512], C.att_alb[b2:b2 + 64, 0:1], None,
                            ALU.mult), inc=(C.ch_v, 1))
                    else:
                        b2 = h * 64
                        sc0 = P.op("scalar", lambda s, ps=ps, b2=b2: s.copy(
                            out=C.att_t32[b2:b2 + 64, 0:512], in_=ps[0:64, 0:512]),
                            waits=[(C.ch_t, mmb), (C.ch_v, C.ch_v.n)], inc=(C.ch_s, 1))
                        P.op("vector", lambda v, b2=b2: v.tensor_mul(
                            C.att_t32[b2:b2 + 64, 0:512], C.att_t32[b2:b2 + 64, 0:512],
                            C.PB[6][0:64, 0:512]), waits=[(C.ch_s, sc0)])
                        P.op("vector", lambda v, b2=b2: v.tensor_scalar(
                            C.att_t32[b2:b2 + 64, 0:512], C.att_t32[b2:b2 + 64, 0:512],
                            C.att_alb[b2:b2 + 64, 1:2], None, ALU.mult))
                        ev = P.op("vector", lambda v, h=h, qt=qt, slot=slot, b2=b2: v.tensor_add(
                            OHQ[h * 64:(h + 1) * 64, slot * 512:(slot + 1) * 512],
                            C.att_t32[b2:b2 + 64, 0:512],
                            C.att_o32[h * 64:(h + 1) * 64, qt * 512:(qt + 1) * 512]),
                            inc=(C.ch_v, 1))
                    out_free.append((C.ch_v, ev))
                    prev_ev = (C.ch_v, ev)
                if pi == passes - 1:
                    dv = P.op("sync", lambda sy, hp=hp, qt=qt, slot=slot: sy.dma_start(
                        out=scr_oh[hp][:, qt * 512:(qt + 1) * 512],
                        in_=OHQ[:, slot * 512:(slot + 1) * 512]),
                        waits=[prev_ev], inc=(C.ch_sd, 16))
                    oh_slot_pend[slot] = (C.ch_sd, C.ch_sd.n)
                return out_free[-1]

            emit_softmax_pass(C, li, l, QF, KF, VT, bh, ready2, qt_sink,
                              causal=cfg.get("causal", False))
        oh_done.append((C.ch_sd, C.ch_sd.n))
    vfl = emit_oh_ship(C, li, scr_oh, oh_done)
    return oh_done, vfl

def emit_wo(C, li, l, scr_oh, oh_waits, vfl, WO, wo_ready, bias_tile=None,
            x_waits=()):
    P = C.P
    OHS = C.att_ohs
    done = []
    pend = {}
    for tc in range(TOK // 512):
        v0 = C.ch_sd.bump(32)
        def emit_ld(sy, tc=tc, v0=v0):
            h = P.holder
            for myp in (0, 1):
                guard = sy.If_eq(h["par"], 0) if myp == 0 else sy.Else()
                with guard:
                    sy.dma_start(
                        out=OHS.rearrange("p (b e) -> p b e", e=512)[:, 0:4, :],
                        in_=scr_oh[:, :, myp * TOK + tc * 512: myp * TOK + (tc + 1) * 512].rearrange(
                            "h p e -> p h e"),
                    ).then_inc(C.ch_sd.h, 16)
                    sy.dma_start(
                        out=OHS.rearrange("p (b e) -> p b e", e=512)[:, 4:8, :],
                        in_=C.oh_sh[1 - myp, li][:, :, tc * 512:(tc + 1) * 512].rearrange(
                            "h p e -> p h e"),
                    ).then_inc(C.ch_sd.h, 16)
            sy.end_ifs()
            return sy.nop()
        if tc == 0:
            vpoll = sh_poll_read(C, 2 * li + 1, [], extra_waits=[])
        P.op("sync", emit_ld, waits=list(oh_waits) + list(pend.get(("ld", tc - 1), ())))
        ldv = (C.ch_sd, v0)
        for m in range(DT):
            ps = C.PB[m % 4]
            w0 = [ldv, wo_ready] + list(x_waits)
            if ("ps", m % 4) in pend:
                w0.append(pend[("ps", m % 4)])
            mmv = None
            for ct in range(8):
                kw = dict(inc=(C.ch_t, 1)) if ct == 7 else {}
                mmv = P.op("tensor", lambda t, m=m, ct=ct, ps=ps: t.matmul(
                    ps[:, 0:512], WO[:, ct * 1024 + m * 128:(ct * 1024 + m * 128) + 128],
                    OHS[:, ct * 512:(ct + 1) * 512],
                    start=(ct == 0), stop=(ct == 7)),
                    waits=(w0 if ct == 0 else ()), **kw)
            if bias_tile is not None:
                ev = P.op("vector", lambda v, m=m, tc=tc, ps=ps: v.scalar_tensor_tensor(
                    C.X[:, m * TOK + tc * 512: m * TOK + (tc + 1) * 512],
                    ps[:, 0:512], bias_tile[:, m:m + 1],
                    C.X[:, m * TOK + tc * 512: m * TOK + (tc + 1) * 512],
                    ALU.add, ALU.add), waits=[(C.ch_t, mmv)], inc=(C.ch_v, 1))
            else:
                ev = P.op("vector", lambda v, m=m, tc=tc, ps=ps: v.tensor_add(
                    C.X[:, m * TOK + tc * 512: m * TOK + (tc + 1) * 512],
                    C.X[:, m * TOK + tc * 512: m * TOK + (tc + 1) * 512],
                    ps[:, 0:512]), waits=[(C.ch_t, mmv)], inc=(C.ch_v, 1))
            pend[("ps", m % 4)] = (C.ch_v, ev)
            done.append((C.ch_v, ev))
        pend[("ld", tc)] = [(C.ch_t, C.ch_t.n)]
    return [done[-1]]


def emit_attn_l0(C, li, XN, xn_ready, scr_oh):
    P = C.P
    WQ, WK, WV = C.att_wq, C.att_wk, C.att_wv
    QF, KF, VT, KT = C.att_qf, C.att_kf, C.att_vt, C.att_kt
    OHQ = C.att_ohq
    SS = C.att_ss
    SSF = C.att_ssf
    PTI = C.att_pti
    BQ = C.att_bq
    BKB = C.att_bkb
    wvq = load_w_fm(C, "L0_wqT", WQ, D, 512)
    wvk = load_w_fm(C, "L0_wkT", WK, D, 512)
    wvv = load_w_fm(C, "L0_wvT", WV, D, 512)
    bq = load_bias_fm(C, "L0_wqb", BQ, 512, col0=0)
    bk = load_bias_fm(C, "L0_wkb", BQ, 512, col0=4)
    bv = load_bias_fm(C, "L0_wvb", BQ, 512, col0=8)
    oh_done = []
    for hp in range(NHP):
        xn_ready = list(xn_ready) + [(C.ch_t, C.ch_t.n), (C.ch_s, C.ch_s.n),
                                     (C.ch_v, C.ch_v.n)]
        BR = C.att_br
        v1 = P.op("gpsimd", lambda g, hp=hp: g.dma_start(
            out=BR[0:1, 0:128], in_=C.inp["L0_wkb"][hp * 128:(hp + 1) * 128].rearrange("(a e) -> a e", a=1)),
            inc=(C.ch_wd, 16))
        mmk = P.op("tensor", lambda t: t.matmul(C.PB[6][:, 0:128], C.onesrow_b_s[0:1, :],
                   BR[0:1, 0:128], start=True, stop=True),
                   waits=[(C.ch_wd, v1)] + list(C.const_ready) + list(xn_ready),
                   inc=(C.ch_t, 1))
        vk1 = P.op("vector", lambda v: v.tensor_copy(BKB[:, 0:128], C.PB[6][:, 0:128]),
                   waits=[(C.ch_t, mmk)], inc=(C.ch_v, 1))
        v2 = P.op("gpsimd", lambda g, hp=hp: g.dma_start(
            out=BR[0:1, 0:128], in_=C.inp["L0_wvb"][hp * 128:(hp + 1) * 128].rearrange("(a e) -> a e", a=1)),
            waits=[(C.ch_t, mmk)], inc=(C.ch_wd, 16))
        mmv2 = P.op("tensor", lambda t: t.matmul(C.PB[6][:, 0:128], C.onesrow_b_s[0:1, :],
                    BR[0:1, 0:128], start=True, stop=True),
                    waits=[(C.ch_v, vk1), (C.ch_wd, v2)], inc=(C.ch_t, 1))
        vk2 = P.op("vector", lambda v: v.tensor_copy(BKB[:, 128:256], C.PB[6][:, 0:128]),
                   waits=[(C.ch_t, mmv2)], inc=(C.ch_v, 1))
        bias_ready = [(C.ch_v, vk2)]
        pq = emit_proj_fm(C, XN, list(xn_ready) + [bq], WQ, 512, hp * 128, wvq, QF,
                          bias=BQ[:, hp:hp + 1])
        pk = emit_proj_fm(C, XN, list(xn_ready) + [bk, bv], WK, 512, hp * 128, wvk, KF,
                          bias=BQ[:, 4 + hp:5 + hp])
        def ktdst(tt):
            return KT[:, tt * 128:(tt + 1) * 128]
        pkt = emit_proj_tm(C, XN, list(xn_ready) + bias_ready, WK, 512, hp * 128,
                           wvk, ktdst, 128, bias_tile=BKB[:, 0:128])
        def vdst(tt):
            return VT.rearrange("p (kt hb e) -> p kt hb e", hb=2, e=65)[:, tt, :, 0:64]
        pv = emit_proj_tm(C, XN, list(xn_ready) + bias_ready, WV, 512, hp * 128,
                          wvv, vdst, 128,
                          bias_tile=BKB[:, 128:256].rearrange("p (hb e) -> p hb e", e=64),
                          src_view=lambda p_: p_[:, 0:128].rearrange("p (hb e) -> p hb e", e=64))
        ready = pq + pk + pkt + pv
        P.op("gpsimd", lambda g: g.memset(SS[:, :], 0.0), waits=ready,
             inc=(C.ch_g, 1))
        gz = P.op("gpsimd", lambda g: g.memset(SSF[:, :], 0.0), waits=ready,
                  inc=(C.ch_g, 1))
        snapv = (C.ch_g, gz)
        pend_out = None
        pend_pti = None
        supd_v = None
        oh_slot_pend = [None, None]
        for ct in range(TFULL // 128):
            w0 = list(ready) + [snapv]
            if pend_out is not None:
                w0.append(pend_out)
            for h in range(2):
                base = h * 64
                P.op("tensor", lambda t, h=h, base=base, ct=ct: t.matmul(
                    C.PB[4 + h][0:64, 0:128], SS[base:base + 64, 0:64],
                    QF[base:base + 64, ct * 128:(ct + 1) * 128],
                    start=True, stop=False, tile_position=(base, 0)),
                    waits=(w0 if h == 0 else ()))
            mmsc = None
            for h in range(2):
                base = h * 64
                kw = dict(inc=(C.ch_t, 1)) if h == 1 else {}
                scw = [pend_pti] if (pend_pti is not None and h == 0) else ()
                mmsc = P.op("tensor", lambda t, h=h, base=base, ct=ct: t.matmul(
                    C.PB[h][:, 0:128], KF[base:base + 64, ct * 128:(ct + 1) * 128],
                    QF[base:base + 64, ct * 128:(ct + 1) * 128],
                    start=True, stop=True, tile_position=(base, 0)),
                    waits=scw, **kw)
            ev = None
            for h in range(2):
                ev = P.op("vector", lambda v, h=h: v.tensor_mul(
                    PTI[:, h * 128:(h + 1) * 128], C.PB[h][:, 0:128],
                    C.trilT_s[:, :]), waits=[(C.ch_t, mmsc)], inc=(C.ch_v, 1))
            mmpv = None
            for h in range(2):
                kw = dict(inc=(C.ch_t, 1)) if h == 1 else {}
                mmpv = P.op("tensor", lambda t, h=h, ct=ct: t.matmul(
                    C.PB[4 + h][0:64, 0:128],
                    VT[:, ct * 130 + h * 65: ct * 130 + h * 65 + 64],
                    PTI[:, h * 128:(h + 1) * 128],
                    start=False, stop=True),
                    waits=([(C.ch_v, ev)] if h == 0 else ()), **kw)
            pend_pti = (C.ch_t, mmpv)
            w1 = [snapv]
            supd = None
            for h in range(2):
                kw = dict(inc=(C.ch_t, 1)) if h == 1 else {}
                supd = P.op("tensor", lambda t, h=h, ct=ct: t.matmul(
                    C.PB[2 + h][0:64, 0:64],
                    KT[:, ct * 128 + h * 64: ct * 128 + (h + 1) * 64],
                    VT[:, ct * 130 + h * 65: ct * 130 + h * 65 + 64],
                    start=True, stop=True),
                    waits=(w1 if h == 0 else ()), **kw)
            supd_v = (C.ch_t, supd)
            P.op("vector", lambda v: v.tensor_add(SSF[0:64, :], SSF[0:64, :],
                 C.PB[2][0:64, 0:64]), waits=[supd_v])
            P.op("vector", lambda v: v.tensor_add(SSF[64:128, :], SSF[64:128, :],
                 C.PB[3][0:64, 0:64]))
            gz2 = P.op("vector", lambda v: v.tensor_copy(SS[:, :], SSF[:, :]),
                       inc=(C.ch_v, 1))
            snapv = (C.ch_v, gz2)
            slot = ct % 2
            ev2 = None
            w_ev = [(C.ch_t, mmpv)]
            if oh_slot_pend[slot] is not None:
                w_ev.append(oh_slot_pend[slot])
            for h in range(2):
                ev2 = P.op("scalar", lambda s, h=h, slot=slot: s.copy(
                    out=OHQ[h * 64:(h + 1) * 64, slot * 128:(slot + 1) * 128],
                    in_=C.PB[4 + h][0:64, 0:128]), waits=(w_ev if h == 0 else ()),
                    inc=(C.ch_s, 1))
            pend_out = (C.ch_s, ev2)
            dv = P.op("sync", lambda sy, hp=hp, ct=ct, slot=slot: sy.dma_start(
                out=scr_oh[hp][:, ct * 128:(ct + 1) * 128],
                in_=OHQ[:, slot * 128:(slot + 1) * 128]),
                waits=[pend_out], inc=(C.ch_sd, 16))
            oh_slot_pend[slot] = (C.ch_sd, C.ch_sd.n)
        oh_done.append((C.ch_sd, C.ch_sd.n))
    vfl = emit_oh_ship(C, li, scr_oh, oh_done)
    return oh_done, vfl

def emit_oh_ship(C, li, scr_oh, oh_done):
    P = C.P
    vsh = C.ch_sd.bump(16)
    vfl = C.ch_sd.bump(16)
    def emit_ship(sy):
        h = P.holder
        for myp in (0, 1):
            guard = sy.If_eq(h["par"], 0) if myp == 0 else sy.Else()
            with guard:
                sy.dma_start(
                    out=C.oh_sh[myp, li][:, :, :],
                    in_=scr_oh[:, :, (1 - myp) * TOK:(2 - myp) * TOK],
                ).then_inc(C.ch_sd.h, 16)
        sy.end_ifs()
        sy.wait_ge(C.ch_sd.h, vsh)
        for myp in (0, 1):
            guard = sy.If_eq(h["par"], 0) if myp == 0 else sy.Else()
            with guard:
                sy.dma_start(out=C.flg[myp, 2 * li + 1, 0:2],
                             in_=C.flagv[0:1, 2 * (2 * li + 1):2 * (2 * li + 1) + 2]
                             ).then_inc(C.ch_sd.h, 16)
        sy.end_ifs()
        return sy.nop()
    P.op("sync", emit_ship, waits=oh_done)
    return vfl


from concourse.ap import AP as _AP

def bc0(ap2d, inner):
    assert len(ap2d.ap) == 2
    return _AP(tensor=ap2d.tensor, offset=ap2d.offset,
               ap=[[ap2d.ap[0][0], ap2d.ap[0][1]],
                   [ap2d.ap[1][0], ap2d.ap[1][1]], [0, inner]])

def emit_top2(C, PRB, GT, tlist, renorm_eps=None, probs_ready=()):
    P = C.P
    E = (PRB.shape[1] if hasattr(PRB, "shape") else None)
    last = None
    for ti, E_ in tlist:
        s = C.ffn_srow
        sl = slice(ti * E_, (ti + 1) * E_)
        w0 = list(probs_ready)
        P.op("vector", lambda v, sl=sl: v.tensor_reduce(
            s[:, 0:1], PRB[:, sl], axis=AX.X, op=ALU.max), waits=w0)
        P.op("vector", lambda v, sl=sl: v.tensor_scalar(
            C.ffn_trow[:, 0:E_], PRB[:, sl], s[:, 0:1], -1e30, ALU.is_ge, ALU.mult))
        P.op("vector", lambda v, sl=sl: v.tensor_add(
            C.ffn_trow[:, 0:E_], C.ffn_trow[:, 0:E_], PRB[:, sl]))
        P.op("vector", lambda v: v.tensor_reduce(
            s[:, 1:2], C.ffn_trow[:, 0:E_], axis=AX.X, op=ALU.max))
        P.op("vector", lambda v, sl=sl: v.tensor_scalar(
            C.ffn_trow[:, 0:E_], PRB[:, sl], s[:, 1:2], None, ALU.is_ge))
        vv = P.op("vector", lambda v, sl=sl: v.tensor_mul(
            C.ffn_trow[:, 0:E_], C.ffn_trow[:, 0:E_], PRB[:, sl]), inc=(C.ch_v, 1))
        if renorm_eps is not None:
            P.op("vector", lambda v: v.tensor_reduce(
                s[:, 2:3], C.ffn_trow[:, 0:E_], axis=AX.X, op=ALU.add))
            if renorm_eps != 0.0:
                P.op("vector", lambda v: v.tensor_scalar(
                    s[:, 2:3], s[:, 2:3], renorm_eps, None, ALU.add))
            P.op("vector", lambda v: v.reciprocal(s[:, 2:3], s[:, 2:3]))
            P.op("vector", lambda v: v.tensor_scalar(
                C.ffn_trow[:, 0:E_], C.ffn_trow[:, 0:E_], s[:, 2:3], None, ALU.mult))
        last = P.op("vector", lambda v, sl=sl: v.tensor_copy(GT[:, sl],
            C.ffn_trow[:, 0:E_]), inc=(C.ch_v, 1))
    return [(C.ch_v, last)]

def emit_gate_transpose(C, GT, GF, nt, E, ready):
    P = C.P
    last = None
    pend = None
    for ti in range(nt):
        w0 = list(ready)
        if pend is not None:
            w0.append(pend)
        mmv = P.op("tensor", lambda t, ti=ti: t.transpose(
            C.PT[0:E, 0:128], GT[:, ti * E:(ti + 1) * E], C.identb_s[:, :]),
            waits=w0, inc=(C.ch_t, 1))
        last = P.op("vector", lambda v, ti=ti: v.tensor_copy(
            GF[0:E, ti * 128:(ti + 1) * 128], C.PT[0:E, 0:128]),
            waits=[(C.ch_t, mmv)], inc=(C.ch_v, 1))
        pend = (C.ch_v, last)
    return [(C.ch_v, last)]


def emit_ffn_l0(C, norm_ready, x_waits):
    P = C.P
    NORM = C.f0_norm
    WB1 = C.f0_wb1
    WB2 = C.f0_wb2
    HT = C.f0_ht
    PRB = C.f0_prb
    GT = C.f0_gt
    GF = C.f0_gf
    EB2 = C.f0_eb2
    SB1 = C.f0_sb1
    SB2 = C.f0_sb2
    EB1 = C.f0_eb1
    RW = C.f0_rw
    rwv = load_w_fm(C, "L0_routerT", RW, D, 8)
    sb1v = load_bias_fm(C, "L0_sb1", SB1, 4096)
    sb2v = load_bias_fm(C, "L0_sb2", SB2, D)
    eb2v = P.op("gpsimd", lambda g: g.dma_start(out=EB2[0:8, :], in_=C.inp["L0_eb2"][:, :]),
                inc=(C.ch_wd, 16))
    pv = None
    for tt in range(TOK // 128):
        mmv = None
        for k in range(DT):
            kw = dict(inc=(C.ch_t, 1)) if k == DT - 1 else {}
            w0 = list(norm_ready) + [rwv] + ([(C.ch_s, pv)] if pv else [])
            mmv = P.op("tensor", lambda t, k=k, tt=tt: t.matmul(
                C.PB[2][:, 0:8],
                NORM.rearrange("p (j t) -> p j t", j=DT)[:, k, tt * 128:(tt + 1) * 128],
                RW[:, k * 8:(k + 1) * 8], start=(k == 0), stop=(k == DT - 1)),
                waits=(w0 if k == 0 else ()), **kw)
        pv = P.op("scalar", lambda s, tt=tt: s.activation(
            out=PRB[:, tt * 8:(tt + 1) * 8], in_=C.PB[2][:, 0:8], func=AF.Sigmoid),
            waits=[(C.ch_t, mmv)], inc=(C.ch_s, 1))
    g2 = emit_top2(C, PRB, GT, [(ti, 8) for ti in range(16)], renorm_eps=1e-6,
                   probs_ready=[(C.ch_s, pv)])
    gf = emit_gate_transpose(C, GT, GF, 16, 8, g2)
    if C.dbg is not None and C.dbg[0] == "g":
        dv = emit_dbg_dump(C, GF[0:8, :], gf)
        finish_program(C, dv)
        return gf
    def half_mlp(w1name, w1idx, w2name, w2idx, e_or_none, Hh, first_sh):
        wfree = [(C.ch_t, C.ch_t.n), (C.ch_s, C.ch_s.n), (C.ch_v, C.ch_v.n)]
        def ld(name, idx, dst, rows, cols, c0, cw):
            if idx is None:
                src = C.inp[name].rearrange("(kc p) e -> p kc e", p=128)[:, :, c0:c0 + cw]
            else:
                src = C.inp[name][idx].rearrange("(kc p) e -> p kc e", p=128)[:, :, c0:c0 + cw]
            v = P.op("gpsimd", lambda g: g.dma_start(
                out=dst.rearrange("p (kc e) -> p kc e", e=cw)[:, :, :], in_=src),
                waits=wfree, inc=(C.ch_wd, 16))
            return (C.ch_wd, v)
        wv1 = ld(w1name, w1idx, WB1, D, 4096, Hh * 2048, 2048)
        if w2idx is None:
            src2 = C.inp[w2name].rearrange("(kc p) e -> p kc e", p=128)[:, Hh * 16:(Hh + 1) * 16, :]
        else:
            src2 = C.inp[w2name][w2idx].rearrange("(kc p) e -> p kc e", p=128)[:, Hh * 16:(Hh + 1) * 16, :]
        wv2v = P.op("gpsimd", lambda g: g.dma_start(
            out=WB2.rearrange("p (kc e) -> p kc e", e=1024)[:, :, :], in_=src2),
            waits=wfree, inc=(C.ch_wd, 16))
        wv2 = (C.ch_wd, wv2v)
        if e_or_none is not None:
            ev1 = P.op("gpsimd", lambda g, e=e_or_none, Hh=Hh: g.dma_start(
                out=EB1[:, 0:16],
                in_=C.inp["L0_eb1"][e, Hh * 2048:(Hh + 1) * 2048].rearrange("(j p) -> p j", p=128),
                allow_slow_non_contiguous=True), waits=wfree, inc=(C.ch_wd, 16))
            b1w = (C.ch_wd, ev1)
        pend_o = {}
        last_hv = None
        for tc in range(4):
            if e_or_none is not None:
                gbw = list(gf) + ([last_hv] if last_hv else []) + wfree
                gb = P.op("tensor", lambda t, e=e_or_none, tc=tc: t.matmul(
                    C.PB[6][:, 0:512], C.sel8_s[0:8, e * 128:(e + 1) * 128],
                    GF[0:8, tc * 512:(tc + 1) * 512],
                    start=True, stop=True), waits=gbw, inc=(C.ch_t, 1))
            hv = None
            for mH in range(16):
                ps = C.PB[2 + (mH % 2)]
                w0 = norm_ready + [wv1] + ([b1w] if e_or_none is not None else [sb1v])
                if (mH % 2) in pend_o:
                    w0.append(pend_o[mH % 2])
                mmv = None
                for k in range(DT):
                    kw = dict(inc=(C.ch_t, 1)) if k == DT - 1 else {}
                    mmv = P.op("tensor", lambda t, k=k, mH=mH, tc=tc, ps=ps: t.matmul(
                        ps[:, 0:512], WB1[:, k * 2048 + mH * 128: k * 2048 + (mH + 1) * 128],
                        NORM.rearrange("p (j t) -> p j t", j=DT)[:, k, tc * 512:(tc + 1) * 512],
                        start=(k == 0), stop=(k == DT - 1)),
                        waits=(w0 if k == 0 else ()), **kw)
                bias_ap = (EB1[:, mH:mH + 1] if e_or_none is not None
                           else SB1[:, Hh * 16 + mH: Hh * 16 + mH + 1])
                sv = P.op("scalar", lambda s, mH=mH, ps=ps, bias_ap=bias_ap: s.activation(
                    out=HT[:, mH * 512:(mH + 1) * 512], in_=ps[:, 0:512],
                    func=AF.Relu, bias=bias_ap, scale=1.0),
                    waits=[(C.ch_t, mmv)], inc=(C.ch_s, 1))
                if e_or_none is not None:
                    hv = P.op("vector", lambda v, mH=mH: v.tensor_mul(
                        HT[:, mH * 512:(mH + 1) * 512], HT[:, mH * 512:(mH + 1) * 512],
                        C.PB[6][:, 0:512]), waits=[(C.ch_s, sv), (C.ch_t, gb)],
                        inc=(C.ch_v, 1))
                    pend_o[mH % 2] = (C.ch_v, hv)
                    last_hv = (C.ch_v, hv)
                else:
                    pend_o[mH % 2] = (C.ch_s, sv)
            h_ready = pend_o[0], pend_o[1]
            for mO in range(DT):
                ps = C.PB[4 + (mO % 2)]
                w0 = [wv2, h_ready[0], h_ready[1]] + list(x_waits)
                if ("o", mO % 2) in pend_o:
                    w0.append(pend_o[("o", mO % 2)])
                mmv = None
                for k in range(16):
                    kw = dict(inc=(C.ch_t, 1)) if k == 15 else {}
                    mmv = P.op("tensor", lambda t, k=k, mO=mO, ps=ps: t.matmul(
                        ps[:, 0:512], WB2[:, k * 1024 + mO * 128: k * 1024 + (mO + 1) * 128],
                        HT[:, k * 512:(k + 1) * 512],
                        start=(k == 0), stop=(k == 15)),
                        waits=(w0 if k == 0 else ()), **kw)
                if e_or_none is None and first_sh:
                    ev = P.op("vector", lambda v, mO=mO, tc=tc, ps=ps: v.scalar_tensor_tensor(
                        C.X[:, mO * TOK + tc * 512: mO * TOK + (tc + 1) * 512],
                        ps[:, 0:512], SB2[:, mO:mO + 1],
                        C.X[:, mO * TOK + tc * 512: mO * TOK + (tc + 1) * 512],
                        ALU.add, ALU.add), waits=[(C.ch_t, mmv)],
                        inc=(C.ch_v, 1))
                else:
                    ev = P.op("vector", lambda v, mO=mO, tc=tc, ps=ps: v.tensor_add(
                        C.X[:, mO * TOK + tc * 512: mO * TOK + (tc + 1) * 512],
                        C.X[:, mO * TOK + tc * 512: mO * TOK + (tc + 1) * 512],
                        ps[:, 0:512]), waits=[(C.ch_t, mmv)], inc=(C.ch_v, 1))
                pend_o[("o", mO % 2)] = (C.ch_v, ev)
        return (C.ch_v, C.ch_v.n)
    lastv = None
    for Hh in range(2):
        lastv = half_mlp("L0_sw1", None, "L0_sw2", None, None, Hh, Hh == 0)
    if C.dbg is not None and C.dbg[0] == "sh":
        return [(C.ch_v, C.ch_v.n)]
    for e in range(8):
        for Hh in range(2):
            lastv = half_mlp("L0_ew1", e, "L0_ew2", e, e, Hh, False)
    pend = {}
    for mO in range(DT):
        for tc in range(4):
            ps = C.PB[2 + ((mO * 4 + tc) % 2)]
            w0 = gf + [(C.ch_wd, eb2v)] + list(x_waits) + [
                (C.ch_v, C.ch_v.n), (C.ch_s, C.ch_s.n)]
            if ((mO * 4 + tc) % 2) in pend:
                w0.append(pend[(mO * 4 + tc) % 2])
            mmv = P.op("tensor", lambda t, mO=mO, tc=tc, ps=ps: t.matmul(
                ps[:, 0:512], EB2[0:8, mO * 128:(mO + 1) * 128],
                GF[0:8, tc * 512:(tc + 1) * 512], start=True, stop=True),
                waits=w0, inc=(C.ch_t, 1))
            ev = P.op("vector", lambda v, mO=mO, tc=tc, ps=ps: v.tensor_add(
                C.X[:, mO * TOK + tc * 512: mO * TOK + (tc + 1) * 512],
                C.X[:, mO * TOK + tc * 512: mO * TOK + (tc + 1) * 512],
                ps[:, 0:512]), waits=[(C.ch_t, mmv)], inc=(C.ch_v, 1))
            pend[(mO * 4 + tc) % 2] = (C.ch_v, ev)
    return [(C.ch_v, C.ch_v.n)]


def emit_ffn_l1(C, XNo, xn_ready, x_waits):
    P = C.P
    nc = C.nc
    RW1 = C.f1_rw; ZF = C.f1_zf; SQZ = C.f1_sqz; RSD = C.f1_rsd
    CTt = C.f1_ct; CS = C.f1_cs; CNT = C.f1_cnt
    LT = C.f1_lt; PRB = C.f1_prb; GT = C.f1_gt; GF1 = C.f1_gf
    WIE = C.f1_wie; WOE = C.f1_woe; HT1 = C.f1_ht; T32 = C.f1_t32
    ETB = C.f1_etb
    rwv = load_w_fm(C, "L1_routerT", RW1, D, 128, waits=list(xn_ready))
    pend = {}
    zv = None
    for tc in range(4):
        ps = C.PB[tc % 2]
        w0 = list(xn_ready) + [rwv] + ([pend[tc % 2]] if tc % 2 in pend else [])
        mmv = None
        for k in range(DT):
            kw = dict(inc=(C.ch_t, 1)) if k == DT - 1 else {}
            mmv = P.op("tensor", lambda t, k=k, tc=tc, ps=ps: t.matmul(
                ps[:, 0:512], RW1[:, k * 128:(k + 1) * 128],
                XNo.rearrange("p (j t) -> p j t", j=DT)[:, k, tc * 512:(tc + 1) * 512],
                start=(k == 0), stop=(k == DT - 1)),
                waits=(w0 if k == 0 else ()), **kw)
        zv = P.op("scalar", lambda s, tc=tc, ps=ps: s.copy(
            out=ZF[:, tc * 512:(tc + 1) * 512], in_=ps[:, 0:512]),
            waits=[(C.ch_t, mmv)], inc=(C.ch_s, 1))
        pend[tc % 2] = (C.ch_s, zv)
    sqv = P.op("scalar", lambda s: s.activation(out=SQZ[:, :], in_=ZF[:, :],
         func=AF.Square), waits=[(C.ch_s, zv)], inc=(C.ch_s, 1))
    rv = None
    for tc in range(4):
        mmv = P.op("tensor", lambda t, tc=tc: t.matmul(
            C.PB[2][0:1, 0:512], C.onescol_b_s[:, 0:1],
            SQZ[:, tc * 512:(tc + 1) * 512], start=True, stop=True),
            waits=[(C.ch_s, sqv)] + ([(C.ch_v, rv)] if rv else []), inc=(C.ch_t, 1))
        sq2 = P.op("scalar", lambda s, tc=tc: s.activation(
            out=RSD[0:1, tc * 512:(tc + 1) * 512], in_=C.PB[2][0:1, 0:512],
            func=AF.Sqrt), waits=[(C.ch_t, mmv)], inc=(C.ch_s, 1))
        P.op("vector", lambda v, tc=tc: v.tensor_scalar(
            RSD[0:1, tc * 512:(tc + 1) * 512], RSD[0:1, tc * 512:(tc + 1) * 512],
            1e-12, None, ALU.max), waits=[(C.ch_s, sq2)])
        rv = P.op("vector", lambda v, tc=tc: v.reciprocal(
            RSD[0:1, tc * 512:(tc + 1) * 512], RSD[0:1, tc * 512:(tc + 1) * 512]),
            inc=(C.ch_v, 1))
    znv = None
    for tc in range(4):
        mmb = P.op("tensor", lambda t, tc=tc: t.matmul(
            C.PB[3][:, 0:512], C.onesrow_f_s[0:1, :],
            RSD[0:1, tc * 512:(tc + 1) * 512], start=True, stop=True),
            waits=[(C.ch_v, rv)] + ([(C.ch_v, znv)] if znv else []), inc=(C.ch_t, 1))
        znv = P.op("vector", lambda v, tc=tc: v.tensor_mul(
            ZF[:, tc * 512:(tc + 1) * 512], ZF[:, tc * 512:(tc + 1) * 512],
            C.PB[3][:, 0:512]), waits=[(C.ch_t, mmb)], inc=(C.ch_v, 1))
    ctv = P.op("gpsimd", lambda g: g.dma_start(out=CTt[0:16, :], in_=C.inp["L1_cent"][:, :]),
               waits=list(xn_ready), inc=(C.ch_wd, 16))
    s1 = P.op("scalar", lambda s: s.activation(out=CS[0:16, :], in_=CTt[0:16, :],
         func=AF.Square), waits=[(C.ch_wd, ctv)], inc=(C.ch_s, 1))
    r1 = P.op("vector", lambda v: v.tensor_reduce(C.ffn_srow[0:16, 0:1], CS[0:16, :],
         axis=AX.X, op=ALU.add), waits=[(C.ch_s, s1)], inc=(C.ch_v, 1))
    s2 = P.op("scalar", lambda s: s.activation(out=C.ffn_srow[0:16, 1:2],
         in_=C.ffn_srow[0:16, 0:1], func=AF.Sqrt), waits=[(C.ch_v, r1)],
         inc=(C.ch_s, 1))
    P.op("vector", lambda v: v.tensor_scalar(C.ffn_srow[0:16, 1:2],
         C.ffn_srow[0:16, 1:2], 1e-12, None, ALU.max), waits=[(C.ch_s, s2)])
    vv2 = P.op("vector", lambda v: v.reciprocal(C.ffn_srow[0:16, 2:3],
         C.ffn_srow[0:16, 1:2]), inc=(C.ch_v, 1))
    cnv = P.op("vector", lambda v: v.tensor_scalar(CS[0:16, :], CTt[0:16, :],
         C.ffn_srow[0:16, 2:3], None, ALU.mult), inc=(C.ch_v, 1))
    mmt = P.op("tensor", lambda t: t.transpose(C.PT[0:128, 0:16], CS[0:16, 0:128],
         C.identb_s[0:16, 0:16]), waits=[(C.ch_v, cnv)], inc=(C.ch_t, 1))
    cntv = P.op("vector", lambda v: v.tensor_copy(CNT[:, 0:16], C.PT[0:128, 0:16]),
         waits=[(C.ch_t, mmt)], inc=(C.ch_v, 1))
    tv = P.op("gpsimd", lambda g: g.dma_start(out=C.ffn_srow[0:1, 3:4],
         in_=C.inp["L1_temp"][0:1, 0:1]), waits=list(xn_ready), inc=(C.ch_wd, 16))
    bcv = P.op("gpsimd", lambda g: g.memset(C.ffn_srow[0:1, 5:6], -2.4258509299404572),
               waits=[(C.ch_wd, tv)], inc=(C.ch_g, 1))
    e1 = P.op("scalar", lambda s: s.activation(out=C.ffn_srow[0:1, 4:5],
         in_=C.ffn_srow[0:1, 3:4], func=AF.Exp, bias=C.ffn_srow[0:1, 5:6],
         scale=1.0), waits=[(C.ch_g, bcv), (C.ch_v, r1)], inc=(C.ch_s, 1))
    mmE = P.op("tensor", lambda t: t.matmul(C.PB[2][:, 0:1], C.onesrow_f_s[0:1, :],
         C.ffn_srow[0:1, 4:5], start=True, stop=True), waits=[(C.ch_s, e1)],
         inc=(C.ch_t, 1))
    etv = P.op("vector", lambda v: v.tensor_copy(ETB[:, 0:1], C.PB[2][:, 0:1]),
         waits=[(C.ch_t, mmE)], inc=(C.ch_v, 1))
    pv = None
    for tt in range(TOK // 128):
        w0 = [(C.ch_v, znv), (C.ch_v, cntv), (C.ch_v, etv)] + ([(C.ch_s, pv)] if pv else [])
        mmv = P.op("tensor", lambda t, tt=tt: t.matmul(
            C.PB[2][:, 0:16], ZF[:, tt * 128:(tt + 1) * 128], CNT[:, 0:16],
            start=True, stop=True), waits=w0, inc=(C.ch_t, 1))
        pv = P.op("scalar", lambda s, tt=tt: s.activation(
            out=LT[:, tt * 16:(tt + 1) * 16], in_=C.PB[2][:, 0:16],
            func=AF.Identity, scale=ETB[:, 0:1]), waits=[(C.ch_t, mmv)],
            inc=(C.ch_s, 1))
    smv = None
    for tt in range(TOK // 128):
        sl = slice(tt * 16, (tt + 1) * 16)
        P.op("vector", lambda v, sl=sl: v.tensor_reduce(
            C.ffn_srow[:, 0:1], LT[:, sl], axis=AX.X, op=ALU.max),
            waits=[(C.ch_s, pv)])
        P.op("vector", lambda v, sl=sl: v.tensor_scalar(
            LT[:, sl], LT[:, sl], C.ffn_srow[:, 0:1], None, ALU.subtract))
        sev = P.op("scalar", lambda s, sl=sl: s.activation(out=LT[:, sl],
             in_=LT[:, sl], func=AF.Exp), waits=[(C.ch_v, C.ch_v.n)], inc=(C.ch_s, 1))
        P.op("vector", lambda v, sl=sl: v.tensor_reduce(
            C.ffn_srow[:, 0:1], LT[:, sl], axis=AX.X, op=ALU.add),
            waits=[(C.ch_s, sev)])
        P.op("vector", lambda v: v.reciprocal(C.ffn_srow[:, 0:1], C.ffn_srow[:, 0:1]))
        smv = P.op("vector", lambda v, sl=sl: v.tensor_scalar(
            PRB[:, sl], LT[:, sl], C.ffn_srow[:, 0:1], None, ALU.mult),
            inc=(C.ch_v, 1))
    g2 = emit_top2(C, PRB, GT, [(ti, 16) for ti in range(16)], renorm_eps=None,
                   probs_ready=[(C.ch_v, smv)])
    gf = emit_gate_transpose(C, GT, GF1, 16, 16, g2)
    for n in range(16):
        wfree = [(C.ch_t, C.ch_t.n), (C.ch_s, C.ch_s.n), (C.ch_v, C.ch_v.n)]
        wv1v = P.op("gpsimd", lambda g, n=n: g.dma_start(
            out=WIE.rearrange("p (kc e) -> p kc e", e=512)[:, :, :],
            in_=C.inp["L1_ie"][n].rearrange("(kc p) e -> p kc e", p=128)),
            waits=wfree, inc=(C.ch_wd, 16))
        wv2v = P.op("gpsimd", lambda g, n=n: g.dma_start(
            out=WOE.rearrange("p (kc e) -> p kc e", e=64)[:, :, :],
            in_=C.inp["L1_oe"][n].rearrange("(kc p) e -> p kc e", p=128)),
            waits=wfree, inc=(C.ch_wd, 16))
        wv1 = (C.ch_wd, wv1v); wv2 = (C.ch_wd, wv2v)
        pend_h = {}
        for tc in range(4):
            gbw = list(gf) + wfree + [(C.ch_v, C.ch_v.n)]
            gb = P.op("tensor", lambda t, n=n, tc=tc: t.matmul(
                C.PB[6][0:64, 0:512], C.sel16_s[0:16, n * 128:n * 128 + 64],
                GF1[0:16, tc * 512:(tc + 1) * 512], start=True, stop=True),
                waits=gbw, inc=(C.ch_t, 1))
            for mH in range(4):
                ps = C.PB[2 + (mH % 2)]
                w0 = [(C.ch_v, znv), wv1] + ([pend_h[mH % 2]] if mH % 2 in pend_h else [])
                mmv = None
                for k in range(DT):
                    kw = dict(inc=(C.ch_t, 1)) if k == DT - 1 else {}
                    mmv = P.op("tensor", lambda t, k=k, mH=mH, tc=tc, ps=ps: t.matmul(
                        ps[:, 0:512], WIE[:, k * 512 + mH * 128: k * 512 + (mH + 1) * 128],
                        XNo.rearrange("p (j t) -> p j t", j=DT)[:, k, tc * 512:(tc + 1) * 512],
                        start=(k == 0), stop=(k == DT - 1)),
                        waits=(w0 if k == 0 else ()), **kw)
                sv = P.op("scalar", lambda s, mH=mH, ps=ps: s.activation(
                    out=HT1[:, mH * 512:(mH + 1) * 512], in_=ps[:, 0:512],
                    func=AF.Gelu), waits=[(C.ch_t, mmv)], inc=(C.ch_s, 1))
                pend_h[mH % 2] = (C.ch_s, sv)
            w0 = [wv2, pend_h[0], pend_h[1]] + ([pend_h.get("o")] if pend_h.get("o") else [])
            mmv = None
            for k in range(4):
                kw = dict(inc=(C.ch_t, 1)) if k == 3 else {}
                mmv = P.op("tensor", lambda t, k=k, ps=None: t.matmul(
                    C.PB[4][0:64, 0:512], WOE[:, k * 64:(k + 1) * 64],
                    HT1[:, k * 512:(k + 1) * 512],
                    start=(k == 0), stop=(k == 3)),
                    waits=(w0 if k == 0 else ()), **kw)
            base = (n % 2) * 64
            j = n // 2
            sv2 = P.op("scalar", lambda s, base=base: s.copy(out=T32[base:base + 64, :],
                 in_=C.PB[4][0:64, 0:512]),
                 waits=[(C.ch_t, mmv), (C.ch_v, C.ch_v.n)], inc=(C.ch_s, 1))
            vv3 = P.op("vector", lambda v, base=base: v.tensor_mul(
                 T32[base:base + 64, :], T32[base:base + 64, :],
                 C.PB[6][0:64, 0:512]), waits=[(C.ch_s, sv2), (C.ch_t, gb)],
                 inc=(C.ch_v, 1))
            ev = P.op("vector", lambda v, base=base, j=j, tc=tc: v.tensor_add(
                 C.X[base:base + 64, j * TOK + tc * 512: j * TOK + (tc + 1) * 512],
                 C.X[base:base + 64, j * TOK + tc * 512: j * TOK + (tc + 1) * 512],
                 T32[base:base + 64, :]), waits=list(x_waits), inc=(C.ch_v, 1))
            pend_h["o"] = (C.ch_v, ev)
    return [(C.ch_v, C.ch_v.n)]


def emit_ffn_l2(C, XNo, xn_ready, x_waits):
    P = C.P
    GW = C.f2_gw; LT = C.f2_lt; PRB = C.f2_prb; GT = C.f2_gt; GF2 = C.f2_gf
    WBg = C.f2_wbg; WBu = C.f2_wbu; WBd = C.f2_wbd
    HTI = C.f2_hti; SGF = C.f2_sgf; SGW = C.f2_sgw
    T32 = C.f2_t32
    gwv = load_w_fm(C, "L2_gateT", GW, D, 8, waits=list(xn_ready))
    sgv = load_w_fm(C, "L2_sgT", SGW, D, 1, waits=list(xn_ready), slow=True)
    pv = None
    for tt in range(TOK // 128):
        w0 = list(xn_ready) + [gwv] + ([(C.ch_s, pv)] if pv else [])
        mmv = None
        for k in range(DT):
            kw = dict(inc=(C.ch_t, 1)) if k == DT - 1 else {}
            mmv = P.op("tensor", lambda t, k=k, tt=tt: t.matmul(
                C.PB[2][:, 0:8],
                XNo.rearrange("p (j t) -> p j t", j=DT)[:, k, tt * 128:(tt + 1) * 128],
                GW[:, k * 8:(k + 1) * 8], start=(k == 0), stop=(k == DT - 1)),
                waits=(w0 if k == 0 else ()), **kw)
        pv = P.op("scalar", lambda s, tt=tt: s.copy(
            out=LT[:, tt * 8:(tt + 1) * 8], in_=C.PB[2][:, 0:8]),
            waits=[(C.ch_t, mmv)], inc=(C.ch_s, 1))
    smv = None
    for tt in range(TOK // 128):
        sl = slice(tt * 8, (tt + 1) * 8)
        P.op("vector", lambda v, sl=sl: v.tensor_reduce(
            C.ffn_srow[:, 0:1], LT[:, sl], axis=AX.X, op=ALU.max),
            waits=[(C.ch_s, pv)])
        P.op("vector", lambda v, sl=sl: v.tensor_scalar(
            LT[:, sl], LT[:, sl], C.ffn_srow[:, 0:1], None, ALU.subtract))
        sev = P.op("scalar", lambda s, sl=sl: s.activation(out=LT[:, sl],
             in_=LT[:, sl], func=AF.Exp), waits=[(C.ch_v, C.ch_v.n)], inc=(C.ch_s, 1))
        P.op("vector", lambda v, sl=sl: v.tensor_reduce(
            C.ffn_srow[:, 0:1], LT[:, sl], axis=AX.X, op=ALU.add),
            waits=[(C.ch_s, sev)])
        P.op("vector", lambda v: v.reciprocal(C.ffn_srow[:, 0:1], C.ffn_srow[:, 0:1]))
        smv = P.op("vector", lambda v, sl=sl: v.tensor_scalar(
            PRB[:, sl], LT[:, sl], C.ffn_srow[:, 0:1], None, ALU.mult),
            inc=(C.ch_v, 1))
    g2 = emit_top2(C, PRB, GT, [(ti, 8) for ti in range(16)], renorm_eps=0.0,
                   probs_ready=[(C.ch_v, smv)])
    gf = emit_gate_transpose(C, GT, GF2, 16, 8, g2)
    pend_sg = None
    for tc in range(4):
        w0 = list(xn_ready) + [sgv] + ([pend_sg] if pend_sg else [])
        mmv = None
        for k in range(DT):
            kw = dict(inc=(C.ch_t, 1)) if k == DT - 1 else {}
            mmv = P.op("tensor", lambda t, k=k, tc=tc: t.matmul(
                C.PB[3][0:1, 0:512], SGW[:, k:k + 1],
                XNo.rearrange("p (j t) -> p j t", j=DT)[:, k, tc * 512:(tc + 1) * 512],
                start=(k == 0), stop=(k == DT - 1)),
                waits=(w0 if k == 0 else ()), **kw)
        svv = P.op("scalar", lambda s, tc=tc: s.activation(
            out=SGF[0:1, tc * 512:(tc + 1) * 512], in_=C.PB[3][0:1, 0:512],
            func=AF.Sigmoid), waits=[(C.ch_t, mmv)], inc=(C.ch_s, 1))
        pend_sg = (C.ch_s, svv)
    sg_done = pend_sg
    def expert(widx):
        wfree = [(C.ch_t, C.ch_t.n), (C.ch_s, C.ch_s.n), (C.ch_v, C.ch_v.n)]
        def ld(name, dst):
            if widx is None:
                src = C.inp[name].rearrange("(kc p) e -> p kc e", p=128)
            else:
                src = C.inp[name][widx].rearrange("(kc p) e -> p kc e", p=128)
            v = P.op("gpsimd", lambda g: g.dma_start(
                out=dst.rearrange("p (kc e) -> p kc e", e=src.shape[2])[:, :, :],
                in_=src), waits=wfree, inc=(C.ch_wd, 16))
            return (C.ch_wd, v)
        if widx is None:
            wg = ld("L2_swg", WBg); wu = ld("L2_swu", WBu); wd = ld("L2_swd", WBd)
        else:
            wg = ld("L2_wg", WBg); wu = ld("L2_wu", WBu); wd = ld("L2_wd", WBd)
        pend = {}
        for tc in range(4):
            if widx is not None:
                gbw = list(gf) + wfree + [(C.ch_v, C.ch_v.n)]
                gb = P.op("tensor", lambda t, e=widx, tc=tc: t.matmul(
                    C.PB[6][:, 0:512], C.sel8_s[0:8, e * 128:(e + 1) * 128],
                    GF2[0:8, tc * 512:(tc + 1) * 512], start=True, stop=True),
                    waits=gbw, inc=(C.ch_t, 1))
            for mI in range(11):
                psg = C.PB[2]
                psu = C.PB[3]
                w0 = list(xn_ready) + [wg, wu] + ([pend.get("gu")] if pend.get("gu") else [])
                mmg = None
                for k in range(DT):
                    kw = dict(inc=(C.ch_t, 1)) if k == DT - 1 else {}
                    mmg = P.op("tensor", lambda t, k=k, mI=mI, tc=tc: t.matmul(
                        psg[:, 0:512], WBg[:, k * 1408 + mI * 128: k * 1408 + (mI + 1) * 128],
                        XNo.rearrange("p (j t) -> p j t", j=DT)[:, k, tc * 512:(tc + 1) * 512],
                        start=(k == 0), stop=(k == DT - 1)),
                        waits=(w0 if k == 0 else ()), **kw)
                mmu = None
                for k in range(DT):
                    kw = dict(inc=(C.ch_t, 1)) if k == DT - 1 else {}
                    mmu = P.op("tensor", lambda t, k=k, mI=mI, tc=tc: t.matmul(
                        psu[:, 0:512], WBu[:, k * 1408 + mI * 128: k * 1408 + (mI + 1) * 128],
                        XNo.rearrange("p (j t) -> p j t", j=DT)[:, k, tc * 512:(tc + 1) * 512],
                        start=(k == 0), stop=(k == DT - 1)),
                        waits=(w0 if k == 0 else ()), **kw)
                sv = P.op("scalar", lambda s, mI=mI: s.activation(
                    out=T32[:, 0:512], in_=psg[:, 0:512],
                    func=AF.Silu), waits=[(C.ch_t, mmu), (C.ch_v, C.ch_v.n)],
                    inc=(C.ch_s, 1))
                if widx is not None:
                    P.op("vector", lambda v, mI=mI: v.tensor_mul(
                        T32[:, 0:512], T32[:, 0:512],
                        C.PB[6][:, 0:512]), waits=[(C.ch_s, sv), (C.ch_t, gb)])
                hv = P.op("vector", lambda v, mI=mI: v.tensor_mul(
                    HTI[:, mI * 512:(mI + 1) * 512], T32[:, 0:512],
                    psu[:, 0:512]), waits=[(C.ch_s, sv)], inc=(C.ch_v, 1))
                pend["gu"] = (C.ch_v, hv)
            for mO in range(DT):
                ps = C.PB[4 + (mO % 2)]
                w0 = [wd, pend["gu"]] + list(x_waits) + ([pend.get(("o", mO % 2))] if pend.get(("o", mO % 2)) else [])
                mmv = None
                for k in range(11):
                    kw = dict(inc=(C.ch_t, 1)) if k == 10 else {}
                    mmv = P.op("tensor", lambda t, k=k, mO=mO, ps=ps: t.matmul(
                        ps[:, 0:512], WBd[:, k * 1024 + mO * 128: k * 1024 + (mO + 1) * 128],
                        HTI[:, k * 512:(k + 1) * 512],
                        start=(k == 0), stop=(k == 10)),
                        waits=(w0 if k == 0 else ()), **kw)
                if widx is not None:
                    ev = P.op("vector", lambda v, mO=mO, tc=tc, ps=ps: v.tensor_add(
                        C.X[:, mO * TOK + tc * 512: mO * TOK + (tc + 1) * 512],
                        C.X[:, mO * TOK + tc * 512: mO * TOK + (tc + 1) * 512],
                        ps[:, 0:512]), waits=[(C.ch_t, mmv)], inc=(C.ch_v, 1))
                else:
                    mmb = P.op("tensor", lambda t, tc=tc: t.matmul(
                        C.PB[6][:, 0:512], C.onesrow_b_s[0:1, :],
                        SGF[0:1, tc * 512:(tc + 1) * 512], start=True, stop=True),
                        waits=[sg_done, (C.ch_v, C.ch_v.n)], inc=(C.ch_t, 1))
                    s32 = P.op("scalar", lambda s, ps=ps: s.copy(
                        out=T32[:, :], in_=ps[:, 0:512]),
                        waits=[(C.ch_t, mmv), (C.ch_v, C.ch_v.n)], inc=(C.ch_s, 1))
                    P.op("vector", lambda v: v.tensor_mul(T32[:, :], T32[:, :],
                        C.PB[6][:, 0:512]), waits=[(C.ch_s, s32), (C.ch_t, mmb)])
                    ev = P.op("vector", lambda v, mO=mO, tc=tc: v.tensor_add(
                        C.X[:, mO * TOK + tc * 512: mO * TOK + (tc + 1) * 512],
                        C.X[:, mO * TOK + tc * 512: mO * TOK + (tc + 1) * 512],
                        T32[:, :]), waits=list(x_waits), inc=(C.ch_v, 1))
                pend[("o", mO % 2)] = (C.ch_v, ev)
    for e in range(8):
        expert(e)
    expert(None)
    return [(C.ch_v, C.ch_v.n)]


def emit_rope(C, T, ready):
    P = C.P
    QS = C.att_qs
    vsw = C.ch_sd.bump(64)
    def emit_swap(sy):
        for a, b2 in ((0, 32), (32, 0), (64, 96), (96, 64)):
            sy.dma_start(out=QS[a:a + 32, :], in_=T[b2:b2 + 32, :]).then_inc(C.ch_sd.h, 16)
        return sy.nop()
    P.op("sync", emit_swap, waits=list(ready) + [(C.ch_v, C.ch_v.n)])
    v1 = P.op("vector", lambda v: v.tensor_mul(T[:, :], T[:, :], C.att_rc[:, :]),
              waits=list(ready) + [(C.ch_sd, vsw)], inc=(C.ch_v, 1))
    v2 = P.op("vector", lambda v: v.tensor_mul(QS[:, :], QS[:, :], C.att_rs2[:, :]),
              inc=(C.ch_v, 1))
    v3 = P.op("vector", lambda v: v.tensor_add(T[:, :], T[:, :], QS[:, :]),
              inc=(C.ch_v, 1))
    return [(C.ch_v, v3)]


_MIDS = [0.25, 0.75, 1.25, 1.75, 2.5, 3.5, 5.0]
_DELTAS = [0.5, 0.5, 0.5, 0.5, 1.0, 1.0, 2.0]

def emit_fp4_quant(C, T, QD, W, waits):
    P = C.P
    nb = W // 16
    BM, RS = C.q_bm, C.q_rs
    A, Q, SG = C.q_a, C.q_q, C.q_sg
    A3 = A[:, 0:W].rearrange("p (b s) -> p b s", s=16)
    T3 = T.rearrange("p (b s) -> p b s", s=16)
    P.op("vector", lambda v: v.tensor_reduce(
        BM[:, 0:nb], T3, axis=AX.X,
        op=ALU.max, apply_absolute_value=True), waits=list(waits))
    P.op("vector", lambda v: v.tensor_scalar(BM[:, 0:nb], BM[:, 0:nb],
         1.0 / 6.0, 1e-12, ALU.mult, ALU.max))
    P.op("vector", lambda v: v.reciprocal(RS[:, 0:nb], BM[:, 0:nb]))
    P.op("vector", lambda v: v.tensor_tensor(
        A3, T3, bc0(RS[:, 0:nb], 16), ALU.mult))
    P.op("vector", lambda v: v.tensor_scalar(SG[:, 0:W], A[:, 0:W], 0.0, None, ALU.is_ge))
    P.op("vector", lambda v: v.tensor_scalar(SG[:, 0:W], SG[:, 0:W], 2.0, -1.0,
         ALU.mult, ALU.add))
    sab = P.op("scalar", lambda s: s.activation(out=A[:, 0:W], in_=A[:, 0:W],
         func=AF.Abs), waits=[(C.ch_v, C.ch_v.n)], inc=(C.ch_s, 1))
    P.op("vector", lambda v: v.tensor_scalar(Q[:, 0:W], A[:, 0:W], _MIDS[0],
         _DELTAS[0], ALU.is_gt, ALU.mult), waits=[(C.ch_s, sab)])
    for j in range(1, 7):
        P.op("vector", lambda v, j=j: v.tensor_scalar(SGQ_tmp(C)[:, 0:W], A[:, 0:W],
             _MIDS[j], _DELTAS[j], ALU.is_gt, ALU.mult))
        P.op("vector", lambda v: v.tensor_add(Q[:, 0:W], Q[:, 0:W], SGQ_tmp(C)[:, 0:W]))
    P.op("vector", lambda v: v.tensor_mul(Q[:, 0:W], Q[:, 0:W], SG[:, 0:W]))
    QD3 = QD.rearrange("p (b s) -> p b s", s=16)
    Q3 = Q[:, 0:W].rearrange("p (b s) -> p b s", s=16)
    vv = P.op("vector", lambda v: v.tensor_tensor(
        QD3, Q3, bc0(BM[:, 0:nb], 16), ALU.mult), inc=(C.ch_v, 1))
    return (C.ch_v, vv)

def SGQ_tmp(C):
    return C.q_t

def emit_quant_weightT(C, name, WQT, nrows, ncols, waits):
    P = C.P
    nc_ = C.nc
    STG = C.q_stg
    QD = C.q_qd
    last = None
    pend_t = None
    for pt in range(nrows // 128):
        w0 = list(waits) + ([last] if last else [])
        lv = P.op("gpsimd", lambda g, pt=pt: g.dma_start(
            out=STG[:, 0:ncols], in_=C.inp[name][pt * 128:(pt + 1) * 128, :]),
            waits=w0 + [(C.ch_v, C.ch_v.n)], inc=(C.ch_wd, 16))
        qv = emit_fp4_quant(C, STG[:, 0:ncols], QD[:, 0:ncols], ncols,
                            [(C.ch_wd, lv)])
        for kc in range(ncols // 128):
            mmv = P.op("tensor", lambda t, kc=kc: t.transpose(
                C.PT[:, 0:128], QD[:, kc * 128:(kc + 1) * 128], C.identb_s[:, :]),
                waits=[qv] + ([pend_t] if pend_t else []), inc=(C.ch_t, 1))
            ev = P.op("vector", lambda v, kc=kc, pt=pt: v.tensor_copy(
                WQT[:, kc * nrows + pt * 128: kc * nrows + (pt + 1) * 128],
                C.PT[:, 0:128]), waits=[(C.ch_t, mmv)], inc=(C.ch_v, 1))
            pend_t = (C.ch_v, ev)
        last = pend_t
    return last


def emit_ffn_l3_phase1(C, li, WGQ, WUQ, x_waits):
    P = C.P
    nc = C.nc
    qg = emit_quant_weightT(C, "L3_wg", WGQ, 2816, 1024, list(x_waits))
    qu = emit_quant_weightT(C, "L3_wu", WUQ, 2816, 1024, [qg])
    XT = C.q_stg
    XQT = C.q_qd
    XQ = C.f3_xq
    HT = C.f3_ht
    HQ = C.f3_hq
    hq_done = []
    for tc in range(4):
        for tt in range(4):
            vld = C.ch_sd.bump(16)
            def emit_ld(sy, tc=tc, tt=tt):
                h = P.holder
                for myp in (0, 1):
                    guard = sy.If_eq(h["par"], 0) if myp == 0 else sy.Else()
                    with guard:
                        sy.dma_start(
                            out=C.f3_xn[:, :, :],
                            in_=C.xn_sh[myp, li].rearrange("p (j t) -> p j t", j=DT)[
                                :, :, tc * 512 + tt * 128: tc * 512 + (tt + 1) * 128],
                        ).then_inc(C.ch_sd.h, 16)
                sy.end_ifs()
                return sy.nop()
            P.op("sync", emit_ld, waits=[(C.ch_t, C.ch_t.n), (C.ch_v, C.ch_v.n)])
            pend = None
            for j in range(DT):
                mmv = P.op("tensor", lambda t, j=j: t.transpose(
                    C.PT[:, 0:128], C.f3_xn[:, j, :], C.identb_s[:, :]),
                    waits=[(C.ch_sd, vld)] + ([pend] if pend else []), inc=(C.ch_t, 1))
                ev = P.op("vector", lambda v, j=j: v.tensor_copy(
                    XT[:, j * 128:(j + 1) * 128], C.PT[:, 0:128]),
                    waits=[(C.ch_t, mmv)], inc=(C.ch_v, 1))
                pend = (C.ch_v, ev)
            qv = emit_fp4_quant(C, XT[:, 0:1024], XQT[:, 0:1024], 1024, [pend])
            pend = None
            for j in range(DT):
                mmv = P.op("tensor", lambda t, j=j: t.transpose(
                    C.PT[:, 0:128], XQT[:, j * 128:(j + 1) * 128], C.identb_s[:, :]),
                    waits=[qv] + ([pend] if pend else []), inc=(C.ch_t, 1))
                ev = P.op("vector", lambda v, j=j, tt=tt: v.tensor_copy(
                    XQ[:, j * 512 + tt * 128: j * 512 + (tt + 1) * 128],
                    C.PT[:, 0:128]), waits=[(C.ch_t, mmv)], inc=(C.ch_v, 1))
                pend = (C.ch_v, ev)
        xq_ready = [(C.ch_v, C.ch_v.n), qu]
        for tt in range(4):
            for iI in range(6):
                iw = 512 if iI < 5 else 256
                mmg = None
                for k in range(DT):
                    kw = dict(inc=(C.ch_t, 1)) if k == DT - 1 else {}
                    mmg = P.op("tensor", lambda t, k=k, tt=tt, iI=iI, iw=iw, tc=tc: t.matmul(
                        C.PB[2][:, 0:iw], XQ[:, k * 512 + tt * 128: k * 512 + (tt + 1) * 128],
                        WGQ[:, k * 2816 + iI * 512: k * 2816 + iI * 512 + iw],
                        start=(k == 0), stop=(k == DT - 1)),
                        waits=(xq_ready if k == 0 else ()), **kw)
                mmu = None
                u_w0 = [(C.ch_v, C.ch_v.n)]
                for k in range(DT):
                    kw = dict(inc=(C.ch_t, 1)) if k == DT - 1 else {}
                    mmu = P.op("tensor", lambda t, k=k, tt=tt, iI=iI, iw=iw, tc=tc: t.matmul(
                        C.PB[3][:, 0:iw], XQ[:, k * 512 + tt * 128: k * 512 + (tt + 1) * 128],
                        WUQ[:, k * 2816 + iI * 512: k * 2816 + iI * 512 + iw],
                        start=(k == 0), stop=(k == DT - 1)),
                        waits=(u_w0 if k == 0 else ()), **kw)
                sv = P.op("scalar", lambda s, iw=iw: s.activation(
                    out=HT[:, 0:iw], in_=C.PB[2][:, 0:iw], func=AF.Silu),
                    waits=[(C.ch_t, mmu), (C.ch_v, C.ch_v.n)], inc=(C.ch_s, 1))
                hv = P.op("vector", lambda v, iw=iw: v.tensor_mul(
                    HT[:, 0:iw], HT[:, 0:iw], C.PB[3][:, 0:iw]),
                    waits=[(C.ch_s, sv)], inc=(C.ch_v, 1))
                qv2 = emit_fp4_quant(C, HT[:, 0:iw], HQ[:, 0:iw], iw, [(C.ch_v, hv)])
                pend_pt = None
                for jb in range(iw // 128):
                    mmv = P.op("tensor", lambda t, jb=jb: t.transpose(
                        C.PT[:, 0:128], HQ[:, jb * 128:(jb + 1) * 128], C.identb_s[:, :]),
                        waits=[qv2] + ([pend_pt] if pend_pt else []), inc=(C.ch_t, 1))
                    ib = iI * 4 + jb
                    ev = P.op("vector", lambda v: v.tensor_copy(
                        C.f3_hf[:, 0:128], C.PT[:, 0:128]),
                        waits=[(C.ch_t, mmv)], inc=(C.ch_v, 1))
                    pend_pt = (C.ch_v, ev)
                    dv = P.op("sync", lambda sy, ib=ib, tc=tc, tt=tt: sy.dma_start(
                        out=C.scr_h[:, ib * 2048 + tc * 512 + tt * 128:
                                    ib * 2048 + tc * 512 + (tt + 1) * 128],
                        in_=C.f3_hf[:, 0:128]), waits=[(C.ch_v, ev)],
                        inc=(C.ch_sd, 16))
                    P.op("vector", lambda v: v.engine_nop(), waits=[(C.ch_sd, dv)])
        hq_done.append((C.ch_sd, C.ch_sd.n))
    return hq_done

def emit_ffn_l3_phase2(C, li, WDQ, hq_done, x_waits):
    P = C.P
    qd = emit_quant_weightT(C, "L3_wd", WDQ, 1024, 2816, list(hq_done))
    HQF = C.f3_hqf
    pend = {}
    for tc in range(4):
        vld = P.op("gpsimd", lambda g, tc=tc: g.dma_start(
            out=HQF.rearrange("p (kc e) -> p kc e", e=512)[:, :, :],
            in_=C.scr_h.rearrange("p (kc t) -> p kc t", t=2048)[:, :, tc * 512:(tc + 1) * 512]),
            waits=list(hq_done) + [(C.ch_t, C.ch_t.n)], inc=(C.ch_wd, 16))
        for mO in range(DT):
            ps = C.PB[4 + (mO % 2)]
            w0 = [qd, (C.ch_wd, vld)] + list(x_waits) + ([pend.get(mO % 2)] if pend.get(mO % 2) else [])
            mmv = None
            for k in range(22):
                kw = dict(inc=(C.ch_t, 1)) if k == 21 else {}
                mmv = P.op("tensor", lambda t, k=k, mO=mO, ps=ps: t.matmul(
                    ps[:, 0:512], WDQ[:, k * 1024 + mO * 128: k * 1024 + (mO + 1) * 128],
                    HQF[:, k * 512:(k + 1) * 512],
                    start=(k == 0), stop=(k == 21)),
                    waits=(w0 if k == 0 else ()), **kw)
            ev = P.op("vector", lambda v, mO=mO, tc=tc, ps=ps: v.tensor_add(
                C.X[:, mO * TOK + tc * 512: mO * TOK + (tc + 1) * 512],
                C.X[:, mO * TOK + tc * 512: mO * TOK + (tc + 1) * 512],
                ps[:, 0:512]), waits=[(C.ch_t, mmv)], inc=(C.ch_v, 1))
            pend[mO % 2] = (C.ch_v, ev)
    return [(C.ch_v, C.ch_v.n)]



_CACHE = {}

def kernel(**inputs):
    if "C" not in _CACHE:
        _CACHE["C"] = build_full([0, 1, 2, 3], stage="full")
    C = _CACHE["C"]
    ep = int(np.random.randint(1, 1 << 30))
    in_maps = [host_prep(inputs, c, ep) for c in range(8)]
    res = run_cores(C, in_maps)
    out = np.zeros((4, TFULL, D), np.float32)
    for c in range(8):
        b, p = c // 2, c % 2
        out[b, p * TOK:(p + 1) * TOK, :] = np.asarray(res.results[c]["x_out"],
                                                      np.float32).T
    return out
